# version 43
# baseline (speedup 1.0000x reference)
import os
import numpy as np
import ml_dtypes
import concourse.bass as bass
import concourse.mybir as mybir
from concourse.bass_utils import run_bass_kernel_spmd

F32 = mybir.dt.float32
BF16 = mybir.dt.bfloat16
F32R = mybir.dt.float32r
AF = mybir.ActivationFunctionType
OP = mybir.AluOpType

NCORES = 8
BL = 2
T = 2048
D = 1024
KB = 8
NST = 16
RNK = 32
NCH = 4
TC = 512
EPS = 1e-5
PI2 = float(np.pi / 2)

COMPUTE_SEMS = ("V", "S", "T", "G")
DMA_SEMS = ("DL", "DL2", "DG", "DG2", "XT0", "XT1", "XT2", "XT3", "WB0",
             "WB1", "WB2", "WB3", "HM", "HC", "FB0", "FB1", "FB2", "FB3",
             "FB4", "FB5", "FB6", "FB7", "GG0", "GG1", "WX0", "WX1",
             "WX2", "WX3", "XG0", "XG1", "XG2", "XG3", "FG0", "FG1",
             "FG2", "FG3")
OWN_SEM = {"vector": "V", "scalar": "S", "tensor": "T", "gpsimd": "G",
           "sync": None}


class Prog:

    def __init__(self):
        self.steps = []
        self.count = {s: 0 for s in COMPUTE_SEMS + DMA_SEMS}
        self.phase = "P0"

    def step(self, eng, fn, deps=(), sem=None, inc=None, hwdrain=False,
             vsmall=False):
        if sem is None:
            sem = OWN_SEM[eng] or "DL"
        if inc is None:
            inc = 16 if sem in DMA_SEMS else 1
        flat = []
        for d in deps:
            if d is None:
                continue
            for t in (d if isinstance(d, list) else [d]):
                if t is None:
                    continue
                if t[0] in DMA_SEMS:
                    t = (t[0], self.count[t[0]])
                flat.append(t)
        if sem in DMA_SEMS and self.count[sem] >= 8 * 16                 and (self.count[sem] // 16) % 8 == 0:
            flat.append((sem, self.count[sem]))
        self.count[sem] += inc
        tok = (sem, self.count[sem])
        self.steps.append((eng, fn, tuple(flat), sem, inc, hwdrain, self.phase,
                           vsmall))
        return tok

    def bar(self):
        return [(s, self.count[s]) for s in COMPUTE_SEMS + DMA_SEMS
                if self.count[s] > 0]

    def emit(self, eng_name, e, sems):
        dr = os.environ.get("KERNEL_DRAINS", "")
        use_drains = (os.environ.get("BASS_SIM_FP64") == "1"
                      or os.environ.get("KERNEL_SIM_DRAINS") == "1"
                      or {"vector": "V", "scalar": "S", "tensor": "T",
                          "gpsimd": "G"}.get(eng_name, "?") in dr)
        waited = {}
        own_emitted = 0
        drained_at = 0
        my = OWN_SEM[eng_name]
        small_pending = False
        for (s_eng, fn, deps, sem, inc, hwdrain, phase, vsmall) in self.steps:
            if s_eng != eng_name:
                continue
            need_drain = False
            for (dsem, dval) in deps:
                if dsem == my:
                    if dval > drained_at:
                        need_drain = True
                else:
                    if waited.get(dsem, 0) < dval:
                        e.wait_ge(sems[dsem], dval)
                        waited[dsem] = dval
            if need_drain and (use_drains or hwdrain
                               or (eng_name == "vector" and small_pending)):
                e.drain()
                drained_at = own_emitted
                small_pending = False
            last = fn(e)
            if vsmall and eng_name == "vector":
                small_pending = True
            last.then_inc(sems[sem], inc)
            if sem == my:
                own_emitted += inc


def build_program():
    nc = bass.Bass("TRN2", target_bir_lowering=False)

    dx = nc.dram_tensor("x", [BL, T, D], F32, kind="ExternalInput").ap()
    dwin = nc.dram_tensor("win", [2, D, 2 * D], F32, kind="ExternalInput").ap()
    dwout = nc.dram_tensor("wout", [2, D, 512], F32, kind="ExternalInput").ap()
    dwx = nc.dram_tensor("wx", [128, 1024], F32, kind="ExternalInput").ap()
    dwdt = nc.dram_tensor("wdt", [2, RNK, D], F32, kind="ExternalInput").ap()
    dwmod = nc.dram_tensor("wmod", [D, 6 * D], F32, kind="ExternalInput").ap()
    dc2t = nc.dram_tensor("c2t", [128, 16], F32, kind="ExternalInput").ap()
    dvecs = nc.dram_tensor("vecs", [128, 384], F32, kind="ExternalInput").ap()
    dth = nc.dram_tensor("theta", [D, D], F32, kind="ExternalInput").ap()
    dthn = nc.dram_tensor("theta_n", [D, D], F32, kind="ExternalInput").ap()
    dph = nc.dram_tensor("phi", [D, D], F32, kind="ExternalInput").ap()
    dphn = nc.dram_tensor("phi_n", [D, D], F32, kind="ExternalInput").ap()
    dids = nc.dram_tensor("idents", [128, 128], BF16, kind="ExternalInput").ap()
    di2f = nc.dram_tensor("i2f", [2, 2], F32, kind="ExternalInput").ap()
    didf = nc.dram_tensor("idf32", [128, 128], F32, kind="ExternalInput").ap()
    dones = nc.dram_tensor("onesr", [1, 128], BF16, kind="ExternalInput").ap()
    dsel8 = nc.dram_tensor("sel8", [8, 1024], BF16, kind="ExternalInput").ap()
    dselbc = nc.dram_tensor("selbc", [16, 2048], BF16, kind="ExternalInput").ap()
    dwcr = nc.dram_tensor("wconvr", [8, 1024], F32, kind="ExternalInput").ap()
    dout = nc.dram_tensor("out", [BL, T, D], F32, kind="ExternalOutput").ap()

    winS = nc.dram_tensor("winS", [2, 2, KB, 128, D], BF16, kind="Internal").ap()
    winX4 = nc.dram_tensor("winX4", [2, 4, KB, 128, D], BF16, kind="Internal").ap()
    ffnS = nc.dram_tensor("ffnS", [2, KB, 128, D], BF16, kind="Internal").ap()
    h1S = nc.dram_tensor("h1S", [BL, NCH, KB, 128, TC], BF16, kind="Internal").ap()
    hmS = nc.dram_tensor("hmS", [BL, NCH, KB, 128, TC], BF16, kind="Internal").ap()
    fwdS = nc.dram_tensor("fwdS", [4, 128, T], BF16, kind="Internal").ap()

    A = lambda n, s, d: nc.alloc_sbuf_tensor(n, s, d).ap()
    vecs = A("vecs_s", [128, 384], F32)
    acoef = A("acoef", [128, 256], F32)
    modv = A("modv", [128, 96], F32)
    idents = A("idents_s", [128, 128], BF16)
    idf32 = A("idf32_s", [128, 128], F32)
    i2f = A("i2f_s", [2, 2], F32)
    onesr = A("onesr_s", [1, 128], BF16)
    sel8 = A("sel8_s", [8, 1024], BF16)
    row8 = A("row8", [8, 256], BF16)
    c2sf = A("c2sf", [128, 16], F32)
    c2sb = A("c2sb", [128, 16], BF16)
    wdt_sb = A("wdt_sb", [RNK, 2 * D], BF16)
    wx_sb = A("wx_sb", [128, 1024], BF16)
    wout_sb = A("wout_sb", [128, 8192], BF16)
    win_b = [A(f"win_b{i}", [128, D], BF16) for i in range(8)]
    hm_all = A("hm_all", [128, KB * 515], BF16)
    ffn_b = [A(f"ffn_b{i}", [128, D], BF16) for i in range(8)]
    xt = [A(f"xt{i}", [128, D], F32) for i in range(4)]
    stA = A("stA", [128, 160], F32)
    xhat = [A(f"xhat{i}", [128, D], BF16) for i in range(2)]
    hh2 = xhat
    x_lin = A("x_lin", [128, KB * TC], F32)
    tailb = A("tailb", [128, 32], BF16)
    dt_c = A("dt_c", [128, KB * TC], F32)
    dtx_c = A("dtx_c", [128, KB * TC], BF16)
    y_c = A("y_c", [128, KB * TC], F32)
    sz_c = A("sz_c", [128, KB * TC], BF16)
    bcB = A("bcB", [16, T], BF16)
    bcC = A("bcC", [16, T], BF16)
    selB = A("selB_s", [16, 2048], BF16)
    dA8 = A("dA8", [128, KB * TC], F32)
    dBx_b = A("dBx_b", [128, KB * TC], BF16)
    hc_b = A("hc_b", [128, KB * TC], BF16)
    dtr_b = hc_b[0:RNK, 0:TC]
    h1t = [dt_c.bitcast(BF16)[:, i * 1024:(i + 1) * 1024] for i in range(4)]
    gag_b = [dA8.bitcast(BF16)[:, 0:512], dA8.bitcast(BF16)[:, 512:1024]]
    carr = [A(f"carr{i}", [128, 128], F32) for i in range(2)]
    scr16 = A("scr16", [128, 2048], BF16)
    gin_b = [scr16[:, 0:512], scr16[:, 512:1024]]
    go_b = [A(f"go_b{i}", [128, TC], F32) for i in range(2)]
    cst = A("cst", [128, 4], F32)

    bcbuf = scr16
    re2 = go_b[0]
    im2 = go_b[1]
    tst = dBx_b
    p0st = dt_c[:, 2048:4096]
    stgb = dBx_b[:, 0:2048]
    mrow = y_c[0:2, 0:2048]
    abc = sz_c[:, 0:1024]
    sbc = sz_c[:, 1024:2048]
    h1c = x_lin.bitcast(BF16)[:, 0:4096]
    h2f = dt_c.bitcast(BF16)[:, 0:4096]
    stg32b = y_c[:, 2048:4096]
    wcr = y_c[0:8, 0:1024]
    stgb2 = dBx_b[:, 2048:4096]
    rb8 = dt_c.bitcast(BF16)[:, 4096:8192]

    pA = nc.alloc_psum_tensor("pA", [128, 2048], F32).ap()
    pB = nc.alloc_psum_tensor("pB", [128, 2048], F32).ap()
    pAb = pA.bitcast(BF16)
    pBb = pB.bitcast(BF16)

    p = Prog()

    def mcol(g, dblk, b):
        c0 = (g * 8 + dblk) * 2 + b
        return modv[:, c0:c0 + 1]

    def vcol(base, dblk):
        return vecs[:, base + dblk:base + dblk + 1]

    tv0 = p.step("sync", lambda e: e.dma_start(out=vecs[:], in_=dvecs[:]))
    tcst = p.step("vector", lambda e: (e.memset(cst[:, 0:1], EPS),
                                       e.memset(cst[:, 1:2], 1.0),
                                       e.memset(cst[:, 2:3], PI2),
                                       e.memset(cst[:, 3:4], 1e-30))[-1])
    tid = p.step("sync", lambda e: e.dma_start(out=idents[:], in_=dids[:]))
    tif = p.step("sync", lambda e: e.dma_start(out=idf32[:], in_=didf[:]))
    ti2 = p.step("sync", lambda e: e.dma_start(out=i2f[:], in_=di2f[:]))
    ton = p.step("sync", lambda e: e.dma_start(out=onesr[:], in_=dones[:]))
    ts8 = p.step("sync", lambda e: e.dma_start(out=sel8[:], in_=dsel8[:]))
    tsbc_l = p.step("sync", lambda e: e.dma_start(out=selB[:], in_=dselbc[:]))
    tc2 = p.step("sync", lambda e: e.dma_start(out=c2sf[:], in_=dc2t[:]))

    tk = p.step("scalar", lambda e: e.activation(acoef[:], vecs[:, 128:384], AF.Exp),
                deps=[tv0])
    tok_acoef = p.step("scalar", lambda e: e.activation(
        acoef[:], acoef[:], AF.Identity, scale=-1.0), deps=[tk])
    tok_c2s = p.step("scalar", lambda e: e.activation(c2sf[:], c2sf[:], AF.Silu),
                     deps=[tc2])
    tok_c2s = p.step("vector", lambda e: e.tensor_copy(c2sb[:], c2sf[:]),
                     deps=[tok_c2s])

    tok_tr = ti2
    tok_ev = None
    tok_k = None
    for r in range(3):
        for k in range(KB):
            sl = slice(r * 2048, (r + 1) * 2048)
            tl = p.step("sync", lambda e, k=k, sl=sl: e.dma_start(
                out=p0st[:], in_=dwmod[k * 128:(k + 1) * 128, sl]),
                deps=[tok_c2s, tok_k])
            tcv = p.step("vector", lambda e: e.tensor_copy(stgb[:], p0st[:]),
                         deps=[tl, tok_k])
            tok_k = p.step("tensor", lambda e, k=k: [
                e.matmul(pA[0:2, q * 512:(q + 1) * 512],
                         c2sb[:, k * 2:(k + 1) * 2],
                         stgb[:, q * 512:(q + 1) * 512],
                         start=(k == 0), stop=(k == KB - 1)) for q in range(4)][-1],
                deps=[tcv, tok_c2s, tok_ev if k == 0 else None])
        tok_ev = p.step("vector", lambda e: e.tensor_copy(mrow, pA[0:2, :]),
                        deps=[tok_k])
        tok_tr = p.step("tensor", lambda e, r=r: [
            e.transpose(pB[:, (r * 16 + j) * 2:(r * 16 + j) * 2 + 2],
                        mrow[0:2, j * 128:(j + 1) * 128], i2f[:])
            for j in range(16)][-1], deps=[tok_ev, tok_tr])
    tok_modp = p.step("vector", lambda e: e.tensor_copy(modv[:], pB[:, 0:96]),
                      deps=[tok_tr])

    def modfix(e):
        for g in (1, 4):
            i = e.tensor_scalar(modv[:, g * 16:(g + 1) * 16],
                                modv[:, g * 16:(g + 1) * 16], 1.0, None, OP.add)
        return i
    tok_mod = p.step("vector", modfix, deps=[tok_modp], vsmall=True)
    for g in (2, 5):
        sl = modv[:, g * 16:(g + 1) * 16]
        tok_mod = p.step("vector", lambda e, sl=sl: e.tensor_scalar(
            stA[:, 0:16], sl, -1.0, None, OP.mult), deps=[tok_mod], vsmall=True)
        tok_mod = p.step("vector", lambda e, sl=sl: e.tensor_tensor(
            stA[:, 0:16], sl, stA[:, 0:16], OP.max), deps=[tok_mod], vsmall=True)
        tok_mod = p.step("vector", lambda e: e.tensor_scalar(
            stA[:, 0:16], stA[:, 0:16], 0.25, None, OP.max), deps=[tok_mod],
            vsmall=True)
        tok_sgn = p.step("scalar", lambda e, sl=sl: e.activation(
            stA[:, 16:32], sl, AF.Sign, bias=cst[:, 3:4]), deps=[tok_mod, tcst])
        tok_mod = p.step("vector", lambda e, sl=sl: e.tensor_tensor(
            sl, stA[:, 0:16], stA[:, 16:32], OP.mult), deps=[tok_sgn, tok_mod],
            vsmall=True)

    tw = p.step("gpsimd", lambda e: e.dma_start(out=wx_sb[:], in_=dwx[:]),
                deps=[tok_tr], sem="DG")
    tw = p.step("gpsimd", lambda e: e.dma_start(
        out=wdt_sb[:, 0:1024], in_=dwdt[0]), deps=[], sem="DG")
    tw = p.step("gpsimd", lambda e: e.dma_start(
        out=wdt_sb[:, 1024:2048], in_=dwdt[1]), deps=[], sem="DG")
    for h in range(4):
        m_, kh = divmod(h, 2)
        tw = p.step("gpsimd", lambda e, m_=m_, kh=kh, h=h: e.dma_start(
            out=wout_sb[:, h * 2048:(h + 1) * 2048],
            in_=dwout[m_].rearrange("(k p) j -> p k j", p=128)[:, kh * 4:(kh + 1) * 4, :]),
            deps=[], sem="DG")
    tdg = [None, None]
    for m in range(2):
        for k in range(KB):
            i = (m * KB + k) % 2
            sdg = "DG" if i == 0 else "DG2"
            tdg[i] = p.step("gpsimd", lambda e, m=m, k=k: e.dma_start(
                out=winS[m, 1, k],
                in_=dwin[m, k * 128:(k + 1) * 128, 1024:2048]), deps=[], sem=sdg)
    wld_sl = [x_lin[:, 0:1024], x_lin[:, 1024:2048],
              x_lin[:, 2048:3072], x_lin[:, 3072:4096]]
    wot_sl = [hc_b[:, 0:1024], hc_b[:, 1024:2048],
              hc_b[:, 2048:3072], hc_b[:, 3072:4096]]
    bcs_sl = [y_c[:, 0:1024], y_c[:, 1024:2048],
              y_c[:, 2048:3072], y_c[:, 3072:4096]]
    fin_sl = wld_sl + [dA8[:, 0:1024], dA8[:, 1024:2048],
                       dA8[:, 2048:3072], dA8[:, 3072:4096]]

    p0c_exp = {}

    def p0c_gen():
        tbd = [None] * 8
        tld_w = [None] * 16
        tmm_w = [None] * 64
        tst_w = [None] * 64
        for m_ in range(2):
            for tap in range(4):
                r = m_ * 4 + tap
                tbd[r] = p.step("sync", lambda e, r=r, tap=tap: e.dma_start(
                    out=bcs_sl[tap],
                    in_=bass.AP(dwcr.tensor, r * 1024, [[0, 128], [1, 1024]])),
                    deps=[tmm_w[28 + tap] if m_ == 1 else
                          (tok_tr if tap < 2 else None)],
                    sem=("DL" if r % 2 == 0 else "DL2"))
            for k in range(KB):
                i = m_ * 8 + k
                tld_w[i] = p.step("sync", lambda e, m_=m_, k=k, i=i: e.dma_start(
                    out=wld_sl[i % 4], in_=dwin[m_, k * 128:(k + 1) * 128, 0:1024]),
                    deps=[tmm_w[(i - 4) * 4 + 3] if i >= 4 else None],
                    sem=f"WX{i % 4}")
                for tap in range(4):
                    j = i * 4 + tap
                    tmm_w[j] = p.step("gpsimd", lambda e, i=i, tap=tap, j=j:
                        e.tensor_tensor(
                            wot_sl[j % 4], wld_sl[i % 4], bcs_sl[tap], OP.mult),
                        deps=[tld_w[i], tbd[m_ * 4 + tap],
                              tst_w[j - 4] if j >= 4 else None])
                    tst_w[j] = p.step("gpsimd", lambda e, m_=m_, tap=tap, k=k,
                                      j=j: e.dma_start(
                        out=winX4[m_, tap, k], in_=wot_sl[j % 4]),
                        deps=[tmm_w[j]], sem="DG2")
                if i % 4 == 3:
                    yield
        p0c_exp["winx4"] = tst_w[63]
        tsn_f = [None] * 16
        tst_f = [None] * 16
        for i in range(16):
            mat, k = divmod(i, KB)
            sA_, sB_ = (dth, dthn) if mat == 0 else (dph, dphn)
            bias = PI2 if mat == 0 else 0.0
            a0, a1 = (2 * i) % 8, (2 * i + 1) % 8
            dep0 = ([tmm_w[63], tld_w[15]] if i < 4 else [tsn_f[i - 4]])
            tl1 = p.step("sync", lambda e, s=sA_, k=k, a0=a0: e.dma_start(
                out=fin_sl[a0], in_=s[k * 128:(k + 1) * 128, :]), deps=dep0,
                sem=f"FB{i % 4}")
            tl2 = p.step("sync", lambda e, s=sB_, k=k, a1=a1: e.dma_start(
                out=fin_sl[a1], in_=s[k * 128:(k + 1) * 128, :]), deps=dep0,
                sem=f"FB{i % 4}")
            tv = p.step("gpsimd", lambda e, a0=a0, a1=a1: e.tensor_tensor(
                fin_sl[a0], fin_sl[a0], fin_sl[a1], OP.add),
                deps=[tl1, tl2])
            tv = p.step("gpsimd", lambda e, a0=a0: e.tensor_scalar(
                fin_sl[a0], fin_sl[a0], 1.5, -1.5, OP.min, OP.max),
                deps=[tv])
            tsn_f[i] = p.step("scalar", lambda e, bias=bias, a0=a0, i=i:
                e.activation(
                    wot_sl[i % 4], fin_sl[a0], AF.Sin,
                    bias=(cst[:, 2:3] if bias != 0.0 else 0.0)),
                deps=[tv, tcst,
                      tst_f[i - 4] if i >= 4 else tst_w[60 + i]])
            tst_f[i] = p.step("gpsimd", lambda e, mat=mat, k=k, i=i: e.dma_start(
                out=ffnS[mat, k], in_=wot_sl[i % 4]), deps=[tsn_f[i]],
                sem=("DG" if i % 2 == 0 else "DG2"))
            if i % 4 == 3:
                yield

    _p0c = p0c_gen()

    def p0c_hook(g):
        next(_p0c, None)
        next(_p0c, None)
        if g == NCH - 1:
            next(_p0c, None)

    bar0 = p.bar()

    def phaseA(b, bar_in, hook=None):
        p.phase = "A"
        tds = []
        t1 = p.step("vector", lambda e, b=b: e.tensor_copy(
            stA[:, 0:8], modv[:, 16 + b:32 + b:2]), deps=bar_in, vsmall=True)
        t1 = p.step("vector", lambda e, b=b: e.tensor_copy(
            stA[:, 8:16], modv[:, 0 + b:16 + b:2]), deps=[t1], vsmall=True)
        t2 = p.step("tensor", lambda e: e.transpose(
            pA[0:8, 0:128], stA[:, 0:8], idf32[:]), deps=[t1, tif] + bar_in)
        t2 = p.step("tensor", lambda e: e.transpose(
            pA[0:8, 128:256], stA[:, 8:16], idf32[:]), deps=[t2])
        t3 = p.step("vector", lambda e: e.tensor_copy(row8[0:8, 0:256],
                                                      pA[0:8, 0:256]), deps=[t2],
                    vsmall=True)
        t4 = p.step("tensor", lambda e: [
            e.matmul(pA[:, 1024 + kd * 128:1024 + (kd + 1) * 128],
                     sel8[:, kd * 128:(kd + 1) * 128],
                     row8[0:8, 0:128], start=True, stop=True)
            for kd in range(KB)][-1], deps=[t3, ts8])
        t5 = p.step("tensor", lambda e: [
            e.matmul(pA[:, kd * 128:(kd + 1) * 128],
                     sel8[:, kd * 128:(kd + 1) * 128],
                     row8[0:8, 128:256], start=True, stop=True)
            for kd in range(KB)][-1], deps=[t4])
        tabc = p.step("vector", lambda e: e.tensor_copy(abc[:], pA[:, 1024:2048]),
                      deps=[t4] + bar_in)
        tsbc = p.step("vector", lambda e: e.tensor_copy(sbc[:], pA[:, 0:1024]),
                      deps=[t5, tabc])

        tdma = None
        prev_tr = tsbc
        prev_ttr_h1 = [tsbc] * 4
        for g in range(NCH):
            if hook is not None:
                hook(g)
            txt = []
            tstats = []
            for i in range(4):
                ti = g * 4 + i
                tl = p.step("sync" if hook is not None else "gpsimd",
                    lambda e, b=b, ti=ti, i=i: e.dma_start(
                        out=xt[i][:], in_=dx[b, ti * 128:(ti + 1) * 128, :]),
                    deps=[prev_tr] if g > 0 else bar_in,
                    sem=(f"XT{i}" if hook is not None else f"XG{i}"))
                txt.append(tl)
                tsv = p.step("vector", lambda e, i=i: e.bn_stats(
                    stA[:, 128:134], xt[i][:, 0:512]),
                    deps=[tl, tsbc, tstats[i - 1] if i > 0 else None], vsmall=True)
                tsv = p.step("vector", lambda e, i=i: e.bn_stats(
                    stA[:, 134:140], xt[i][:, 512:1024]), deps=[tsv], vsmall=True)
                tsv = p.step("vector", lambda e, i=i: e.bn_aggr(
                    stA[:, 32 + 2 * i:34 + 2 * i], stA[:, 128:140]), deps=[tsv],
                    vsmall=True)
                tstats.append(tsv)
            tin = p.step("scalar", lambda e: e.activation(
                stA[:, 144:148], stA[:, 33:41:2], AF.Ln, bias=cst[:, 0:1]),
                deps=tstats + [tok_acoef])
            tin = p.step("scalar", lambda e: e.activation(
                stA[:, 48:52], stA[:, 144:148], AF.Exp, scale=-0.5), deps=[tin])
            tnm = p.step("vector", lambda e: e.tensor_tensor(
                stA[:, 52:56], stA[:, 32:40:2], stA[:, 48:52], OP.mult),
                deps=[tin], vsmall=True)
            tnm = p.step("vector", lambda e: e.tensor_scalar(
                stA[:, 52:56], stA[:, 52:56], -1.0, None, OP.mult), deps=[tnm],
                vsmall=True)
            th1 = [None] * 4
            ttr_hm = [None] * 4
            ttr_h1 = [None] * 4
            for i in range(4):
                txh = p.step("scalar", lambda e, i=i: e.activation(
                    xhat[i % 2][:], xt[i][:], AF.Identity,
                    scale=stA[:, 48 + i:49 + i], bias=stA[:, 52 + i:53 + i]),
                    deps=[tnm, txt[i], th1[i - 2] if i >= 2 else None])
                tm1 = p.step("vector", lambda e, i=i: e.tensor_tensor(
                    h1t[i][:], xhat[i % 2][:], abc[:], OP.mult),
                    deps=[txh, tsbc, prev_ttr_h1[i]])
                tm2 = p.step("vector", lambda e, i=i: e.scalar_tensor_tensor(
                    h1t[i][:], h1t[i][:], 0.0, sbc[:], OP.bypass, OP.add,
                    accum_out=stA[:, 64 + i:65 + i]), deps=[tm1])
                tsq = p.step("scalar", lambda e, i=i: e.activation(
                    xhat[i % 2][:], h1t[i][:], AF.Square,
                    accum_out=stA[:, 80 + i:81 + i]), deps=[tm2, txh])
                th1[i] = tsq
            tv2 = p.step("vector", lambda e: e.tensor_scalar(
                stA[:, 96:100], stA[:, 64:68], 1.0 / D, None, OP.mult), deps=th1,
                vsmall=True)
            tv2 = p.step("vector", lambda e: e.tensor_tensor(
                stA[:, 100:104], stA[:, 96:100], stA[:, 96:100], OP.mult),
                deps=[tv2], vsmall=True)
            tv2 = p.step("vector", lambda e: e.tensor_scalar(
                stA[:, 104:108], stA[:, 80:84], 1.0 / D, None, OP.mult), deps=[tv2],
                vsmall=True)
            tv2 = p.step("vector", lambda e: e.tensor_tensor(
                stA[:, 104:108], stA[:, 104:108], stA[:, 100:104], OP.subtract),
                deps=[tv2], vsmall=True)
            ti2_ = p.step("scalar", lambda e: e.activation(
                stA[:, 148:152], stA[:, 104:108], AF.Ln, bias=cst[:, 0:1]),
                deps=[tv2] + th1)
            ti2_ = p.step("scalar", lambda e: e.activation(
                stA[:, 108:112], stA[:, 148:152], AF.Exp, scale=-0.5), deps=[ti2_])
            tnm2 = p.step("vector", lambda e: e.tensor_tensor(
                stA[:, 112:116], stA[:, 96:100], stA[:, 108:112], OP.mult),
                deps=[ti2_], vsmall=True)
            tnm2 = p.step("vector", lambda e: e.tensor_scalar(
                stA[:, 112:116], stA[:, 112:116], -1.0, None, OP.mult), deps=[tnm2],
                vsmall=True)
            for i in range(4):
                thh = p.step("scalar", lambda e, i=i: e.activation(
                    hh2[i % 2][:], h1t[i][:], AF.Identity,
                    scale=stA[:, 108 + i:109 + i], bias=stA[:, 112 + i:113 + i]),
                    deps=[tnm2, th1[i], ttr_hm[i - 2] if i >= 2 else None])
                ttr_hm[i] = p.step("tensor", lambda e, i=i: [
                    e.transpose(pAb[:, kd * 512 + i * 128:kd * 512 + (i + 1) * 128],
                                hh2[i % 2][:, kd * 128:(kd + 1) * 128], idents[:])
                    for kd in range(KB)][-1],
                    deps=[thh, tid, prev_tr if i < 2 else None,
                          tsbc if i < 2 else None])
                ttr_h1[i] = p.step("tensor", lambda e, i=i: [
                    e.transpose(pBb[:, kd * 512 + i * 128:kd * 512 + (i + 1) * 128],
                                h1t[i][:, kd * 128:(kd + 1) * 128], idents[:])
                    for kd in range(KB)][-1], deps=[th1[i], ttr_hm[i]])
            prev_ttr_h1 = ttr_h1
            tev = None
            for kd in range(KB):
                tev = p.step("vector", lambda e, kd=kd: e.tensor_scalar(
                    tst[:, kd * 512:(kd + 1) * 512], pAb[:, kd * 512:(kd + 1) * 512],
                    vcol(0, kd), vcol(8, kd), OP.mult, OP.add),
                    deps=[ttr_hm[3], tv0, tdma])
            td = None
            for kd in range(KB):
                td = p.step("gpsimd", lambda e, b=b, g=g, kd=kd: e.dma_start(
                    out=hmS[b, g, kd], in_=tst[:, kd * 512:(kd + 1) * 512]),
                    deps=[tev], sem="DG")
            tds.append(td)
            tev2 = None
            for kd in range(KB):
                tev2 = p.step("vector", lambda e, kd=kd: e.tensor_copy(
                    tst[:, kd * 512:(kd + 1) * 512], pBb[:, kd * 512:(kd + 1) * 512]),
                    deps=[ttr_h1[3], td])
            for kd in range(KB):
                tdma = p.step("gpsimd", lambda e, b=b, g=g, kd=kd: e.dma_start(
                    out=h1S[b, g, kd], in_=tst[:, kd * 512:(kd + 1) * 512]),
                    deps=[tev2], sem="DG")
            prev_tr = tev2
        return p.bar(), {"hmst": tds}

    state = {"out_dma": None}

    szxb = [xt[0].bitcast(BF16), xt[1].bitcast(BF16)]
    xs_b = x_lin.bitcast(BF16)[:, 4096:8192]

    kt_list4 = [(k, tap) for k in range(KB) for tap in range(4)]
    hmk = hm_all.rearrange("p (k t) -> p k t", k=KB)

    def mk_mctx(b, m):
        OFF = 3 if m == 0 else 0
        HOF = 0 if m == 0 else 512

        def sz_sl(c, j):
            if c % 2 == 0:
                return sz_c[:, j * 512:(j + 1) * 512]
            if m == 0:
                return szxb[j // 4][:, (j % 4) * 512:(j % 4 + 1) * 512]
            return wout_sb[:, j * 512:(j + 1) * 512]

        def rhs_hm0(k):
            sl = hm_all[:, k * 515 + OFF:k * 515 + OFF + 512]
            return sl[:, ::-1] if m == 1 else sl

        def rhs_x0(k, tap):
            if m == 0:
                return hm_all[:, k * 515 + tap:k * 515 + tap + 512]
            return hm_all[:, k * 515 + 3 - tap:k * 515 + 515 - tap][:, ::-1]

        return dict(b=b, m=m, OFF=OFF, HOF=HOF, sz_sl=sz_sl,
                    rhs_hm0=rhs_hm0, rhs_x0=rhs_x0)

    def mamba0(b, m, bar_in, pre0=None, nxt=None, pa_ext=None,
               prev_hand=None):
        p.phase = "pre"
        OFF = 3 if m == 0 else 0
        HOF = 0 if m == 0 else 512
        tok_wrl = None
        if m == 0 and b > 0:
            for h in range(2):
                tok_wrl = p.step("gpsimd", lambda e, h=h: e.dma_start(
                    out=wout_sb[:, h * 2048:(h + 1) * 2048],
                    in_=dwout[0].rearrange("(k p) j -> p k j", p=128)
                    [:, h * 4:(h + 1) * 4, :]),
                    deps=bar_in, sem="DG")
        tok_yf = None
        tok_scan_done = None
        tok_dbx_done = None
        tok_wout_done = None
        tok_traw7 = None
        st_xl = None
        carry_tok = {}
        wo_hist = {}
        yf_hist = {}
        pre_t = {}
        ctx = mk_mctx(b, m)
        sz_sl = ctx["sz_sl"]
        if pre0 is None:
            hz_dep = (([prev_hand["tail"]] if prev_hand is not None else [])
                      if pa_ext is not None else bar_in)
            tok_hz = p.step("vector", lambda e: e.memset(
                hmk[:, :, HOF:HOF + 3], 0.0), deps=hz_dep, vsmall=True)

        def early(ctx_, c, st, out):
            b = ctx_["b"]
            m = ctx_["m"]
            OFF = ctx_["OFF"]
            sz_sl = ctx_["sz_sl"]
            rhs_hm0 = ctx_["rhs_hm0"]
            rhs_x0 = ctx_["rhs_x0"]
            kt_list = kt_list4
            thm = [None] * KB
            for k in range(KB):
                cc = c if m == 0 else NCH - 1 - c
                thm[k] = p.step("sync", lambda e, cc=cc, k=k: e.dma_start(
                    out=hm_all[:, k * 515 + OFF:k * 515 + OFF + 512],
                    in_=hmS[b, cc, k]), deps=st["hm"], sem="HM")
            out["thm"] = thm
            yield

            def jps(j):
                if j < 4:
                    return pB[:, j * 512:(j + 1) * 512]
                return pA[:, (j - 4) * 512:(j - 3) * 512]
            twl = [None] * KB
            twlb = [None] * KB
            zmm = [None] * KB
            zmmb = [None] * KB
            szl = [None] * KB
            out["sz"] = szl
            for k in range(4):
                twl[k] = p.step("sync", lambda e, k=k: e.dma_start(
                    out=win_b[k][:], in_=winS[m, 1, k]),
                    deps=[thm[k], st["zbuf"]], sem=f"WB{k}")
            yield
            for k in range(4):
                zmm[k] = p.step("tensor", lambda e, k=k: [
                    e.matmul(jps(j)[:, 0:512],
                             win_b[k][:, j * 128:(j + 1) * 128],
                             rhs_hm0(k), start=(k == 0), stop=(k == KB - 1))
                    for j in range(6)][-1],
                    deps=[twl[k]] + ([st["pb"]] if k == 0 else []))
                twl[k + 4] = p.step("sync", lambda e, k=k: e.dma_start(
                    out=win_b[k][:], in_=winS[m, 1, k + 4]),
                    deps=[thm[k + 4], zmm[k]], sem=f"WB{k}")
            yield
            for k in range(4, KB):
                zmm[k] = p.step("tensor", lambda e, k=k: [
                    e.matmul(jps(j)[:, 0:512],
                             win_b[k - 4][:, j * 128:(j + 1) * 128],
                             rhs_hm0(k), start=(k == 0), stop=(k == KB - 1))
                    for j in range(6)][-1], deps=[twl[k]])
            yield
            for jj in range(6):
                szl[jj] = p.step("scalar", lambda e, jj=jj, c=c:
                    e.activation(sz_sl(c, jj), jps(jj)[:, 0:512], AF.Silu),
                    deps=[zmm[KB - 1]] + ([st["szfree"]] if jj == 0 else []))
            for k in range(4):
                twlb[k] = p.step("sync", lambda e, k=k: e.dma_start(
                    out=win_b[k][:, 0:256], in_=winS[m, 1, k][:, 768:1024]),
                    deps=[thm[k], zmm[k + 4]], sem=f"WB{k}")
            yield
            for k in range(4):
                zmmb[k] = p.step("tensor", lambda e, k=k: [
                    e.matmul(pB[:, j * 512:(j + 1) * 512],
                             win_b[k][:, j * 128:(j + 1) * 128],
                             rhs_hm0(k), start=(k == 0), stop=(k == KB - 1))
                    for j in range(2)][-1],
                    deps=[twlb[k]] + ([szl[5]] if k == 0 else []))
                twlb[k + 4] = p.step("sync", lambda e, k=k: e.dma_start(
                    out=win_b[k][:, 0:256],
                    in_=winS[m, 1, k + 4][:, 768:1024]),
                    deps=[thm[k + 4], zmmb[k]], sem=f"WB{k}")
            for k in range(4, KB):
                zmmb[k] = p.step("tensor", lambda e, k=k: [
                    e.matmul(pB[:, j * 512:(j + 1) * 512],
                             win_b[k - 4][:, j * 128:(j + 1) * 128],
                             rhs_hm0(k), start=(k == 0), stop=(k == KB - 1))
                    for j in range(2)][-1], deps=[twlb[k]])
            yield
            for jj in range(6, KB):
                szl[jj] = p.step("scalar", lambda e, jj=jj, c=c:
                    e.activation(sz_sl(c, jj),
                                 pB[:, (jj - 6) * 512:(jj - 5) * 512], AF.Silu),
                    deps=[zmmb[KB - 1]])
            out["zlast"] = zmmb[KB - 1]
            tw2 = [None] * 32
            xmm = [None] * 32
            for idx in range(4):
                k, tap = kt_list[idx]
                tw2[idx] = p.step("sync", lambda e, k=k, tap=tap, idx=idx:
                    e.dma_start(out=win_b[4 + idx % 4][:], in_=winX4[m, tap, k]),
                    deps=[thm[k], st["xbuf"]], sem=f"WX{idx % 4}")
            yield
            for g in range(4):
                for idx in range(g * 8, g * 8 + 8):
                    k, tap = kt_list[idx]
                    xmm[idx] = p.step("tensor", lambda e, k=k, tap=tap, idx=idx: [
                        e.matmul(jps(j)[:, 0:512],
                                 win_b[4 + idx % 4][:, j * 128:(j + 1) * 128],
                                 rhs_x0(k, tap), start=(idx == 0),
                                 stop=(idx == 31))
                        for j in range(6)][-1],
                        deps=[tw2[idx]] + ([szl[7]] if idx == 0 else []))
                    if idx + 4 < 32:
                        k2, tap2 = kt_list[idx + 4]
                        tw2[idx + 4] = p.step("sync",
                            lambda e, k2=k2, tap2=tap2, idx=idx: e.dma_start(
                                out=win_b[4 + (idx + 4) % 4][:],
                                in_=winX4[m, tap2, k2]),
                            deps=[thm[k2], xmm[idx]], sem=f"WX{(idx + 4) % 4}")
                yield
            sil = [None] * KB
            out["si"] = sil
            for j in range(6):
                sil[j] = p.step("scalar", lambda e, j=j: e.activation(
                    xs_b[:, j * 512:(j + 1) * 512],
                    jps(j)[:, 0:512], AF.Silu,
                    bias=vcol(32 + m * 8, j)),
                    deps=[xmm[31]] + ([st["xlfree"]] if j == 0 else []))
            tw2b = [None] * 32
            xmmb = [None] * 32
            for idx in range(4):
                k, tap = kt_list[idx]
                tw2b[idx] = p.step("sync", lambda e, k=k, tap=tap, idx=idx:
                    e.dma_start(out=win_b[4 + idx % 4][:, 0:256],
                                in_=winX4[m, tap, k][:, 768:1024]),
                    deps=[thm[k], xmm[28 + idx]], sem=f"WX{idx % 4}")
            yield
            for g in range(2):
                for idx in range(g * 16, g * 16 + 16):
                    k, tap = kt_list[idx]
                    xmmb[idx] = p.step("tensor", lambda e, k=k, tap=tap, idx=idx: [
                        e.matmul(pB[:, j * 512:(j + 1) * 512],
                                 win_b[4 + idx % 4][:, j * 128:(j + 1) * 128],
                                 rhs_x0(k, tap), start=(idx == 0),
                                 stop=(idx == 31))
                        for j in range(2)][-1],
                        deps=[tw2b[idx]] + ([sil[5]] if idx == 0 else []))
                    if idx + 4 < 32:
                        k2, tap2 = kt_list[idx + 4]
                        tw2b[idx + 4] = p.step("sync",
                            lambda e, k2=k2, tap2=tap2, idx=idx: e.dma_start(
                                out=win_b[4 + (idx + 4) % 4][:, 0:256],
                                in_=winX4[m, tap2, k2][:, 768:1024]),
                            deps=[thm[k2], xmmb[idx]], sem=f"WX{(idx + 4) % 4}")
                if g == 0:
                    yield
            for j in range(2):
                sil[6 + j] = p.step("scalar", lambda e, j=j: e.activation(
                    xs_b[:, (6 + j) * 512:(7 + j) * 512],
                    pB[:, j * 512:(j + 1) * 512], AF.Silu,
                    bias=vcol(32 + m * 8, 6 + j)), deps=[xmmb[31]])
            if m == 0:
                out["tail"] = p.step("vector", lambda e: e.tensor_copy(
                    hmk[:, :, 0:3], hmk[:, :, 512:515]),
                    deps=[xmmb[31], zmmb[KB - 1]], vsmall=True)
            else:
                out["tail"] = p.step("vector", lambda e: e.tensor_copy(
                    hmk[:, :, 512:515], hmk[:, :, 0:3]),
                    deps=[xmmb[31], zmmb[KB - 1]], vsmall=True)
            out["xlast"] = xmmb[31]
            out["pb_done"] = sil[7]

        def mk_st(c):
            if c == 0:
                bi = list(bar_in)
                st0 = {"pb": bi, "szfree": bi, "xlfree": bi,
                       "hm": bi + [tok_hz], "zbuf": bi, "xbuf": bi}
                if pa_ext is not None:
                    cc0 = 0 if m == 0 else NCH - 1
                    st0["hm"] = ([pa_ext["hmst"][cc0], tok_hz]
                                 + ([prev_hand["tail"]]
                                    if prev_hand is not None else []))
                    st0["zbuf"] = ([tdg[0], tdg[1]] if prev_hand is None
                                   else [prev_hand["zlast"]])
                    st0["xbuf"] = ([p0c_exp["winx4"]] if prev_hand is None
                                   else [prev_hand["xlast"]])
                return st0
            return {"hm": [pre_t[c - 1]["tail"]],
                    "zbuf": pre_t[c - 1]["zlast"],
                    "pb": tok_traw7,
                    "szfree": ([wo_hist[c - 2], yf_hist[c - 2],
                                state.get(("epi_q1", b, c - 2))]
                               if c >= 2 else list(bar_in)),
                    "xbuf": pre_t[c - 1]["xlast"],
                    "xlfree": st_xl}

        if pre0 is None:
            out0 = {}
            for _ in early(ctx, 0, mk_st(0), out0):
                pass
            pre_t[0] = out0
        else:
            pre_t[0] = pre0

        for c in range(NCH):
            p.phase = "pre"
            pre = pre_t[c]
            sil = pre["si"]
            szl = pre["sz"]
            def mmwx(e):
                for k in range(KB):
                    base = m * 512 + k * 64
                    e.matmul(pB[0:RNK, 0:512], wx_sb[:, base:base + 32],
                             xs_b[:, k * 512:(k + 1) * 512],
                             start=(k == 0), stop=(k == KB - 1))
                    e.matmul(pB[0:16, 512:1024], wx_sb[:, base + 32:base + 48],
                             xs_b[:, k * 512:(k + 1) * 512],
                             start=(k == 0), stop=(k == KB - 1))
                    i = e.matmul(pB[0:16, 1024:1536],
                                 wx_sb[:, base + 48:base + 64],
                                 xs_b[:, k * 512:(k + 1) * 512],
                                 start=(k == 0), stop=(k == KB - 1))
                return i
            tok_wx = p.step("tensor", mmwx,
                            deps=[sil[KB - 1], tok_wout_done]
                            + (bar_in if c == 0 else []))
            tok_dtr = p.step("scalar", lambda e: e.activation(
                dtr_b[:], pB[0:RNK, 0:512], AF.Identity), deps=[tok_wx])
            tok_bcr = p.step("scalar", lambda e, c=c: (
                e.activation(bcB[:, c * 512:(c + 1) * 512],
                             pB[0:16, 512:1024], AF.Identity),
                e.activation(bcC[:, c * 512:(c + 1) * 512],
                             pB[0:16, 1024:1536], AF.Identity)
            )[-1], deps=[tok_dtr])
            tok_dt = [None] * KB
            traw_l = [None] * KB
            for kd in range(KB):
                bank = kd % 4
                tmm = p.step("tensor", lambda e, kd=kd, bank=bank: e.matmul(
                    pB[:, bank * 512:(bank + 1) * 512],
                    wdt_sb[:, m * 1024 + kd * 128:m * 1024 + (kd + 1) * 128],
                    dtr_b[:], start=True, stop=True),
                    deps=[tok_dtr, tok_bcr if kd in (1, 2) else None,
                          tok_dt[kd - 4] if kd >= 4 else None])
                traw_l[kd] = p.step("scalar", lambda e, kd=kd, bank=bank:
                    e.activation(
                        dt_c[:, kd * 512:(kd + 1) * 512],
                        pB[:, bank * 512:(bank + 1) * 512],
                        AF.Identity, bias=vcol(48 + m * 8, kd)),
                    deps=[tmm, tok_scan_done if kd == 0 else None])
                tab = p.step("scalar", lambda e, kd=kd: e.activation(
                    dA8[:, kd * 512:(kd + 1) * 512],
                    dt_c[:, kd * 512:(kd + 1) * 512],
                    AF.Abs), deps=[traw_l[kd],
                                   tok_scan_done if kd < 2 else None])
                tab = p.step("scalar", lambda e, kd=kd: e.activation(
                    dA8[:, kd * 512:(kd + 1) * 512],
                    dA8[:, kd * 512:(kd + 1) * 512],
                    AF.Exp, scale=-1.0), deps=[tab])
                tab = p.step("scalar", lambda e, kd=kd: e.activation(
                    dA8[:, kd * 512:(kd + 1) * 512],
                    dA8[:, kd * 512:(kd + 1) * 512],
                    AF.Ln, bias=cst[:, 1:2]), deps=[tab])
                tok_dt[kd] = p.step("vector", lambda e, kd=kd:
                    e.scalar_tensor_tensor(
                        dt_c[:, kd * 512:(kd + 1) * 512],
                        dt_c[:, kd * 512:(kd + 1) * 512],
                        0.0, dA8[:, kd * 512:(kd + 1) * 512],
                        OP.max, OP.add), deps=[tab])
            tok_traw7 = traw_l[KB - 1]
            tok_dtx_w = p.step("vector", lambda e: e.tensor_tensor(
                dtx_c[:, 0:4096], dt_c[:, 0:4096], xs_b[:, 0:4096], OP.mult),
                deps=[tok_dt[KB - 1], tok_dbx_done if c > 0 else None])
            tok_dtx = [tok_dtx_w] * KB
            tok_yi = [None] * KB
            for kd in range(KB):
                tok_yi[kd] = p.step("scalar", lambda e, kd=kd: e.activation(
                    y_c[:, kd * 512:(kd + 1) * 512],
                    xs_b[:, kd * 512:(kd + 1) * 512],
                    AF.Identity, scale=vcol(16 + m * 8, kd)),
                    deps=[sil[kd], tok_dt[KB - 1], tok_yf])
            st_xl = [tok_dtx[KB - 1], tok_yi[KB - 1]]
            p.phase = "scan"
            out_nxt = None
            if c + 1 < NCH:
                out_n = {}
                gen = early(ctx, c + 1, mk_st(c + 1), out_n)
                pre_t[c + 1] = out_n
            elif nxt is not None:
                nctx = mk_mctx(*nxt)
                tok_hz2 = p.step("vector", lambda e: e.memset(
                    hmk[:, :, nctx["HOF"]:nctx["HOF"] + 3], 0.0),
                    deps=[pre_t[c]["tail"]], vsmall=True)
                st_n = {"hm": [pre_t[c]["tail"], tok_hz2],
                        "zbuf": pre_t[c]["zlast"],
                        "pb": tok_traw7,
                        "szfree": [wo_hist[c - 2], yf_hist[c - 2],
                                   state.get(("epi_q1", b, c - 2))],
                        "xbuf": pre_t[c]["xlast"],
                        "xlfree": st_xl}
                out_nxt = {}
                gen = early(nctx, 0, st_n, out_nxt)
            else:
                gen = None
            pc = c % 2
            tok_exp_p = [None] * KB
            tok_scan_p = [None] * KB
            tok_dbx_g = None
            tok_prod = [None] * KB
            tok_prod_g = None
            tok_acc_g = None
            tok_acc_gG = None
            tok_accV = None
            tok_accG = None
            tevB_p = None
            tevC_p = None
            for n in range(NST):
                tbcB = p.step("tensor", lambda e, n=n, c=c: e.matmul(
                    pA[:, 1024:1536],
                    selB[:, n * 128:(n + 1) * 128],
                    bcB[:, c * 512:(c + 1) * 512], start=True, stop=True),
                    deps=[tok_bcr, tsbc_l, tevB_p,
                          tok_dbx_g if n >= 2 else tok_dt[KB - 1]])
                tbcC = p.step("tensor", lambda e, n=n, c=c: e.matmul(
                    pA[:, 1536:2048],
                    selB[:, n * 128:(n + 1) * 128],
                    bcC[:, c * 512:(c + 1) * 512],
                    start=True, stop=True),
                    deps=[tbcB, tevC_p, tok_prod_g if n >= 2 else None])
                par = n % 2
                tevB = p.step("scalar", lambda e, par=par: e.activation(
                    bcbuf[:, par * 512:(par + 1) * 512],
                    pA[:, 1024:1536], AF.Identity),
                    deps=[tbcB, tok_dbx_g if n >= 2 else None])
                tevC = p.step("scalar", lambda e, par=par: e.activation(
                    bcbuf[:, 1024 + par * 512:1024 + (par + 1) * 512],
                    pA[:, 1536:2048], AF.Identity),
                    deps=[tbcC, tok_prod_g if n >= 2 else None])
                tevB_p = tevB
                tevC_p = tevC
                tok_exp = [None] * KB
                for kd in range(KB):
                    war = tok_scan_p[kd] if n > 0 else tok_dt[KB - 1]
                    tok_exp[kd] = p.step("scalar", lambda e, kd=kd, n=n:
                        e.activation(
                            dA8[:, kd * 512:(kd + 1) * 512],
                            dt_c[:, kd * 512:(kd + 1) * 512], AF.Exp,
                            scale=acoef[:, m * 128 + kd * 16 + n:
                                        m * 128 + kd * 16 + n + 1]),
                        deps=[tok_dt[kd] if n == 0 else tok_exp_p[kd], war,
                              tok_acoef])
                brep = bass.AP(scr16.tensor, par * 512,
                               [[2048, 128], [0, KB], [1, 512]])
                crep = bass.AP(scr16.tensor, 1024 + par * 512,
                               [[2048, 128], [0, KB], [1, 512]])
                tok_dbx_w = p.step("vector", lambda e, brep=brep:
                    e.tensor_tensor(dBx_b[:, 0:4096], dtx_c[:, 0:4096],
                                    brep, OP.mult),
                    deps=[tevB, tok_dtx[KB - 1],
                          tok_accV if n > 0 else None,
                          tok_accG if n > 0 else None])
                tok_dbx = [tok_dbx_w] * KB
                tok_dbx_g = tok_dbx_w
                tok_scan = [None] * KB
                for kd in range(KB):
                    if c == 0:
                        init = 0.0
                        depc = None
                    else:
                        init = carr[1 - pc][:, n * 8 + kd:n * 8 + kd + 1]
                        depc = carry_tok[(c - 1, n)]
                    tok_scan[kd] = p.step("vector",
                        lambda e, kd=kd, init=init: e.tensor_tensor_scan(
                            hc_b[:, kd * 512:(kd + 1) * 512],
                            dA8[:, kd * 512:(kd + 1) * 512],
                            dBx_b[:, kd * 512:(kd + 1) * 512],
                            init, OP.mult, OP.add),
                        deps=[tok_exp[kd], tok_dbx[kd], depc,
                              tok_prod[kd] if n > 0 else None])
                tok_scan_p = tok_scan
                tok_exp_p = tok_exp
                tcar = p.step("vector", lambda e, n=n, pc=pc: e.tensor_copy(
                    carr[pc][:, n * 8:(n + 1) * 8], hc_b[:, 511::512]),
                    deps=[tok_scan[KB - 1]])
                carry_tok[(c, n)] = tcar
                tok_prod_w = p.step("vector", lambda e, crep=crep:
                    e.tensor_tensor(dBx_b[:, 0:4096], hc_b[:, 0:4096],
                                    crep, OP.mult),
                    deps=[tevC, tok_scan[KB - 1],
                          tok_accG if n > 0 else None])
                tok_prod = [tok_prod_w] * KB
                tok_prod_g = tok_prod_w
                tok_accV = p.step("vector", lambda e: e.tensor_tensor(
                    y_c[:, 0:2560], dBx_b[:, 0:2560], y_c[:, 0:2560], OP.add),
                    deps=[tok_prod_w,
                          tok_yi[4] if n == 0 else None])
                tok_accG = p.step("gpsimd", lambda e: e.tensor_tensor(
                    y_c[:, 2560:4096], dBx_b[:, 2560:4096],
                    y_c[:, 2560:4096], OP.add),
                    deps=[tok_prod_w, tok_accG if n > 0 else tok_yi[KB - 1]])
                tok_acc_g = tok_accV
                tok_acc_gG = tok_accG
                if gen is not None:
                    next(gen, None)
            if gen is not None:
                for _ in gen:
                    pass
            tok_scan_done = [tok_acc_g, tok_acc_gG]
            tok_dbx_done = [tok_acc_g, tok_acc_gG]
            def sz_h(cc_, h):
                if cc_ % 2 == 0:
                    return sz_c[:, h * 2048:(h + 1) * 2048]
                if m == 0:
                    return szxb[h][:, 0:2048]
                return wout_sb[:, h * 2048:(h + 1) * 2048]
            tok_yf1 = p.step("vector", lambda e, c=c: e.tensor_tensor(
                sz_h(c, 0), y_c[:, 0:2048], sz_h(c, 0), OP.mult),
                deps=[tok_accV, tok_accG, szl[KB - 1]])
            tok_yf2 = p.step("vector", lambda e, c=c: e.tensor_tensor(
                sz_h(c, 1), y_c[:, 2048:4096], sz_h(c, 1), OP.mult),
                deps=[tok_accV, tok_accG])
            tok_yf_l = [tok_yf1] * 4 + [tok_yf2] * 4
            tok_yf = tok_yf2
            yf_hist[c] = tok_yf
            p.phase = "pre"
            tok_wo = None
            for k in range(KB):
                tok_wo = p.step("tensor", lambda e, c=c, k=k: [
                    e.matmul(pB[:, j * 512:(j + 1) * 512],
                             wout_sb[:, m * 4096 + k * 512 + j * 128:
                                     m * 4096 + k * 512 + (j + 1) * 128],
                             sz_sl(c, k), start=(k == 0), stop=(k == KB - 1))
                    for j in range(4)][-1],
                    deps=[tok_yf_l[k], tok_dt[KB - 1] if k == 0 else None,
                          tok_wrl if k == 0 else None]
                    + (([pre_t[c + 1]["pb_done"]] if c + 1 < NCH else
                        ([out_nxt["pb_done"]] if out_nxt is not None else []))
                       if k == 0 else []))
            wo_hist[c] = tok_wo
            if m == 0:
                tev = None
                tdma = None
                prev_f = state.get(("fwd_dma", c - 1))
                for j in range(4):
                    tev = p.step("scalar", lambda e, j=j: e.activation(
                        scr16[:, j * 512:(j + 1) * 512],
                        pB[:, j * 512:(j + 1) * 512], AF.Identity),
                        deps=[tok_wo, prev_f])
                    tdma = p.step("gpsimd", lambda e, j=j, c=c: e.dma_start(
                        out=fwdS[j, :, c * 512:(c + 1) * 512],
                        in_=scr16[:, j * 512:(j + 1) * 512]),
                        deps=[tev], sem="DG")
                state[("fwd_dma", c)] = tdma
                tok_wout_done = [tev, tdma]
            else:
                tok_wout_done = epilogue(b, c, tok_wo, st_xl,
                                         lambda j, c=c: sz_sl(c, j),
                                         tok_sd=tok_scan_done)
        hand = {"tail": pre_t[NCH - 1]["tail"],
                "zlast": pre_t[NCH - 1]["zlast"],
                "xlast": pre_t[NCH - 1]["xlast"]}
        return p.bar(), out_nxt, hand


    def epilogue(b, c, tok_wo, tok_xfree, stash_sl, tok_sd=None):
        p.phase = "epi"
        th1l = [None] * KB
        for k in range(KB):
            th1l[k] = p.step("sync", lambda e, b=b, c=c, k=k: e.dma_start(
                out=h1c[:, k * 512:(k + 1) * 512], in_=h1S[b, c, k]),
                deps=[tok_xfree, state.get("epi_gin")], sem="HC")
        tfa = [None] * KB
        for k in range(KB):
            tfa[k] = p.step("sync", lambda e, k=k: e.dma_start(
                out=ffn_b[k][:], in_=ffnS[0, k]),
                deps=[state.get(("ffn_rb", k))], sem=f"FB{k}")
        tgt = [None] * KB
        tgag_p = [None] * KB
        tgin_p = [None] * KB
        for dblk in range(KB):
            if dblk < 4:
                tfl = p.step("gpsimd", lambda e, dblk=dblk, c=c: e.dma_start(
                    out=gag_b[dblk % 2][:], in_=fwdS[dblk, :, c * 512:(c + 1) * 512]),
                    deps=[state[("fwd_dma", c)],
                          tgt[dblk - 2] if dblk >= 2 else
                          (tok_sd if tok_sd is not None else tok_wo)],
                    sem=f"GG{dblk % 2}")
                tgin = p.step("vector", lambda e, dblk=dblk: e.tensor_tensor(
                    gin_b[dblk % 2][:], gag_b[dblk % 2][:],
                    h1c[:, dblk * 512:(dblk + 1) * 512], OP.add),
                    deps=[tfl, th1l[dblk], tgag_p[dblk - 2] if dblk >= 2 else None])
            else:
                tgin = p.step("vector", lambda e, dblk=dblk: e.tensor_tensor(
                    gin_b[dblk % 2][:], pB[:, (dblk - 4) * 512:(dblk - 3) * 512],
                    h1c[:, dblk * 512:(dblk + 1) * 512], OP.add),
                    deps=[tok_wo, th1l[dblk], tgag_p[dblk - 2]])
            tgin_p[dblk] = tgin
            tgel = p.step("scalar", lambda e, dblk=dblk: e.activation(
                go_b[dblk % 2][:], gin_b[dblk % 2][:], AF.Gelu),
                deps=[tgin, tgag_p[dblk - 2] if dblk >= 2 else None])
            tgag = p.step("scalar", lambda e, dblk=dblk, b=b: e.activation(
                gag_b[dblk % 2][:], go_b[dblk % 2][:], AF.Identity,
                scale=mcol(2, dblk, b)),
                deps=[tgel, tgt[dblk - 2] if dblk >= 2 else None])
            tgag_p[dblk] = tgag
            tgt[dblk] = p.step("tensor", lambda e, dblk=dblk: [
                e.transpose(pAb[:, i * 1024 + dblk * 128:i * 1024 + (dblk + 1) * 128],
                            gag_b[dblk % 2][:, i * 128:(i + 1) * 128], idents[:])
                for i in range(4)][-1], deps=[tgag])
        state["epi_gin"] = tgin_p[KB - 1]
        txt = [None] * 4
        tstats = [None] * 4
        for i in range(4):
            tl = p.step("gpsimd", lambda e, b=b, c=c, i=i: e.dma_start(
                out=xt[i][:], in_=dx[b, (c * 4 + i) * 128:(c * 4 + i + 1) * 128, :]),
                deps=[tok_xfree, state["out_dma"]], sem=f"XG{i}")
            txt[i] = p.step("vector", lambda e, i=i: e.tensor_tensor(
                xt[i][:], xt[i][:], pAb[:, i * 1024:(i + 1) * 1024], OP.add),
                deps=[tl, tgt[KB - 1]])
            tsv = p.step("vector", lambda e, i=i: e.bn_stats(
                stA[:, 128:134], xt[i][:, 0:512]),
                deps=[txt[i], tstats[i - 1] if i > 0 else None], vsmall=True)
            tsv = p.step("vector", lambda e, i=i: e.bn_stats(
                stA[:, 134:140], xt[i][:, 512:1024]), deps=[tsv], vsmall=True)
            tstats[i] = p.step("vector", lambda e, i=i: e.bn_aggr(
                stA[:, 32 + 2 * i:34 + 2 * i], stA[:, 128:140]), deps=[tsv],
                vsmall=True)
        tin = p.step("scalar", lambda e: e.activation(
            stA[:, 144:148], stA[:, 33:41:2], AF.Ln, bias=cst[:, 0:1]), deps=tstats)
        tin = p.step("scalar", lambda e: e.activation(
            stA[:, 48:52], stA[:, 144:148], AF.Exp, scale=-0.5), deps=[tin])
        tnm = p.step("vector", lambda e: e.tensor_tensor(
            stA[:, 52:56], stA[:, 32:40:2], stA[:, 48:52], OP.mult), deps=[tin],
            vsmall=True)
        tnm = p.step("vector", lambda e: e.tensor_scalar(
            stA[:, 52:56], stA[:, 52:56], -1.0, None, OP.mult), deps=[tnm],
            vsmall=True)
        ttr = [None] * 4
        for i in range(4):
            txh = p.step("scalar", lambda e, i=i: e.activation(
                xhat[i % 2][:], xt[i][:], AF.Identity,
                scale=stA[:, 48 + i:49 + i], bias=stA[:, 52 + i:53 + i]),
                deps=[tnm, txt[i], ttr[i - 2] if i >= 2 else None])
            ttr[i] = p.step("tensor", lambda e, i=i: [
                e.transpose(pAb[:, kd * 512 + i * 128:kd * 512 + (i + 1) * 128],
                            xhat[i % 2][:, kd * 128:(kd + 1) * 128], idents[:])
                for kd in range(KB)][-1],
                deps=[txh, txt[3]])
        tevh = [None] * KB
        for kd in range(KB):
            tevh[kd] = p.step("scalar", lambda e, kd=kd, b=b: e.activation(
                h2f[:, kd * 512:(kd + 1) * 512], pAb[:, kd * 512:(kd + 1) * 512],
                AF.Identity, scale=mcol(4, kd, b), bias=mcol(3, kd, b)),
                deps=[ttr[3]])
        tok_h2 = tevh[KB - 1]
        mmA = [[None] * KB, [None] * KB]
        for h in range(2):
            ps_h = pA if h == 0 else pB
            for k in range(KB):
                mmA[h][k] = p.step("tensor", lambda e, k=k, h=h, ps_h=ps_h: [
                    e.matmul(ps_h[:, j * 512:(j + 1) * 512],
                             ffn_b[k][:, (h * 4 + j) * 128:(h * 4 + j + 1) * 128],
                             h2f[:, k * 512:(k + 1) * 512],
                             start=(k == 0), stop=(k == KB - 1))
                    for j in range(4)][-1],
                    deps=[tfa[k], tok_h2 if k == 0 else None,
                          txt[3] if k == 0 and h == 0 else None])
        tfb = [None] * KB
        for k in range(KB):
            tfb[k] = p.step("gpsimd", lambda e, k=k: e.dma_start(
                out=ffn_b[k][:], in_=ffnS[1, k]),
                deps=[mmA[1][k]], sem=f"FG{k % 4}")
        tevA = [None] * KB
        for j in range(KB):
            tevA[j] = p.step("scalar", lambda e, j=j: e.activation(
                stash_sl(j),
                (pA if j < 4 else pB)[:, (j % 4) * 512:(j % 4 + 1) * 512],
                AF.Identity),
                deps=[mmA[j // 4][KB - 1], tok_wo if j == 0 else None])
        mmB = [[None] * KB, [None] * KB]
        for h in range(2):
            ps_h = pA if h == 0 else pB
            for k in range(KB):
                mmB[h][k] = p.step("tensor", lambda e, k=k, h=h, ps_h=ps_h: [
                    e.matmul(ps_h[:, j * 512:(j + 1) * 512],
                             ffn_b[k][:, (h * 4 + j) * 128:(h * 4 + j + 1) * 128],
                             h2f[:, k * 512:(k + 1) * 512],
                             start=(k == 0), stop=(k == KB - 1))
                    for j in range(4)][-1],
                    deps=[tfb[k] if h == 0 else None,
                          tevA[3 + h * 4] if k == 0 else None])
                if h == 1:
                    state[("ffn_rb", k)] = mmB[1][k]
        tok_tr_l = None
        for j in range(KB):
            tq1 = p.step("vector", lambda e, j=j: e.tensor_tensor(
                re2[:], stash_sl(j), stash_sl(j), OP.mult),
                deps=[mmB[j // 4][KB - 1], tok_tr_l])
            if j == KB - 1:
                state[("epi_q1", b, c)] = tq1
            tq2 = p.step("scalar", lambda e, j=j: e.activation(
                im2[:], (pA if j < 4 else pB)[:, (j % 4) * 512:(j % 4 + 1) * 512],
                AF.Square), deps=[tq1])
            tad = p.step("vector", lambda e: e.tensor_tensor(
                re2[:], re2[:], im2[:], OP.add), deps=[tq2])
            tsr = p.step("scalar", lambda e: e.activation(
                re2[:], re2[:], AF.Sqrt), deps=[tad])
            tok_tr_l = p.step("scalar", lambda e, j=j, b=b: e.activation(
                rb8[:, j * 512:(j + 1) * 512], re2[:], AF.Identity,
                scale=mcol(5, j, b)), deps=[tsr])
        tok_t2 = p.step("tensor", lambda e: [
            e.transpose(pAb[:, i * 1024 + kd * 128:i * 1024 + (kd + 1) * 128],
                        rb8[:, kd * 512 + i * 128:kd * 512 + (i + 1) * 128],
                        idents[:])
            for kd in range(KB) for i in range(4)][-1],
            deps=[tok_tr_l, txt[3], ttr[3]])
        tdo = None
        for i in range(4):
            tx2 = p.step("vector", lambda e, i=i: e.tensor_tensor(
                xt[i][:], xt[i][:], pAb[:, i * 1024:(i + 1) * 1024], OP.add),
                deps=[tok_t2, txt[i], tok_tr_l])
            tdo = p.step("gpsimd", lambda e, b=b, c=c, i=i: e.dma_start(
                out=dout[b, (c * 4 + i) * 128:(c * 4 + i + 1) * 128, :],
                in_=xt[i][:]), deps=[tx2], sem="DG")
        state["out_dma"] = tdo
        return tdo

    barrier = [tok_mod, tok_tr, ts8, tid, tif, tcst, tv0, tok_acoef]
    hand = None
    for b in range(BL):
        barrier, pa_ext = phaseA(b, barrier,
                                 hook=(p0c_hook if b == 0 else None))
        barrier, pre_m1, hand = mamba0(b, 0, barrier, nxt=(b, 1),
                                       pa_ext=pa_ext, prev_hand=hand)
        barrier, _, hand = mamba0(b, 1, barrier, pre0=pre_m1)

    p.step("gpsimd", lambda e: e.nop(), deps=p.bar())

    import contextlib
    stack = contextlib.ExitStack()
    sems = {}
    for s in COMPUTE_SEMS + DMA_SEMS:
        sems[s] = stack.enter_context(nc.semaphore(s))
    with nc.Block() as block:
        @block.sync
        def _(e):
            p.emit("sync", e, sems)

        @block.vector
        def _(e):
            p.emit("vector", e, sems)

        @block.scalar
        def _(e):
            p.emit("scalar", e, sems)

        @block.tensor
        def _(e):
            p.emit("tensor", e, sems)

        @block.gpsimd
        def _(e):
            p.emit("gpsimd", e, sems)
    stack.close()
    return nc


_CACHE = {}


def _host_layout(inputs):
    bf = ml_dtypes.bfloat16
    ident = np.eye(128, dtype=np.float32)
    i2f = np.eye(2, dtype=np.float32)
    ones = np.ones((1, 128), np.float32).astype(bf)
    sel8 = np.zeros((8, 1024), np.float32)
    for kd in range(8):
        sel8[kd, kd * 128:(kd + 1) * 128] = 1.0
    selbc = np.zeros((16, 2048), np.float32)
    for n in range(16):
        selbc[n, n * 128:(n + 1) * 128] = 1.0

    def packvec(v):
        return np.ascontiguousarray(v.reshape(KB, 128).T)

    vecs = np.zeros((128, 384), np.float32)
    vecs[:, 0:8] = packvec(inputs["L0_attn_ln_g"])
    vecs[:, 8:16] = packvec(inputs["L0_attn_ln_b"])
    for m, tag in enumerate(("m1", "m2")):
        vecs[:, 16 + m * 8:24 + m * 8] = packvec(inputs[f"L0_attn_{tag}_Dp"])
        vecs[:, 32 + m * 8:40 + m * 8] = packvec(inputs[f"L0_attn_{tag}_bconv"])
        vecs[:, 48 + m * 8:56 + m * 8] = packvec(inputs[f"L0_attn_{tag}_bdt"])
        wc = inputs[f"L0_attn_{tag}_Wconv"].reshape(KB, 128, 4)
        vecs[:, 64 + m * 32:96 + m * 32] = wc.transpose(1, 0, 2).reshape(128, 32)
        al = inputs[f"L0_attn_{tag}_Alog"].reshape(KB, 128, NST)
        vecs[:, 128 + m * 128:256 + m * 128] = al.transpose(1, 0, 2).reshape(128, 128)

    win = np.stack([inputs["L0_attn_m1_Win"], inputs["L0_attn_m2_Win"]])
    wout = np.stack([inputs["L0_attn_m1_Wout"], inputs["L0_attn_m2_Wout"]])
    wx = np.concatenate(
        [inputs[f"L0_attn_{tag}_Wx"].reshape(KB, 128, 64).transpose(1, 0, 2)
         .reshape(128, 512) for tag in ("m1", "m2")], axis=1)
    wdt = np.stack([inputs["L0_attn_m1_Wdt"], inputs["L0_attn_m2_Wdt"]])

    wconvr = np.zeros((8, 1024), np.float32)
    for m, tag in enumerate(("m1", "m2")):
        wc = inputs[f"L0_attn_{tag}_Wconv"]
        for tap in range(4):
            wconvr[m * 4 + tap] = wc[:, tap]
    common = {
        "win": np.ascontiguousarray(win),
        "wout": np.ascontiguousarray(wout),
        "wx": np.ascontiguousarray(wx),
        "wdt": np.ascontiguousarray(wdt),
        "wmod": np.ascontiguousarray(inputs["L0_wmod"]),
        "vecs": vecs,
        "theta": np.ascontiguousarray(inputs["L0_ffn_theta"]),
        "theta_n": np.ascontiguousarray(inputs["L0_ffn_theta_noise"]),
        "phi": np.ascontiguousarray(inputs["L0_ffn_phi"]),
        "phi_n": np.ascontiguousarray(inputs["L0_ffn_phi_noise"]),
        "idents": ident.astype(bf),
        "idf32": ident,
        "i2f": i2f,
        "onesr": ones,
        "sel8": sel8.astype(bf),
        "selbc": selbc.astype(bf),
        "wconvr": wconvr,
    }
    in_maps = []
    x = inputs["L0_attn_x"]
    c = inputs["L0_c"]
    for core in range(NCORES):
        mm = dict(common)
        mm["x"] = np.ascontiguousarray(x[2 * core:2 * core + 2])
        cc = c[2 * core:2 * core + 2]
        c2t = cc.T.reshape(KB, 128, 2).transpose(1, 0, 2).reshape(128, 16)
        mm["c2t"] = np.ascontiguousarray(c2t)
        in_maps.append(mm)
    return in_maps


def kernel(**inputs):
    if "nc" not in _CACHE:
        _CACHE["nc"] = build_program()
    nc = _CACHE["nc"]
    in_maps = _host_layout(inputs)
    res = run_bass_kernel_spmd(nc, in_maps, core_ids=list(range(NCORES)))
    out = np.empty((16, T, D), np.float32)
    for core in range(NCORES):
        out[2 * core:2 * core + 2] = res.results[core]["out"]
    return out


# version 44
# speedup vs baseline: 1.0035x; 1.0035x over previous
import os
import numpy as np
import ml_dtypes
import concourse.bass as bass
import concourse.mybir as mybir
from concourse.bass_utils import run_bass_kernel_spmd

F32 = mybir.dt.float32
BF16 = mybir.dt.bfloat16
F32R = mybir.dt.float32r
AF = mybir.ActivationFunctionType
OP = mybir.AluOpType

NCORES = 8
BL = 2
T = 2048
D = 1024
KB = 8
NST = 16
RNK = 32
NCH = 4
TC = 512
EPS = 1e-5
PI2 = float(np.pi / 2)

COMPUTE_SEMS = ("V", "S", "T", "G")
DMA_SEMS = ("DL", "DL2", "DG", "DG2", "XT0", "XT1", "XT2", "XT3", "WB0",
             "WB1", "WB2", "WB3", "HM", "HC", "FB0", "FB1", "FB2", "FB3",
             "FB4", "FB5", "FB6", "FB7", "GG0", "GG1", "WX0", "WX1",
             "WX2", "WX3", "XG0", "XG1", "XG2", "XG3", "FG0", "FG1",
             "FG2", "FG3")
OWN_SEM = {"vector": "V", "scalar": "S", "tensor": "T", "gpsimd": "G",
           "sync": None}


class Prog:

    def __init__(self):
        self.steps = []
        self.count = {s: 0 for s in COMPUTE_SEMS + DMA_SEMS}
        self.phase = "P0"

    def step(self, eng, fn, deps=(), sem=None, inc=None, hwdrain=False,
             vsmall=False):
        if sem is None:
            sem = OWN_SEM[eng] or "DL"
        if inc is None:
            inc = 16 if sem in DMA_SEMS else 1
        flat = []
        for d in deps:
            if d is None:
                continue
            for t in (d if isinstance(d, list) else [d]):
                if t is None:
                    continue
                if t[0] in DMA_SEMS:
                    t = (t[0], self.count[t[0]])
                flat.append(t)
        if sem in DMA_SEMS and self.count[sem] >= 8 * 16                 and (self.count[sem] // 16) % 8 == 0:
            flat.append((sem, self.count[sem]))
        self.count[sem] += inc
        tok = (sem, self.count[sem])
        self.steps.append((eng, fn, tuple(flat), sem, inc, hwdrain, self.phase,
                           vsmall))
        return tok

    def bar(self):
        return [(s, self.count[s]) for s in COMPUTE_SEMS + DMA_SEMS
                if self.count[s] > 0]

    def emit(self, eng_name, e, sems):
        dr = os.environ.get("KERNEL_DRAINS", "")
        use_drains = (os.environ.get("BASS_SIM_FP64") == "1"
                      or os.environ.get("KERNEL_SIM_DRAINS") == "1"
                      or {"vector": "V", "scalar": "S", "tensor": "T",
                          "gpsimd": "G"}.get(eng_name, "?") in dr)
        waited = {}
        own_emitted = 0
        drained_at = 0
        my = OWN_SEM[eng_name]
        small_pending = False
        for (s_eng, fn, deps, sem, inc, hwdrain, phase, vsmall) in self.steps:
            if s_eng != eng_name:
                continue
            need_drain = False
            for (dsem, dval) in deps:
                if dsem == my:
                    if dval > drained_at:
                        need_drain = True
                else:
                    if waited.get(dsem, 0) < dval:
                        e.wait_ge(sems[dsem], dval)
                        waited[dsem] = dval
            if need_drain and (use_drains or hwdrain
                               or (eng_name == "vector" and small_pending)):
                e.drain()
                drained_at = own_emitted
                small_pending = False
            last = fn(e)
            if vsmall and eng_name == "vector":
                small_pending = True
            last.then_inc(sems[sem], inc)
            if sem == my:
                own_emitted += inc


def build_program():
    nc = bass.Bass("TRN2", target_bir_lowering=False)

    dx = nc.dram_tensor("x", [BL, T, D], F32, kind="ExternalInput").ap()
    dwin = nc.dram_tensor("win", [2, D, 2 * D], F32, kind="ExternalInput").ap()
    dwout = nc.dram_tensor("wout", [2, D, 512], F32, kind="ExternalInput").ap()
    dwx = nc.dram_tensor("wx", [128, 1024], F32, kind="ExternalInput").ap()
    dwdt = nc.dram_tensor("wdt", [2, RNK, D], F32, kind="ExternalInput").ap()
    dwmod = nc.dram_tensor("wmod", [D, 6 * D], F32, kind="ExternalInput").ap()
    dc2t = nc.dram_tensor("c2t", [128, 16], F32, kind="ExternalInput").ap()
    dvecs = nc.dram_tensor("vecs", [128, 384], F32, kind="ExternalInput").ap()
    dth = nc.dram_tensor("theta", [D, D], F32, kind="ExternalInput").ap()
    dthn = nc.dram_tensor("theta_n", [D, D], F32, kind="ExternalInput").ap()
    dph = nc.dram_tensor("phi", [D, D], F32, kind="ExternalInput").ap()
    dphn = nc.dram_tensor("phi_n", [D, D], F32, kind="ExternalInput").ap()
    dids = nc.dram_tensor("idents", [128, 128], BF16, kind="ExternalInput").ap()
    di2f = nc.dram_tensor("i2f", [2, 2], F32, kind="ExternalInput").ap()
    didf = nc.dram_tensor("idf32", [128, 128], F32, kind="ExternalInput").ap()
    dones = nc.dram_tensor("onesr", [1, 128], BF16, kind="ExternalInput").ap()
    dsel8 = nc.dram_tensor("sel8", [8, 1024], BF16, kind="ExternalInput").ap()
    dselbc = nc.dram_tensor("selbc", [16, 2048], BF16, kind="ExternalInput").ap()
    dwcr = nc.dram_tensor("wconvr", [8, 1024], F32, kind="ExternalInput").ap()
    dout = nc.dram_tensor("out", [BL, T, D], F32, kind="ExternalOutput").ap()

    winS = nc.dram_tensor("winS", [2, 2, KB, 128, D], BF16, kind="Internal").ap()
    winX4 = nc.dram_tensor("winX4", [2, 4, KB, 128, D], BF16, kind="Internal").ap()
    ffnS = nc.dram_tensor("ffnS", [2, KB, 128, D], BF16, kind="Internal").ap()
    h1S = nc.dram_tensor("h1S", [BL, NCH, KB, 128, TC], BF16, kind="Internal").ap()
    hmS = nc.dram_tensor("hmS", [BL, NCH, KB, 128, TC], BF16, kind="Internal").ap()
    fwdS = nc.dram_tensor("fwdS", [4, 128, T], BF16, kind="Internal").ap()

    A = lambda n, s, d: nc.alloc_sbuf_tensor(n, s, d).ap()
    vecs = A("vecs_s", [128, 384], F32)
    acoef = A("acoef", [128, 256], F32)
    modv = A("modv", [128, 96], F32)
    idents = A("idents_s", [128, 128], BF16)
    idf32 = A("idf32_s", [128, 128], F32)
    i2f = A("i2f_s", [2, 2], F32)
    onesr = A("onesr_s", [1, 128], BF16)
    sel8 = A("sel8_s", [8, 1024], BF16)
    row8 = A("row8", [8, 256], BF16)
    c2sf = A("c2sf", [128, 16], F32)
    c2sb = A("c2sb", [128, 16], BF16)
    wdt_sb = A("wdt_sb", [RNK, 2 * D], BF16)
    wx_sb = A("wx_sb", [128, 1024], BF16)
    wout_sb = A("wout_sb", [128, 8192], BF16)
    win_b = [A(f"win_b{i}", [128, D], BF16) for i in range(8)]
    hm_all = A("hm_all", [128, KB * 515], BF16)
    ffn_b = [A(f"ffn_b{i}", [128, D], BF16) for i in range(8)]
    xt = [A(f"xt{i}", [128, D], F32) for i in range(4)]
    stA = A("stA", [128, 160], F32)
    xhat = [A(f"xhat{i}", [128, D], BF16) for i in range(2)]
    hh2 = xhat
    x_lin = A("x_lin", [128, KB * TC], F32)
    tailb = A("tailb", [128, 32], BF16)
    dt_c = A("dt_c", [128, KB * TC], F32)
    dtx_c = A("dtx_c", [128, KB * TC], BF16)
    y_c = A("y_c", [128, KB * TC], F32)
    sz_c = A("sz_c", [128, KB * TC], BF16)
    bcB = A("bcB", [16, T], BF16)
    bcC = A("bcC", [16, T], BF16)
    selB = A("selB_s", [16, 2048], BF16)
    dA8 = A("dA8", [128, KB * TC], F32)
    dBx_b = A("dBx_b", [128, KB * TC], BF16)
    hc_b = A("hc_b", [128, KB * TC], BF16)
    dtr_b = hc_b[0:RNK, 0:TC]
    h1t = [dt_c.bitcast(BF16)[:, i * 1024:(i + 1) * 1024] for i in range(4)]
    gag_b = [dA8.bitcast(BF16)[:, 0:512], dA8.bitcast(BF16)[:, 512:1024]]
    carr = [A(f"carr{i}", [128, 128], F32) for i in range(2)]
    scr16 = A("scr16", [128, 2048], BF16)
    gin_b = [scr16[:, 0:512], scr16[:, 512:1024]]
    go_b = [A(f"go_b{i}", [128, TC], F32) for i in range(2)]
    cst = A("cst", [128, 4], F32)

    bcbuf = scr16
    re2 = go_b[0]
    im2 = go_b[1]
    tst = dBx_b
    p0st = dt_c[:, 2048:4096]
    stgb = dBx_b[:, 0:2048]
    mrow = y_c[0:2, 0:2048]
    abc = sz_c[:, 0:1024]
    sbc = sz_c[:, 1024:2048]
    h1c = x_lin.bitcast(BF16)[:, 0:4096]
    h2f = dt_c.bitcast(BF16)[:, 0:4096]
    stg32b = y_c[:, 2048:4096]
    wcr = y_c[0:8, 0:1024]
    stgb2 = dBx_b[:, 2048:4096]
    rb8 = dt_c.bitcast(BF16)[:, 4096:8192]

    pA = nc.alloc_psum_tensor("pA", [128, 2048], F32).ap()
    pB = nc.alloc_psum_tensor("pB", [128, 2048], F32).ap()
    pAb = pA.bitcast(BF16)
    pBb = pB.bitcast(BF16)

    p = Prog()

    def mcol(g, dblk, b):
        c0 = (g * 8 + dblk) * 2 + b
        return modv[:, c0:c0 + 1]

    def vcol(base, dblk):
        return vecs[:, base + dblk:base + dblk + 1]

    tv0 = p.step("sync", lambda e: e.dma_start(out=vecs[:], in_=dvecs[:]))
    tcst = p.step("vector", lambda e: (e.memset(cst[:, 0:1], EPS),
                                       e.memset(cst[:, 1:2], 1.0),
                                       e.memset(cst[:, 2:3], PI2),
                                       e.memset(cst[:, 3:4], 1e-30))[-1])
    tid = p.step("sync", lambda e: e.dma_start(out=idents[:], in_=dids[:]))
    tif = p.step("sync", lambda e: e.dma_start(out=idf32[:], in_=didf[:]))
    ti2 = p.step("sync", lambda e: e.dma_start(out=i2f[:], in_=di2f[:]))
    ton = p.step("sync", lambda e: e.dma_start(out=onesr[:], in_=dones[:]))
    ts8 = p.step("sync", lambda e: e.dma_start(out=sel8[:], in_=dsel8[:]))
    tsbc_l = p.step("sync", lambda e: e.dma_start(out=selB[:], in_=dselbc[:]))
    tc2 = p.step("sync", lambda e: e.dma_start(out=c2sf[:], in_=dc2t[:]))

    tk = p.step("scalar", lambda e: e.activation(acoef[:], vecs[:, 128:384], AF.Exp),
                deps=[tv0])
    tok_acoef = p.step("scalar", lambda e: e.activation(
        acoef[:], acoef[:], AF.Identity, scale=-1.0), deps=[tk])
    tok_c2s = p.step("scalar", lambda e: e.activation(c2sf[:], c2sf[:], AF.Silu),
                     deps=[tc2])
    tok_c2s = p.step("vector", lambda e: e.tensor_copy(c2sb[:], c2sf[:]),
                     deps=[tok_c2s])

    tok_tr = ti2
    tok_ev = None
    tok_k = None
    for r in range(3):
        for k in range(KB):
            sl = slice(r * 2048, (r + 1) * 2048)
            tl = p.step("sync", lambda e, k=k, sl=sl: e.dma_start(
                out=p0st[:], in_=dwmod[k * 128:(k + 1) * 128, sl]),
                deps=[tok_c2s, tok_k])
            tcv = p.step("vector", lambda e: e.tensor_copy(stgb[:], p0st[:]),
                         deps=[tl, tok_k])
            tok_k = p.step("tensor", lambda e, k=k: [
                e.matmul(pA[0:2, q * 512:(q + 1) * 512],
                         c2sb[:, k * 2:(k + 1) * 2],
                         stgb[:, q * 512:(q + 1) * 512],
                         start=(k == 0), stop=(k == KB - 1)) for q in range(4)][-1],
                deps=[tcv, tok_c2s, tok_ev if k == 0 else None])
        tok_ev = p.step("vector", lambda e: e.tensor_copy(mrow, pA[0:2, :]),
                        deps=[tok_k])
        tok_tr = p.step("tensor", lambda e, r=r: [
            e.transpose(pB[:, (r * 16 + j) * 2:(r * 16 + j) * 2 + 2],
                        mrow[0:2, j * 128:(j + 1) * 128], i2f[:])
            for j in range(16)][-1], deps=[tok_ev, tok_tr])
    tok_modp = p.step("vector", lambda e: e.tensor_copy(modv[:], pB[:, 0:96]),
                      deps=[tok_tr])

    def modfix(e):
        for g in (1, 4):
            i = e.tensor_scalar(modv[:, g * 16:(g + 1) * 16],
                                modv[:, g * 16:(g + 1) * 16], 1.0, None, OP.add)
        return i
    tok_mod = p.step("vector", modfix, deps=[tok_modp], vsmall=True)
    for g in (2, 5):
        sl = modv[:, g * 16:(g + 1) * 16]
        tok_mod = p.step("vector", lambda e, sl=sl: e.tensor_scalar(
            stA[:, 0:16], sl, -1.0, None, OP.mult), deps=[tok_mod], vsmall=True)
        tok_mod = p.step("vector", lambda e, sl=sl: e.tensor_tensor(
            stA[:, 0:16], sl, stA[:, 0:16], OP.max), deps=[tok_mod], vsmall=True)
        tok_mod = p.step("vector", lambda e: e.tensor_scalar(
            stA[:, 0:16], stA[:, 0:16], 0.25, None, OP.max), deps=[tok_mod],
            vsmall=True)
        tok_sgn = p.step("scalar", lambda e, sl=sl: e.activation(
            stA[:, 16:32], sl, AF.Sign, bias=cst[:, 3:4]), deps=[tok_mod, tcst])
        tok_mod = p.step("vector", lambda e, sl=sl: e.tensor_tensor(
            sl, stA[:, 0:16], stA[:, 16:32], OP.mult), deps=[tok_sgn, tok_mod],
            vsmall=True)

    tw = p.step("gpsimd", lambda e: e.dma_start(out=wx_sb[:], in_=dwx[:]),
                deps=[tok_tr], sem="DG")
    tw = p.step("gpsimd", lambda e: e.dma_start(
        out=wdt_sb[:, 0:1024], in_=dwdt[0]), deps=[], sem="DG")
    tw = p.step("gpsimd", lambda e: e.dma_start(
        out=wdt_sb[:, 1024:2048], in_=dwdt[1]), deps=[], sem="DG")
    for h in range(4):
        m_, kh = divmod(h, 2)
        tw = p.step("gpsimd", lambda e, m_=m_, kh=kh, h=h: e.dma_start(
            out=wout_sb[:, h * 2048:(h + 1) * 2048],
            in_=dwout[m_].rearrange("(k p) j -> p k j", p=128)[:, kh * 4:(kh + 1) * 4, :]),
            deps=[], sem="DG")
    tdg = [None, None]
    for m in range(2):
        for k in range(KB):
            i = (m * KB + k) % 2
            sdg = "DG" if i == 0 else "DG2"
            tdg[i] = p.step("gpsimd", lambda e, m=m, k=k: e.dma_start(
                out=winS[m, 1, k],
                in_=dwin[m, k * 128:(k + 1) * 128, 1024:2048]), deps=[], sem=sdg)
    wld_sl = [x_lin[:, 0:1024], x_lin[:, 1024:2048],
              x_lin[:, 2048:3072], x_lin[:, 3072:4096]]
    wot_sl = [hc_b[:, 0:1024], hc_b[:, 1024:2048],
              hc_b[:, 2048:3072], hc_b[:, 3072:4096]]
    bcs_sl = [y_c[:, 0:1024], y_c[:, 1024:2048],
              y_c[:, 2048:3072], y_c[:, 3072:4096]]
    fin_sl = wld_sl + [dA8[:, 0:1024], dA8[:, 1024:2048],
                       dA8[:, 2048:3072], dA8[:, 3072:4096]]

    p0c_exp = {}

    def p0c_gen():
        tbd = [None] * 8
        tld_w = [None] * 16
        tmm_w = [None] * 64
        tst_w = [None] * 64
        for m_ in range(2):
            for tap in range(4):
                r = m_ * 4 + tap
                tbd[r] = p.step("sync", lambda e, r=r, tap=tap: e.dma_start(
                    out=bcs_sl[tap],
                    in_=bass.AP(dwcr.tensor, r * 1024, [[0, 128], [1, 1024]])),
                    deps=[tmm_w[28 + tap] if m_ == 1 else
                          (tok_tr if tap < 2 else None)],
                    sem=("DL" if r % 2 == 0 else "DL2"))
            for k in range(KB):
                i = m_ * 8 + k
                tld_w[i] = p.step("sync", lambda e, m_=m_, k=k, i=i: e.dma_start(
                    out=wld_sl[i % 4], in_=dwin[m_, k * 128:(k + 1) * 128, 0:1024]),
                    deps=[tmm_w[(i - 4) * 4 + 3] if i >= 4 else None],
                    sem=f"WX{i % 4}")
                for tap in range(4):
                    j = i * 4 + tap
                    tmm_w[j] = p.step("gpsimd", lambda e, i=i, tap=tap, j=j:
                        e.tensor_tensor(
                            wot_sl[j % 4], wld_sl[i % 4], bcs_sl[tap], OP.mult),
                        deps=[tld_w[i], tbd[m_ * 4 + tap],
                              tst_w[j - 4] if j >= 4 else None])
                    tst_w[j] = p.step("gpsimd", lambda e, m_=m_, tap=tap, k=k,
                                      j=j: e.dma_start(
                        out=winX4[m_, tap, k], in_=wot_sl[j % 4]),
                        deps=[tmm_w[j]], sem="DG2")
                if i % 4 == 3:
                    yield
        p0c_exp["winx4"] = tst_w[63]
        tsn_f = [None] * 16
        tst_f = [None] * 16
        for i in range(16):
            mat, k = divmod(i, KB)
            sA_, sB_ = (dth, dthn) if mat == 0 else (dph, dphn)
            bias = PI2 if mat == 0 else 0.0
            a0, a1 = (2 * i) % 8, (2 * i + 1) % 8
            dep0 = ([tmm_w[63], tld_w[15]] if i < 4 else [tsn_f[i - 4]])
            tl1 = p.step("sync", lambda e, s=sA_, k=k, a0=a0: e.dma_start(
                out=fin_sl[a0], in_=s[k * 128:(k + 1) * 128, :]), deps=dep0,
                sem=f"FB{i % 4}")
            tl2 = p.step("sync", lambda e, s=sB_, k=k, a1=a1: e.dma_start(
                out=fin_sl[a1], in_=s[k * 128:(k + 1) * 128, :]), deps=dep0,
                sem=f"FB{i % 4}")
            tv = p.step("gpsimd", lambda e, a0=a0, a1=a1: e.tensor_tensor(
                fin_sl[a0], fin_sl[a0], fin_sl[a1], OP.add),
                deps=[tl1, tl2])
            tv = p.step("gpsimd", lambda e, a0=a0: e.tensor_scalar(
                fin_sl[a0], fin_sl[a0], 1.5, -1.5, OP.min, OP.max),
                deps=[tv])
            tsn_f[i] = p.step("scalar", lambda e, bias=bias, a0=a0, i=i:
                e.activation(
                    wot_sl[i % 4], fin_sl[a0], AF.Sin,
                    bias=(cst[:, 2:3] if bias != 0.0 else 0.0)),
                deps=[tv, tcst,
                      tst_f[i - 4] if i >= 4 else tst_w[60 + i]])
            tst_f[i] = p.step("gpsimd", lambda e, mat=mat, k=k, i=i: e.dma_start(
                out=ffnS[mat, k], in_=wot_sl[i % 4]), deps=[tsn_f[i]],
                sem=("DG" if i % 2 == 0 else "DG2"))
            if i % 4 == 3:
                yield

    _p0c = p0c_gen()

    def p0c_hook(g):
        next(_p0c, None)
        next(_p0c, None)
        if g == NCH - 1:
            next(_p0c, None)

    bar0 = p.bar()

    def phaseA(b, bar_in, hook=None):
        p.phase = "A"
        tds = []
        t1 = p.step("vector", lambda e, b=b: e.tensor_copy(
            stA[:, 0:8], modv[:, 16 + b:32 + b:2]), deps=bar_in, vsmall=True)
        t1 = p.step("vector", lambda e, b=b: e.tensor_copy(
            stA[:, 8:16], modv[:, 0 + b:16 + b:2]), deps=[t1], vsmall=True)
        t2 = p.step("tensor", lambda e: e.transpose(
            pA[0:8, 0:128], stA[:, 0:8], idf32[:]), deps=[t1, tif] + bar_in)
        t2 = p.step("tensor", lambda e: e.transpose(
            pA[0:8, 128:256], stA[:, 8:16], idf32[:]), deps=[t2])
        t3 = p.step("vector", lambda e: e.tensor_copy(row8[0:8, 0:256],
                                                      pA[0:8, 0:256]), deps=[t2],
                    vsmall=True)
        t4 = p.step("tensor", lambda e: [
            e.matmul(pA[:, 1024 + kd * 128:1024 + (kd + 1) * 128],
                     sel8[:, kd * 128:(kd + 1) * 128],
                     row8[0:8, 0:128], start=True, stop=True)
            for kd in range(KB)][-1], deps=[t3, ts8])
        t5 = p.step("tensor", lambda e: [
            e.matmul(pA[:, kd * 128:(kd + 1) * 128],
                     sel8[:, kd * 128:(kd + 1) * 128],
                     row8[0:8, 128:256], start=True, stop=True)
            for kd in range(KB)][-1], deps=[t4])
        tabc = p.step("vector", lambda e: e.tensor_copy(abc[:], pA[:, 1024:2048]),
                      deps=[t4] + bar_in)
        tsbc = p.step("vector", lambda e: e.tensor_copy(sbc[:], pA[:, 0:1024]),
                      deps=[t5, tabc])

        tdma = None
        prev_tr = tsbc
        prev_ttr_h1 = [tsbc] * 4
        for g in range(NCH):
            if hook is not None:
                hook(g)
            txt = []
            tstats = []
            for i in range(4):
                ti = g * 4 + i
                tl = p.step("sync" if hook is not None else "gpsimd",
                    lambda e, b=b, ti=ti, i=i: e.dma_start(
                        out=xt[i][:], in_=dx[b, ti * 128:(ti + 1) * 128, :]),
                    deps=[prev_tr] if g > 0 else bar_in,
                    sem=(f"XT{i}" if hook is not None else f"XG{i}"))
                txt.append(tl)
                tsv = p.step("vector", lambda e, i=i: e.bn_stats(
                    stA[:, 128:134], xt[i][:, 0:512]),
                    deps=[tl, tsbc, tstats[i - 1] if i > 0 else None], vsmall=True)
                tsv = p.step("vector", lambda e, i=i: e.bn_stats(
                    stA[:, 134:140], xt[i][:, 512:1024]), deps=[tsv], vsmall=True)
                tsv = p.step("vector", lambda e, i=i: e.bn_aggr(
                    stA[:, 32 + 2 * i:34 + 2 * i], stA[:, 128:140]), deps=[tsv],
                    vsmall=True)
                tstats.append(tsv)
            tin = p.step("scalar", lambda e: e.activation(
                stA[:, 144:148], stA[:, 33:41:2], AF.Ln, bias=cst[:, 0:1]),
                deps=tstats + [tok_acoef])
            tin = p.step("scalar", lambda e: e.activation(
                stA[:, 48:52], stA[:, 144:148], AF.Exp, scale=-0.5), deps=[tin])
            tnm = p.step("vector", lambda e: e.tensor_tensor(
                stA[:, 52:56], stA[:, 32:40:2], stA[:, 48:52], OP.mult),
                deps=[tin], vsmall=True)
            tnm = p.step("vector", lambda e: e.tensor_scalar(
                stA[:, 52:56], stA[:, 52:56], -1.0, None, OP.mult), deps=[tnm],
                vsmall=True)
            th1 = [None] * 4
            ttr_hm = [None] * 4
            ttr_h1 = [None] * 4
            for i in range(4):
                txh = p.step("scalar", lambda e, i=i: e.activation(
                    xhat[i % 2][:], xt[i][:], AF.Identity,
                    scale=stA[:, 48 + i:49 + i], bias=stA[:, 52 + i:53 + i]),
                    deps=[tnm, txt[i], th1[i - 2] if i >= 2 else None])
                tm1 = p.step("vector", lambda e, i=i: e.tensor_tensor(
                    h1t[i][:], xhat[i % 2][:], abc[:], OP.mult),
                    deps=[txh, tsbc, prev_ttr_h1[i]])
                tm2 = p.step("vector", lambda e, i=i: e.scalar_tensor_tensor(
                    h1t[i][:], h1t[i][:], 0.0, sbc[:], OP.bypass, OP.add,
                    accum_out=stA[:, 64 + i:65 + i]), deps=[tm1])
                tsq = p.step("scalar", lambda e, i=i: e.activation(
                    xhat[i % 2][:], h1t[i][:], AF.Square,
                    accum_out=stA[:, 80 + i:81 + i]), deps=[tm2, txh])
                th1[i] = tsq
            tv2 = p.step("vector", lambda e: e.tensor_scalar(
                stA[:, 96:100], stA[:, 64:68], 1.0 / D, None, OP.mult), deps=th1,
                vsmall=True)
            tv2 = p.step("vector", lambda e: e.tensor_tensor(
                stA[:, 100:104], stA[:, 96:100], stA[:, 96:100], OP.mult),
                deps=[tv2], vsmall=True)
            tv2 = p.step("vector", lambda e: e.tensor_scalar(
                stA[:, 104:108], stA[:, 80:84], 1.0 / D, None, OP.mult), deps=[tv2],
                vsmall=True)
            tv2 = p.step("vector", lambda e: e.tensor_tensor(
                stA[:, 104:108], stA[:, 104:108], stA[:, 100:104], OP.subtract),
                deps=[tv2], vsmall=True)
            ti2_ = p.step("scalar", lambda e: e.activation(
                stA[:, 148:152], stA[:, 104:108], AF.Ln, bias=cst[:, 0:1]),
                deps=[tv2] + th1)
            ti2_ = p.step("scalar", lambda e: e.activation(
                stA[:, 108:112], stA[:, 148:152], AF.Exp, scale=-0.5), deps=[ti2_])
            tnm2 = p.step("vector", lambda e: e.tensor_tensor(
                stA[:, 112:116], stA[:, 96:100], stA[:, 108:112], OP.mult),
                deps=[ti2_], vsmall=True)
            tnm2 = p.step("vector", lambda e: e.tensor_scalar(
                stA[:, 112:116], stA[:, 112:116], -1.0, None, OP.mult), deps=[tnm2],
                vsmall=True)
            for i in range(4):
                thh = p.step("scalar", lambda e, i=i: e.activation(
                    hh2[i % 2][:], h1t[i][:], AF.Identity,
                    scale=stA[:, 108 + i:109 + i], bias=stA[:, 112 + i:113 + i]),
                    deps=[tnm2, th1[i], ttr_hm[i - 2] if i >= 2 else None])
                ttr_hm[i] = p.step("tensor", lambda e, i=i: [
                    e.transpose(pAb[:, kd * 512 + i * 128:kd * 512 + (i + 1) * 128],
                                hh2[i % 2][:, kd * 128:(kd + 1) * 128], idents[:])
                    for kd in range(KB)][-1],
                    deps=[thh, tid, prev_tr if i < 2 else None,
                          tsbc if i < 2 else None])
                ttr_h1[i] = p.step("tensor", lambda e, i=i: [
                    e.transpose(pBb[:, kd * 512 + i * 128:kd * 512 + (i + 1) * 128],
                                h1t[i][:, kd * 128:(kd + 1) * 128], idents[:])
                    for kd in range(KB)][-1], deps=[th1[i], ttr_hm[i]])
            prev_ttr_h1 = ttr_h1
            tev = None
            for kd in range(KB):
                tev = p.step("vector", lambda e, kd=kd: e.tensor_scalar(
                    tst[:, kd * 512:(kd + 1) * 512], pAb[:, kd * 512:(kd + 1) * 512],
                    vcol(0, kd), vcol(8, kd), OP.mult, OP.add),
                    deps=[ttr_hm[3], tv0, tdma])
            td = None
            for kd in range(KB):
                td = p.step("gpsimd", lambda e, b=b, g=g, kd=kd: e.dma_start(
                    out=hmS[b, g, kd], in_=tst[:, kd * 512:(kd + 1) * 512]),
                    deps=[tev], sem="DG")
            tds.append(td)
            tev2 = None
            for kd in range(KB):
                tev2 = p.step("vector", lambda e, kd=kd: e.tensor_copy(
                    tst[:, kd * 512:(kd + 1) * 512], pBb[:, kd * 512:(kd + 1) * 512]),
                    deps=[ttr_h1[3], td])
            for kd in range(KB):
                tdma = p.step("gpsimd", lambda e, b=b, g=g, kd=kd: e.dma_start(
                    out=h1S[b, g, kd], in_=tst[:, kd * 512:(kd + 1) * 512]),
                    deps=[tev2], sem="DG")
            prev_tr = tev2
        return p.bar(), {"hmst": tds}

    state = {"out_dma": None}

    szxb = [xt[0].bitcast(BF16), xt[1].bitcast(BF16)]
    xs_b = x_lin.bitcast(BF16)[:, 4096:8192]

    kt_list4 = [(k, tap) for k in range(KB) for tap in range(4)]
    hmk = hm_all.rearrange("p (k t) -> p k t", k=KB)

    def mk_mctx(b, m):
        OFF = 3 if m == 0 else 0
        HOF = 0 if m == 0 else 512

        def sz_sl(c, j):
            if c % 2 == 0:
                return sz_c[:, j * 512:(j + 1) * 512]
            if m == 0:
                return szxb[j // 4][:, (j % 4) * 512:(j % 4 + 1) * 512]
            return wout_sb[:, j * 512:(j + 1) * 512]

        def rhs_hm0(k):
            sl = hm_all[:, k * 515 + OFF:k * 515 + OFF + 512]
            return sl[:, ::-1] if m == 1 else sl

        def rhs_x0(k, tap):
            if m == 0:
                return hm_all[:, k * 515 + tap:k * 515 + tap + 512]
            return hm_all[:, k * 515 + 3 - tap:k * 515 + 515 - tap][:, ::-1]

        return dict(b=b, m=m, OFF=OFF, HOF=HOF, sz_sl=sz_sl,
                    rhs_hm0=rhs_hm0, rhs_x0=rhs_x0)

    def mamba0(b, m, bar_in, pre0=None, nxt=None, pa_ext=None,
               prev_hand=None):
        p.phase = "pre"
        OFF = 3 if m == 0 else 0
        HOF = 0 if m == 0 else 512
        tok_wrl = None
        if m == 0 and b > 0:
            for h in range(2):
                tok_wrl = p.step("gpsimd", lambda e, h=h: e.dma_start(
                    out=wout_sb[:, h * 2048:(h + 1) * 2048],
                    in_=dwout[0].rearrange("(k p) j -> p k j", p=128)
                    [:, h * 4:(h + 1) * 4, :]),
                    deps=bar_in, sem="DG")
        tok_yf = None
        tok_scan_done = None
        tok_dbx_done = None
        tok_wout_done = None
        tok_traw7 = None
        st_xl = None
        carry_tok = {}
        wo_hist = {}
        yf_hist = {}
        pre_t = {}
        ctx = mk_mctx(b, m)
        sz_sl = ctx["sz_sl"]
        if pre0 is None:
            hz_dep = (([prev_hand["tail"]] if prev_hand is not None else [])
                      if pa_ext is not None else bar_in)
            tok_hz = p.step("vector", lambda e: e.memset(
                hmk[:, :, HOF:HOF + 3], 0.0), deps=hz_dep, vsmall=True)

        def early(ctx_, c, st, out):
            b = ctx_["b"]
            m = ctx_["m"]
            OFF = ctx_["OFF"]
            sz_sl = ctx_["sz_sl"]
            rhs_hm0 = ctx_["rhs_hm0"]
            rhs_x0 = ctx_["rhs_x0"]
            kt_list = kt_list4
            thm = [None] * KB
            for k in range(KB):
                cc = c if m == 0 else NCH - 1 - c
                thm[k] = p.step("sync", lambda e, cc=cc, k=k: e.dma_start(
                    out=hm_all[:, k * 515 + OFF:k * 515 + OFF + 512],
                    in_=hmS[b, cc, k]), deps=st["hm"], sem="HM")
            out["thm"] = thm
            yield

            def jps(j):
                if j < 4:
                    return pB[:, j * 512:(j + 1) * 512]
                return pA[:, (j - 4) * 512:(j - 3) * 512]
            twl = [None] * KB
            twlb = [None] * KB
            zmm = [None] * KB
            zmmb = [None] * KB
            szl = [None] * KB
            out["sz"] = szl
            for k in range(4):
                twl[k] = p.step("sync", lambda e, k=k: e.dma_start(
                    out=win_b[k][:], in_=winS[m, 1, k]),
                    deps=[thm[k], st["zbuf"]], sem=f"WB{k}")
            yield
            for k in range(4):
                zmm[k] = p.step("tensor", lambda e, k=k: [
                    e.matmul(jps(j)[:, 0:512],
                             win_b[k][:, j * 128:(j + 1) * 128],
                             rhs_hm0(k), start=(k == 0), stop=(k == KB - 1))
                    for j in range(6)][-1],
                    deps=[twl[k]] + ([st["pb"]] if k == 0 else []))
                twl[k + 4] = p.step("sync", lambda e, k=k: e.dma_start(
                    out=win_b[k][:], in_=winS[m, 1, k + 4]),
                    deps=[thm[k + 4], zmm[k]], sem=f"WB{k}")
            yield
            for k in range(4, KB):
                zmm[k] = p.step("tensor", lambda e, k=k: [
                    e.matmul(jps(j)[:, 0:512],
                             win_b[k - 4][:, j * 128:(j + 1) * 128],
                             rhs_hm0(k), start=(k == 0), stop=(k == KB - 1))
                    for j in range(6)][-1], deps=[twl[k]])
            yield
            for jj in range(6):
                szl[jj] = p.step("scalar", lambda e, jj=jj, c=c:
                    e.activation(sz_sl(c, jj), jps(jj)[:, 0:512], AF.Silu),
                    deps=[zmm[KB - 1]] + ([st["szfree"]] if jj == 0 else []))
            for k in range(4):
                twlb[k] = p.step("sync", lambda e, k=k: e.dma_start(
                    out=win_b[k][:, 0:256], in_=winS[m, 1, k][:, 768:1024]),
                    deps=[thm[k], zmm[k + 4]], sem=f"WB{k}")
            yield
            for k in range(4):
                zmmb[k] = p.step("tensor", lambda e, k=k: [
                    e.matmul(pB[:, j * 512:(j + 1) * 512],
                             win_b[k][:, j * 128:(j + 1) * 128],
                             rhs_hm0(k), start=(k == 0), stop=(k == KB - 1))
                    for j in range(2)][-1],
                    deps=[twlb[k]] + ([szl[5]] if k == 0 else []))
                twlb[k + 4] = p.step("sync", lambda e, k=k: e.dma_start(
                    out=win_b[k][:, 0:256],
                    in_=winS[m, 1, k + 4][:, 768:1024]),
                    deps=[thm[k + 4], zmmb[k]], sem=f"WB{k}")
            for k in range(4, KB):
                zmmb[k] = p.step("tensor", lambda e, k=k: [
                    e.matmul(pB[:, j * 512:(j + 1) * 512],
                             win_b[k - 4][:, j * 128:(j + 1) * 128],
                             rhs_hm0(k), start=(k == 0), stop=(k == KB - 1))
                    for j in range(2)][-1], deps=[twlb[k]])
            yield
            for jj in range(6, KB):
                szl[jj] = p.step("scalar", lambda e, jj=jj, c=c:
                    e.activation(sz_sl(c, jj),
                                 pB[:, (jj - 6) * 512:(jj - 5) * 512], AF.Silu),
                    deps=[zmmb[KB - 1]])
            out["zlast"] = zmmb[KB - 1]
            tw2 = [None] * 32
            xmm = [None] * 32
            for idx in range(4):
                k, tap = kt_list[idx]
                tw2[idx] = p.step("sync", lambda e, k=k, tap=tap, idx=idx:
                    e.dma_start(out=win_b[4 + idx % 4][:], in_=winX4[m, tap, k]),
                    deps=[thm[k], st["xbuf"]], sem=f"WX{idx % 4}")
            yield
            for g in range(4):
                for idx in range(g * 8, g * 8 + 8):
                    k, tap = kt_list[idx]
                    xmm[idx] = p.step("tensor", lambda e, k=k, tap=tap, idx=idx: [
                        e.matmul(jps(j)[:, 0:512],
                                 win_b[4 + idx % 4][:, j * 128:(j + 1) * 128],
                                 rhs_x0(k, tap), start=(idx == 0),
                                 stop=(idx == 31))
                        for j in range(6)][-1],
                        deps=[tw2[idx]] + ([szl[7]] if idx == 0 else []))
                    if idx + 4 < 32:
                        k2, tap2 = kt_list[idx + 4]
                        tw2[idx + 4] = p.step("sync",
                            lambda e, k2=k2, tap2=tap2, idx=idx: e.dma_start(
                                out=win_b[4 + (idx + 4) % 4][:],
                                in_=winX4[m, tap2, k2]),
                            deps=[thm[k2], xmm[idx]], sem=f"WX{(idx + 4) % 4}")
                yield
            sil = [None] * KB
            out["si"] = sil
            for j in range(6):
                sil[j] = p.step("scalar", lambda e, j=j: e.activation(
                    xs_b[:, j * 512:(j + 1) * 512],
                    jps(j)[:, 0:512], AF.Silu,
                    bias=vcol(32 + m * 8, j)),
                    deps=[xmm[31]] + ([st["xlfree"]] if j == 0 else []))
            tw2b = [None] * 32
            xmmb = [None] * 32
            for idx in range(4):
                k, tap = kt_list[idx]
                tw2b[idx] = p.step("sync", lambda e, k=k, tap=tap, idx=idx:
                    e.dma_start(out=win_b[4 + idx % 4][:, 0:256],
                                in_=winX4[m, tap, k][:, 768:1024]),
                    deps=[thm[k], xmm[28 + idx]], sem=f"WX{idx % 4}")
            yield
            for g in range(2):
                for idx in range(g * 16, g * 16 + 16):
                    k, tap = kt_list[idx]
                    xmmb[idx] = p.step("tensor", lambda e, k=k, tap=tap, idx=idx: [
                        e.matmul(pB[:, j * 512:(j + 1) * 512],
                                 win_b[4 + idx % 4][:, j * 128:(j + 1) * 128],
                                 rhs_x0(k, tap), start=(idx == 0),
                                 stop=(idx == 31))
                        for j in range(2)][-1],
                        deps=[tw2b[idx]] + ([sil[5]] if idx == 0 else []))
                    if idx + 4 < 32:
                        k2, tap2 = kt_list[idx + 4]
                        tw2b[idx + 4] = p.step("sync",
                            lambda e, k2=k2, tap2=tap2, idx=idx: e.dma_start(
                                out=win_b[4 + (idx + 4) % 4][:, 0:256],
                                in_=winX4[m, tap2, k2][:, 768:1024]),
                            deps=[thm[k2], xmmb[idx]], sem=f"WX{(idx + 4) % 4}")
                if g == 0:
                    yield
            for j in range(2):
                sil[6 + j] = p.step("scalar", lambda e, j=j: e.activation(
                    xs_b[:, (6 + j) * 512:(7 + j) * 512],
                    pB[:, j * 512:(j + 1) * 512], AF.Silu,
                    bias=vcol(32 + m * 8, 6 + j)), deps=[xmmb[31]])
            if m == 0:
                out["tail"] = p.step("vector", lambda e: e.tensor_copy(
                    hmk[:, :, 0:3], hmk[:, :, 512:515]),
                    deps=[xmmb[31], zmmb[KB - 1]], vsmall=True)
            else:
                out["tail"] = p.step("vector", lambda e: e.tensor_copy(
                    hmk[:, :, 512:515], hmk[:, :, 0:3]),
                    deps=[xmmb[31], zmmb[KB - 1]], vsmall=True)
            out["xlast"] = xmmb[31]
            out["pb_done"] = sil[7]

        def mk_st(c):
            if c == 0:
                bi = list(bar_in)
                st0 = {"pb": bi, "szfree": bi, "xlfree": bi,
                       "hm": bi + [tok_hz], "zbuf": bi, "xbuf": bi}
                if pa_ext is not None:
                    cc0 = 0 if m == 0 else NCH - 1
                    st0["hm"] = ([pa_ext["hmst"][cc0], tok_hz]
                                 + ([prev_hand["tail"]]
                                    if prev_hand is not None else []))
                    st0["zbuf"] = ([tdg[0], tdg[1]] if prev_hand is None
                                   else [prev_hand["zlast"]])
                    st0["xbuf"] = ([p0c_exp["winx4"]] if prev_hand is None
                                   else [prev_hand["xlast"]])
                return st0
            return {"hm": [pre_t[c - 1]["tail"]],
                    "zbuf": pre_t[c - 1]["zlast"],
                    "pb": tok_traw7,
                    "szfree": ([wo_hist[c - 2], yf_hist[c - 2],
                                state.get(("epi_q1", b, c - 2))]
                               if c >= 2 else list(bar_in)),
                    "xbuf": pre_t[c - 1]["xlast"],
                    "xlfree": st_xl}

        if pre0 is None:
            out0 = {}
            for _ in early(ctx, 0, mk_st(0), out0):
                pass
            pre_t[0] = out0
        else:
            pre_t[0] = pre0

        for c in range(NCH):
            p.phase = "pre"
            pre = pre_t[c]
            sil = pre["si"]
            szl = pre["sz"]
            def mmwx(e):
                for k in range(KB):
                    base = m * 512 + k * 64
                    e.matmul(pB[0:RNK, 0:512], wx_sb[:, base:base + 32],
                             xs_b[:, k * 512:(k + 1) * 512],
                             start=(k == 0), stop=(k == KB - 1))
                    e.matmul(pB[0:16, 512:1024], wx_sb[:, base + 32:base + 48],
                             xs_b[:, k * 512:(k + 1) * 512],
                             start=(k == 0), stop=(k == KB - 1))
                    i = e.matmul(pB[0:16, 1024:1536],
                                 wx_sb[:, base + 48:base + 64],
                                 xs_b[:, k * 512:(k + 1) * 512],
                                 start=(k == 0), stop=(k == KB - 1))
                return i
            tok_wx = p.step("tensor", mmwx,
                            deps=[sil[KB - 1], tok_wout_done]
                            + (bar_in if c == 0 else []))
            tok_dtr = p.step("vector", lambda e: e.tensor_copy(
                dtr_b[:], pB[0:RNK, 0:512]), deps=[tok_wx])
            tok_bcr = p.step("vector", lambda e, c=c: (
                e.tensor_copy(bcB[:, c * 512:(c + 1) * 512], pB[0:16, 512:1024]),
                e.tensor_copy(bcC[:, c * 512:(c + 1) * 512], pB[0:16, 1024:1536])
            )[-1], deps=[tok_dtr])
            tok_dt = [None] * KB
            traw_l = [None] * KB
            for kd in range(KB):
                bank = kd % 4
                tmm = p.step("tensor", lambda e, kd=kd, bank=bank: e.matmul(
                    pB[:, bank * 512:(bank + 1) * 512],
                    wdt_sb[:, m * 1024 + kd * 128:m * 1024 + (kd + 1) * 128],
                    dtr_b[:], start=True, stop=True),
                    deps=[tok_dtr, tok_bcr if kd in (1, 2) else None,
                          tok_dt[kd - 4] if kd >= 4 else None])
                traw_l[kd] = p.step("scalar", lambda e, kd=kd, bank=bank:
                    e.activation(
                        dt_c[:, kd * 512:(kd + 1) * 512],
                        pB[:, bank * 512:(bank + 1) * 512],
                        AF.Identity, bias=vcol(48 + m * 8, kd)),
                    deps=[tmm, tok_scan_done if kd == 0 else None])
                tab = p.step("scalar", lambda e, kd=kd: e.activation(
                    dA8[:, kd * 512:(kd + 1) * 512],
                    dt_c[:, kd * 512:(kd + 1) * 512],
                    AF.Abs), deps=[traw_l[kd],
                                   tok_scan_done if kd < 2 else None])
                tab = p.step("scalar", lambda e, kd=kd: e.activation(
                    dA8[:, kd * 512:(kd + 1) * 512],
                    dA8[:, kd * 512:(kd + 1) * 512],
                    AF.Exp, scale=-1.0), deps=[tab])
                tab = p.step("scalar", lambda e, kd=kd: e.activation(
                    dA8[:, kd * 512:(kd + 1) * 512],
                    dA8[:, kd * 512:(kd + 1) * 512],
                    AF.Ln, bias=cst[:, 1:2]), deps=[tab])
                tok_dt[kd] = p.step("vector", lambda e, kd=kd:
                    e.scalar_tensor_tensor(
                        dt_c[:, kd * 512:(kd + 1) * 512],
                        dt_c[:, kd * 512:(kd + 1) * 512],
                        0.0, dA8[:, kd * 512:(kd + 1) * 512],
                        OP.max, OP.add), deps=[tab])
            tok_traw7 = traw_l[KB - 1]
            tok_dtx_w = p.step("vector", lambda e: e.tensor_tensor(
                dtx_c[:, 0:4096], dt_c[:, 0:4096], xs_b[:, 0:4096], OP.mult),
                deps=[tok_dt[KB - 1], tok_dbx_done if c > 0 else None])
            tok_dtx = [tok_dtx_w] * KB
            tok_yi = [None] * KB
            for kd in range(KB):
                tok_yi[kd] = p.step("scalar", lambda e, kd=kd: e.activation(
                    y_c[:, kd * 512:(kd + 1) * 512],
                    xs_b[:, kd * 512:(kd + 1) * 512],
                    AF.Identity, scale=vcol(16 + m * 8, kd)),
                    deps=[sil[kd], tok_dt[KB - 1], tok_yf])
            st_xl = [tok_dtx[KB - 1], tok_yi[KB - 1]]
            p.phase = "scan"
            out_nxt = None
            if c + 1 < NCH:
                out_n = {}
                gen = early(ctx, c + 1, mk_st(c + 1), out_n)
                pre_t[c + 1] = out_n
            elif nxt is not None:
                nctx = mk_mctx(*nxt)
                tok_hz2 = p.step("vector", lambda e: e.memset(
                    hmk[:, :, nctx["HOF"]:nctx["HOF"] + 3], 0.0),
                    deps=[pre_t[c]["tail"]], vsmall=True)
                st_n = {"hm": [pre_t[c]["tail"], tok_hz2],
                        "zbuf": pre_t[c]["zlast"],
                        "pb": tok_traw7,
                        "szfree": [wo_hist[c - 2], yf_hist[c - 2],
                                   state.get(("epi_q1", b, c - 2))],
                        "xbuf": pre_t[c]["xlast"],
                        "xlfree": st_xl}
                out_nxt = {}
                gen = early(nctx, 0, st_n, out_nxt)
            else:
                gen = None
            pc = c % 2
            tok_exp_p = [None] * KB
            tok_scan_p = [None] * KB
            tok_dbx_g = None
            tok_prod = [None] * KB
            tok_prod_g = None
            tok_acc_g = None
            tok_acc_gG = None
            tok_accV = None
            tok_accG = None
            tevB_p = None
            tevC_p = None
            for n in range(NST):
                tbcB = p.step("tensor", lambda e, n=n, c=c: e.matmul(
                    pA[:, 1024:1536],
                    selB[:, n * 128:(n + 1) * 128],
                    bcB[:, c * 512:(c + 1) * 512], start=True, stop=True),
                    deps=[tok_bcr, tsbc_l, tevB_p,
                          tok_dbx_g if n >= 2 else tok_dt[KB - 1]])
                tbcC = p.step("tensor", lambda e, n=n, c=c: e.matmul(
                    pA[:, 1536:2048],
                    selB[:, n * 128:(n + 1) * 128],
                    bcC[:, c * 512:(c + 1) * 512],
                    start=True, stop=True),
                    deps=[tbcB, tevC_p, tok_prod_g if n >= 2 else None])
                par = n % 2
                tevB = p.step("scalar", lambda e, par=par: e.activation(
                    bcbuf[:, par * 512:(par + 1) * 512],
                    pA[:, 1024:1536], AF.Identity),
                    deps=[tbcB, tok_dbx_g if n >= 2 else None])
                tevC = p.step("scalar", lambda e, par=par: e.activation(
                    bcbuf[:, 1024 + par * 512:1024 + (par + 1) * 512],
                    pA[:, 1536:2048], AF.Identity),
                    deps=[tbcC, tok_prod_g if n >= 2 else None])
                tevB_p = tevB
                tevC_p = tevC
                tok_exp = [None] * KB
                for kd in range(KB):
                    war = tok_scan_p[kd] if n > 0 else tok_dt[KB - 1]
                    tok_exp[kd] = p.step("scalar", lambda e, kd=kd, n=n:
                        e.activation(
                            dA8[:, kd * 512:(kd + 1) * 512],
                            dt_c[:, kd * 512:(kd + 1) * 512], AF.Exp,
                            scale=acoef[:, m * 128 + kd * 16 + n:
                                        m * 128 + kd * 16 + n + 1]),
                        deps=[tok_dt[kd] if n == 0 else tok_exp_p[kd], war,
                              tok_acoef])
                brep = bass.AP(scr16.tensor, par * 512,
                               [[2048, 128], [0, KB], [1, 512]])
                crep = bass.AP(scr16.tensor, 1024 + par * 512,
                               [[2048, 128], [0, KB], [1, 512]])
                tok_dbx_w = p.step("vector", lambda e, brep=brep:
                    e.tensor_tensor(dBx_b[:, 0:4096], dtx_c[:, 0:4096],
                                    brep, OP.mult),
                    deps=[tevB, tok_dtx[KB - 1],
                          tok_accV if n > 0 else None,
                          tok_accG if n > 0 else None])
                tok_dbx = [tok_dbx_w] * KB
                tok_dbx_g = tok_dbx_w
                tok_scan = [None] * KB
                for kd in range(KB):
                    if c == 0:
                        init = 0.0
                        depc = None
                    else:
                        init = carr[1 - pc][:, n * 8 + kd:n * 8 + kd + 1]
                        depc = carry_tok[(c - 1, n)]
                    tok_scan[kd] = p.step("vector",
                        lambda e, kd=kd, init=init: e.tensor_tensor_scan(
                            hc_b[:, kd * 512:(kd + 1) * 512],
                            dA8[:, kd * 512:(kd + 1) * 512],
                            dBx_b[:, kd * 512:(kd + 1) * 512],
                            init, OP.mult, OP.add),
                        deps=[tok_exp[kd], tok_dbx[kd], depc,
                              tok_prod[kd] if n > 0 else None])
                tok_scan_p = tok_scan
                tok_exp_p = tok_exp
                tcar = p.step("vector", lambda e, n=n, pc=pc: e.tensor_copy(
                    carr[pc][:, n * 8:(n + 1) * 8], hc_b[:, 511::512]),
                    deps=[tok_scan[KB - 1]])
                carry_tok[(c, n)] = tcar
                tok_prod_w = p.step("vector", lambda e, crep=crep:
                    e.tensor_tensor(dBx_b[:, 0:4096], hc_b[:, 0:4096],
                                    crep, OP.mult),
                    deps=[tevC, tok_scan[KB - 1],
                          tok_accG if n > 0 else None])
                tok_prod = [tok_prod_w] * KB
                tok_prod_g = tok_prod_w
                tok_accV = p.step("vector", lambda e: e.tensor_tensor(
                    y_c[:, 0:2560], dBx_b[:, 0:2560], y_c[:, 0:2560], OP.add),
                    deps=[tok_prod_w,
                          tok_yi[4] if n == 0 else None])
                tok_accG = p.step("gpsimd", lambda e: e.tensor_tensor(
                    y_c[:, 2560:4096], dBx_b[:, 2560:4096],
                    y_c[:, 2560:4096], OP.add),
                    deps=[tok_prod_w, tok_accG if n > 0 else tok_yi[KB - 1]])
                tok_acc_g = tok_accV
                tok_acc_gG = tok_accG
                if gen is not None:
                    next(gen, None)
            if gen is not None:
                for _ in gen:
                    pass
            tok_scan_done = [tok_acc_g, tok_acc_gG]
            tok_dbx_done = [tok_acc_g, tok_acc_gG]
            def sz_h(cc_, h):
                if cc_ % 2 == 0:
                    return sz_c[:, h * 2048:(h + 1) * 2048]
                if m == 0:
                    return szxb[h][:, 0:2048]
                return wout_sb[:, h * 2048:(h + 1) * 2048]
            tok_yf1 = p.step("vector", lambda e, c=c: e.tensor_tensor(
                sz_h(c, 0), y_c[:, 0:2048], sz_h(c, 0), OP.mult),
                deps=[tok_accV, tok_accG, szl[KB - 1]])
            tok_yf2 = p.step("vector", lambda e, c=c: e.tensor_tensor(
                sz_h(c, 1), y_c[:, 2048:4096], sz_h(c, 1), OP.mult),
                deps=[tok_accV, tok_accG])
            tok_yf_l = [tok_yf1] * 4 + [tok_yf2] * 4
            tok_yf = tok_yf2
            yf_hist[c] = tok_yf
            p.phase = "pre"
            tok_wo = None
            for k in range(KB):
                tok_wo = p.step("tensor", lambda e, c=c, k=k: [
                    e.matmul(pB[:, j * 512:(j + 1) * 512],
                             wout_sb[:, m * 4096 + k * 512 + j * 128:
                                     m * 4096 + k * 512 + (j + 1) * 128],
                             sz_sl(c, k), start=(k == 0), stop=(k == KB - 1))
                    for j in range(4)][-1],
                    deps=[tok_yf_l[k], tok_dt[KB - 1] if k == 0 else None,
                          tok_wrl if k == 0 else None]
                    + (([pre_t[c + 1]["pb_done"]] if c + 1 < NCH else
                        ([out_nxt["pb_done"]] if out_nxt is not None else []))
                       if k == 0 else []))
            wo_hist[c] = tok_wo
            if m == 0:
                tev = None
                tdma = None
                prev_f = state.get(("fwd_dma", c - 1))
                for j in range(4):
                    tev = p.step("scalar", lambda e, j=j: e.activation(
                        scr16[:, j * 512:(j + 1) * 512],
                        pB[:, j * 512:(j + 1) * 512], AF.Identity),
                        deps=[tok_wo, prev_f])
                    tdma = p.step("gpsimd", lambda e, j=j, c=c: e.dma_start(
                        out=fwdS[j, :, c * 512:(c + 1) * 512],
                        in_=scr16[:, j * 512:(j + 1) * 512]),
                        deps=[tev], sem="DG")
                state[("fwd_dma", c)] = tdma
                tok_wout_done = [tev, tdma]
            else:
                tok_wout_done = epilogue(b, c, tok_wo, st_xl,
                                         lambda j, c=c: sz_sl(c, j),
                                         tok_sd=tok_scan_done)
        hand = {"tail": pre_t[NCH - 1]["tail"],
                "zlast": pre_t[NCH - 1]["zlast"],
                "xlast": pre_t[NCH - 1]["xlast"]}
        return p.bar(), out_nxt, hand


    def epilogue(b, c, tok_wo, tok_xfree, stash_sl, tok_sd=None):
        p.phase = "epi"
        th1l = [None] * KB
        for k in range(KB):
            th1l[k] = p.step("sync", lambda e, b=b, c=c, k=k: e.dma_start(
                out=h1c[:, k * 512:(k + 1) * 512], in_=h1S[b, c, k]),
                deps=[tok_xfree, state.get("epi_gin")], sem="HC")
        tfa = [None] * KB
        for k in range(KB):
            tfa[k] = p.step("sync", lambda e, k=k: e.dma_start(
                out=ffn_b[k][:], in_=ffnS[0, k]),
                deps=[state.get(("ffn_rb", k))], sem=f"FB{k}")
        tgt = [None] * KB
        tgag_p = [None] * KB
        tgin_p = [None] * KB
        for dblk in range(KB):
            if dblk < 4:
                tfl = p.step("gpsimd", lambda e, dblk=dblk, c=c: e.dma_start(
                    out=gag_b[dblk % 2][:], in_=fwdS[dblk, :, c * 512:(c + 1) * 512]),
                    deps=[state[("fwd_dma", c)],
                          tgt[dblk - 2] if dblk >= 2 else
                          (tok_sd if tok_sd is not None else tok_wo)],
                    sem=f"GG{dblk % 2}")
                tgin = p.step("vector", lambda e, dblk=dblk: e.tensor_tensor(
                    gin_b[dblk % 2][:], gag_b[dblk % 2][:],
                    h1c[:, dblk * 512:(dblk + 1) * 512], OP.add),
                    deps=[tfl, th1l[dblk], tgag_p[dblk - 2] if dblk >= 2 else None])
            else:
                tgin = p.step("vector", lambda e, dblk=dblk: e.tensor_tensor(
                    gin_b[dblk % 2][:], pB[:, (dblk - 4) * 512:(dblk - 3) * 512],
                    h1c[:, dblk * 512:(dblk + 1) * 512], OP.add),
                    deps=[tok_wo, th1l[dblk], tgag_p[dblk - 2]])
            tgin_p[dblk] = tgin
            tgel = p.step("scalar", lambda e, dblk=dblk: e.activation(
                go_b[dblk % 2][:], gin_b[dblk % 2][:], AF.Gelu),
                deps=[tgin, tgag_p[dblk - 2] if dblk >= 2 else None])
            tgag = p.step("scalar", lambda e, dblk=dblk, b=b: e.activation(
                gag_b[dblk % 2][:], go_b[dblk % 2][:], AF.Identity,
                scale=mcol(2, dblk, b)),
                deps=[tgel, tgt[dblk - 2] if dblk >= 2 else None])
            tgag_p[dblk] = tgag
            tgt[dblk] = p.step("tensor", lambda e, dblk=dblk: [
                e.transpose(pAb[:, i * 1024 + dblk * 128:i * 1024 + (dblk + 1) * 128],
                            gag_b[dblk % 2][:, i * 128:(i + 1) * 128], idents[:])
                for i in range(4)][-1], deps=[tgag])
        state["epi_gin"] = tgin_p[KB - 1]
        txt = [None] * 4
        tstats = [None] * 4
        for i in range(4):
            tl = p.step("gpsimd", lambda e, b=b, c=c, i=i: e.dma_start(
                out=xt[i][:], in_=dx[b, (c * 4 + i) * 128:(c * 4 + i + 1) * 128, :]),
                deps=[tok_xfree, state["out_dma"]], sem=f"XG{i}")
            txt[i] = p.step("vector", lambda e, i=i: e.tensor_tensor(
                xt[i][:], xt[i][:], pAb[:, i * 1024:(i + 1) * 1024], OP.add),
                deps=[tl, tgt[KB - 1]])
            tsv = p.step("vector", lambda e, i=i: e.bn_stats(
                stA[:, 128:134], xt[i][:, 0:512]),
                deps=[txt[i], tstats[i - 1] if i > 0 else None], vsmall=True)
            tsv = p.step("vector", lambda e, i=i: e.bn_stats(
                stA[:, 134:140], xt[i][:, 512:1024]), deps=[tsv], vsmall=True)
            tstats[i] = p.step("vector", lambda e, i=i: e.bn_aggr(
                stA[:, 32 + 2 * i:34 + 2 * i], stA[:, 128:140]), deps=[tsv],
                vsmall=True)
        tin = p.step("scalar", lambda e: e.activation(
            stA[:, 144:148], stA[:, 33:41:2], AF.Ln, bias=cst[:, 0:1]), deps=tstats)
        tin = p.step("scalar", lambda e: e.activation(
            stA[:, 48:52], stA[:, 144:148], AF.Exp, scale=-0.5), deps=[tin])
        tnm = p.step("vector", lambda e: e.tensor_tensor(
            stA[:, 52:56], stA[:, 32:40:2], stA[:, 48:52], OP.mult), deps=[tin],
            vsmall=True)
        tnm = p.step("vector", lambda e: e.tensor_scalar(
            stA[:, 52:56], stA[:, 52:56], -1.0, None, OP.mult), deps=[tnm],
            vsmall=True)
        ttr = [None] * 4
        for i in range(4):
            txh = p.step("scalar", lambda e, i=i: e.activation(
                xhat[i % 2][:], xt[i][:], AF.Identity,
                scale=stA[:, 48 + i:49 + i], bias=stA[:, 52 + i:53 + i]),
                deps=[tnm, txt[i], ttr[i - 2] if i >= 2 else None])
            ttr[i] = p.step("tensor", lambda e, i=i: [
                e.transpose(pAb[:, kd * 512 + i * 128:kd * 512 + (i + 1) * 128],
                            xhat[i % 2][:, kd * 128:(kd + 1) * 128], idents[:])
                for kd in range(KB)][-1],
                deps=[txh, txt[3]])
        tevh = [None] * KB
        for kd in range(KB):
            tevh[kd] = p.step("scalar", lambda e, kd=kd, b=b: e.activation(
                h2f[:, kd * 512:(kd + 1) * 512], pAb[:, kd * 512:(kd + 1) * 512],
                AF.Identity, scale=mcol(4, kd, b), bias=mcol(3, kd, b)),
                deps=[ttr[3]])
        tok_h2 = tevh[KB - 1]
        mmA = [[None] * KB, [None] * KB]
        for h in range(2):
            ps_h = pA if h == 0 else pB
            for k in range(KB):
                mmA[h][k] = p.step("tensor", lambda e, k=k, h=h, ps_h=ps_h: [
                    e.matmul(ps_h[:, j * 512:(j + 1) * 512],
                             ffn_b[k][:, (h * 4 + j) * 128:(h * 4 + j + 1) * 128],
                             h2f[:, k * 512:(k + 1) * 512],
                             start=(k == 0), stop=(k == KB - 1))
                    for j in range(4)][-1],
                    deps=[tfa[k], tok_h2 if k == 0 else None,
                          txt[3] if k == 0 and h == 0 else None])
        tfb = [None] * KB
        for k in range(KB):
            tfb[k] = p.step("gpsimd", lambda e, k=k: e.dma_start(
                out=ffn_b[k][:], in_=ffnS[1, k]),
                deps=[mmA[1][k]], sem=f"FG{k % 4}")
        tevA = [None] * KB
        for j in range(KB):
            tevA[j] = p.step("scalar", lambda e, j=j: e.activation(
                stash_sl(j),
                (pA if j < 4 else pB)[:, (j % 4) * 512:(j % 4 + 1) * 512],
                AF.Identity),
                deps=[mmA[j // 4][KB - 1], tok_wo if j == 0 else None])
        mmB = [[None] * KB, [None] * KB]
        for h in range(2):
            ps_h = pA if h == 0 else pB
            for k in range(KB):
                mmB[h][k] = p.step("tensor", lambda e, k=k, h=h, ps_h=ps_h: [
                    e.matmul(ps_h[:, j * 512:(j + 1) * 512],
                             ffn_b[k][:, (h * 4 + j) * 128:(h * 4 + j + 1) * 128],
                             h2f[:, k * 512:(k + 1) * 512],
                             start=(k == 0), stop=(k == KB - 1))
                    for j in range(4)][-1],
                    deps=[tfb[k] if h == 0 else None,
                          tevA[3 + h * 4] if k == 0 else None])
                if h == 1:
                    state[("ffn_rb", k)] = mmB[1][k]
        tok_tr_l = None
        for j in range(KB):
            tq1 = p.step("vector", lambda e, j=j: e.tensor_tensor(
                re2[:], stash_sl(j), stash_sl(j), OP.mult),
                deps=[mmB[j // 4][KB - 1], tok_tr_l])
            if j == KB - 1:
                state[("epi_q1", b, c)] = tq1
            tq2 = p.step("scalar", lambda e, j=j: e.activation(
                im2[:], (pA if j < 4 else pB)[:, (j % 4) * 512:(j % 4 + 1) * 512],
                AF.Square), deps=[tq1])
            tad = p.step("vector", lambda e: e.tensor_tensor(
                re2[:], re2[:], im2[:], OP.add), deps=[tq2])
            tsr = p.step("scalar", lambda e: e.activation(
                re2[:], re2[:], AF.Sqrt), deps=[tad])
            tok_tr_l = p.step("scalar", lambda e, j=j, b=b: e.activation(
                rb8[:, j * 512:(j + 1) * 512], re2[:], AF.Identity,
                scale=mcol(5, j, b)), deps=[tsr])
        tok_t2 = p.step("tensor", lambda e: [
            e.transpose(pAb[:, i * 1024 + kd * 128:i * 1024 + (kd + 1) * 128],
                        rb8[:, kd * 512 + i * 128:kd * 512 + (i + 1) * 128],
                        idents[:])
            for kd in range(KB) for i in range(4)][-1],
            deps=[tok_tr_l, txt[3], ttr[3]])
        tdo = None
        for i in range(4):
            tx2 = p.step("vector", lambda e, i=i: e.tensor_tensor(
                xt[i][:], xt[i][:], pAb[:, i * 1024:(i + 1) * 1024], OP.add),
                deps=[tok_t2, txt[i], tok_tr_l])
            tdo = p.step("gpsimd", lambda e, b=b, c=c, i=i: e.dma_start(
                out=dout[b, (c * 4 + i) * 128:(c * 4 + i + 1) * 128, :],
                in_=xt[i][:]), deps=[tx2], sem="DG")
        state["out_dma"] = tdo
        return tdo

    barrier = [tok_mod, tok_tr, ts8, tid, tif, tcst, tv0, tok_acoef]
    hand = None
    for b in range(BL):
        barrier, pa_ext = phaseA(b, barrier,
                                 hook=(p0c_hook if b == 0 else None))
        barrier, pre_m1, hand = mamba0(b, 0, barrier, nxt=(b, 1),
                                       pa_ext=pa_ext, prev_hand=hand)
        barrier, _, hand = mamba0(b, 1, barrier, pre0=pre_m1)

    p.step("gpsimd", lambda e: e.nop(), deps=p.bar())

    import contextlib
    stack = contextlib.ExitStack()
    sems = {}
    for s in COMPUTE_SEMS + DMA_SEMS:
        sems[s] = stack.enter_context(nc.semaphore(s))
    with nc.Block() as block:
        @block.sync
        def _(e):
            p.emit("sync", e, sems)

        @block.vector
        def _(e):
            p.emit("vector", e, sems)

        @block.scalar
        def _(e):
            p.emit("scalar", e, sems)

        @block.tensor
        def _(e):
            p.emit("tensor", e, sems)

        @block.gpsimd
        def _(e):
            p.emit("gpsimd", e, sems)
    stack.close()
    return nc


_CACHE = {}


def _host_layout(inputs):
    bf = ml_dtypes.bfloat16
    ident = np.eye(128, dtype=np.float32)
    i2f = np.eye(2, dtype=np.float32)
    ones = np.ones((1, 128), np.float32).astype(bf)
    sel8 = np.zeros((8, 1024), np.float32)
    for kd in range(8):
        sel8[kd, kd * 128:(kd + 1) * 128] = 1.0
    selbc = np.zeros((16, 2048), np.float32)
    for n in range(16):
        selbc[n, n * 128:(n + 1) * 128] = 1.0

    def packvec(v):
        return np.ascontiguousarray(v.reshape(KB, 128).T)

    vecs = np.zeros((128, 384), np.float32)
    vecs[:, 0:8] = packvec(inputs["L0_attn_ln_g"])
    vecs[:, 8:16] = packvec(inputs["L0_attn_ln_b"])
    for m, tag in enumerate(("m1", "m2")):
        vecs[:, 16 + m * 8:24 + m * 8] = packvec(inputs[f"L0_attn_{tag}_Dp"])
        vecs[:, 32 + m * 8:40 + m * 8] = packvec(inputs[f"L0_attn_{tag}_bconv"])
        vecs[:, 48 + m * 8:56 + m * 8] = packvec(inputs[f"L0_attn_{tag}_bdt"])
        wc = inputs[f"L0_attn_{tag}_Wconv"].reshape(KB, 128, 4)
        vecs[:, 64 + m * 32:96 + m * 32] = wc.transpose(1, 0, 2).reshape(128, 32)
        al = inputs[f"L0_attn_{tag}_Alog"].reshape(KB, 128, NST)
        vecs[:, 128 + m * 128:256 + m * 128] = al.transpose(1, 0, 2).reshape(128, 128)

    win = np.stack([inputs["L0_attn_m1_Win"], inputs["L0_attn_m2_Win"]])
    wout = np.stack([inputs["L0_attn_m1_Wout"], inputs["L0_attn_m2_Wout"]])
    wx = np.concatenate(
        [inputs[f"L0_attn_{tag}_Wx"].reshape(KB, 128, 64).transpose(1, 0, 2)
         .reshape(128, 512) for tag in ("m1", "m2")], axis=1)
    wdt = np.stack([inputs["L0_attn_m1_Wdt"], inputs["L0_attn_m2_Wdt"]])

    wconvr = np.zeros((8, 1024), np.float32)
    for m, tag in enumerate(("m1", "m2")):
        wc = inputs[f"L0_attn_{tag}_Wconv"]
        for tap in range(4):
            wconvr[m * 4 + tap] = wc[:, tap]
    common = {
        "win": np.ascontiguousarray(win),
        "wout": np.ascontiguousarray(wout),
        "wx": np.ascontiguousarray(wx),
        "wdt": np.ascontiguousarray(wdt),
        "wmod": np.ascontiguousarray(inputs["L0_wmod"]),
        "vecs": vecs,
        "theta": np.ascontiguousarray(inputs["L0_ffn_theta"]),
        "theta_n": np.ascontiguousarray(inputs["L0_ffn_theta_noise"]),
        "phi": np.ascontiguousarray(inputs["L0_ffn_phi"]),
        "phi_n": np.ascontiguousarray(inputs["L0_ffn_phi_noise"]),
        "idents": ident.astype(bf),
        "idf32": ident,
        "i2f": i2f,
        "onesr": ones,
        "sel8": sel8.astype(bf),
        "selbc": selbc.astype(bf),
        "wconvr": wconvr,
    }
    in_maps = []
    x = inputs["L0_attn_x"]
    c = inputs["L0_c"]
    for core in range(NCORES):
        mm = dict(common)
        mm["x"] = np.ascontiguousarray(x[2 * core:2 * core + 2])
        cc = c[2 * core:2 * core + 2]
        c2t = cc.T.reshape(KB, 128, 2).transpose(1, 0, 2).reshape(128, 16)
        mm["c2t"] = np.ascontiguousarray(c2t)
        in_maps.append(mm)
    return in_maps


def kernel(**inputs):
    if "nc" not in _CACHE:
        _CACHE["nc"] = build_program()
    nc = _CACHE["nc"]
    in_maps = _host_layout(inputs)
    res = run_bass_kernel_spmd(nc, in_maps, core_ids=list(range(NCORES)))
    out = np.empty((16, T, D), np.float32)
    for core in range(NCORES):
        out[2 * core:2 * core + 2] = res.results[core]["out"]
    return out


# version 45
# speedup vs baseline: 1.0092x; 1.0056x over previous
import os
import numpy as np
import ml_dtypes
import concourse.bass as bass
import concourse.mybir as mybir
from concourse.bass_utils import run_bass_kernel_spmd

F32 = mybir.dt.float32
BF16 = mybir.dt.bfloat16
F32R = mybir.dt.float32r
AF = mybir.ActivationFunctionType
OP = mybir.AluOpType

NCORES = 8
BL = 2
T = 2048
D = 1024
KB = 8
NST = 16
RNK = 32
NCH = 4
TC = 512
EPS = 1e-5
PI2 = float(np.pi / 2)

COMPUTE_SEMS = ("V", "S", "T", "G")
DMA_SEMS = ("DL", "DL2", "DG", "DG2", "XT0", "XT1", "XT2", "XT3", "WB0",
             "WB1", "WB2", "WB3", "HM", "HC", "FB0", "FB1", "FB2", "FB3",
             "FB4", "FB5", "FB6", "FB7", "GG0", "GG1", "WX0", "WX1",
             "WX2", "WX3", "XG0", "XG1", "XG2", "XG3", "FG0", "FG1",
             "FG2", "FG3")
OWN_SEM = {"vector": "V", "scalar": "S", "tensor": "T", "gpsimd": "G",
           "sync": None}


class Prog:

    def __init__(self):
        self.steps = []
        self.count = {s: 0 for s in COMPUTE_SEMS + DMA_SEMS}
        self.phase = "P0"

    def step(self, eng, fn, deps=(), sem=None, inc=None, hwdrain=False,
             vsmall=False):
        if sem is None:
            sem = OWN_SEM[eng] or "DL"
        if inc is None:
            inc = 16 if sem in DMA_SEMS else 1
        flat = []
        for d in deps:
            if d is None:
                continue
            for t in (d if isinstance(d, list) else [d]):
                if t is None:
                    continue
                if t[0] in DMA_SEMS:
                    t = (t[0], self.count[t[0]])
                flat.append(t)
        if sem in DMA_SEMS and self.count[sem] >= 8 * 16                 and (self.count[sem] // 16) % 8 == 0:
            flat.append((sem, self.count[sem]))
        self.count[sem] += inc
        tok = (sem, self.count[sem])
        self.steps.append((eng, fn, tuple(flat), sem, inc, hwdrain, self.phase,
                           vsmall))
        return tok

    def bar(self):
        return [(s, self.count[s]) for s in COMPUTE_SEMS + DMA_SEMS
                if self.count[s] > 0]

    def emit(self, eng_name, e, sems):
        dr = os.environ.get("KERNEL_DRAINS", "")
        use_drains = (os.environ.get("BASS_SIM_FP64") == "1"
                      or os.environ.get("KERNEL_SIM_DRAINS") == "1"
                      or {"vector": "V", "scalar": "S", "tensor": "T",
                          "gpsimd": "G"}.get(eng_name, "?") in dr)
        waited = {}
        own_emitted = 0
        drained_at = 0
        my = OWN_SEM[eng_name]
        small_pending = False
        for (s_eng, fn, deps, sem, inc, hwdrain, phase, vsmall) in self.steps:
            if s_eng != eng_name:
                continue
            need_drain = False
            for (dsem, dval) in deps:
                if dsem == my:
                    if dval > drained_at:
                        need_drain = True
                else:
                    if waited.get(dsem, 0) < dval:
                        e.wait_ge(sems[dsem], dval)
                        waited[dsem] = dval
            if need_drain and (use_drains or hwdrain
                               or (eng_name == "vector" and small_pending)):
                e.drain()
                drained_at = own_emitted
                small_pending = False
            last = fn(e)
            if vsmall and eng_name == "vector":
                small_pending = True
            last.then_inc(sems[sem], inc)
            if sem == my:
                own_emitted += inc


def build_program():
    nc = bass.Bass("TRN2", target_bir_lowering=False)

    dx = nc.dram_tensor("x", [BL, T, D], F32, kind="ExternalInput").ap()
    dwin = nc.dram_tensor("win", [2, D, 2 * D], F32, kind="ExternalInput").ap()
    dwout = nc.dram_tensor("wout", [2, D, 512], F32, kind="ExternalInput").ap()
    dwx = nc.dram_tensor("wx", [128, 1024], F32, kind="ExternalInput").ap()
    dwdt = nc.dram_tensor("wdt", [2, RNK, D], F32, kind="ExternalInput").ap()
    dwmod = nc.dram_tensor("wmod", [D, 6 * D], F32, kind="ExternalInput").ap()
    dc2t = nc.dram_tensor("c2t", [128, 16], F32, kind="ExternalInput").ap()
    dvecs = nc.dram_tensor("vecs", [128, 384], F32, kind="ExternalInput").ap()
    dth = nc.dram_tensor("theta", [D, D], F32, kind="ExternalInput").ap()
    dthn = nc.dram_tensor("theta_n", [D, D], F32, kind="ExternalInput").ap()
    dph = nc.dram_tensor("phi", [D, D], F32, kind="ExternalInput").ap()
    dphn = nc.dram_tensor("phi_n", [D, D], F32, kind="ExternalInput").ap()
    dids = nc.dram_tensor("idents", [128, 128], BF16, kind="ExternalInput").ap()
    di2f = nc.dram_tensor("i2f", [2, 2], F32, kind="ExternalInput").ap()
    didf = nc.dram_tensor("idf32", [128, 128], F32, kind="ExternalInput").ap()
    dones = nc.dram_tensor("onesr", [1, 128], BF16, kind="ExternalInput").ap()
    dsel8 = nc.dram_tensor("sel8", [8, 1024], BF16, kind="ExternalInput").ap()
    dselbc = nc.dram_tensor("selbc", [16, 2048], BF16, kind="ExternalInput").ap()
    dwcr = nc.dram_tensor("wconvr", [8, 1024], F32, kind="ExternalInput").ap()
    dout = nc.dram_tensor("out", [BL, T, D], F32, kind="ExternalOutput").ap()

    winS = nc.dram_tensor("winS", [2, 2, KB, 128, D], BF16, kind="Internal").ap()
    winX4 = nc.dram_tensor("winX4", [2, 4, KB, 128, D], BF16, kind="Internal").ap()
    ffnS = nc.dram_tensor("ffnS", [2, KB, 128, D], BF16, kind="Internal").ap()
    h1S = nc.dram_tensor("h1S", [BL, NCH, KB, 128, TC], BF16, kind="Internal").ap()
    hmS = nc.dram_tensor("hmS", [BL, NCH, KB, 128, TC], BF16, kind="Internal").ap()
    fwdS = nc.dram_tensor("fwdS", [4, 128, T], BF16, kind="Internal").ap()

    A = lambda n, s, d: nc.alloc_sbuf_tensor(n, s, d).ap()
    vecs = A("vecs_s", [128, 384], F32)
    acoef = A("acoef", [128, 256], F32)
    modv = A("modv", [128, 96], F32)
    idents = A("idents_s", [128, 128], BF16)
    idf32 = A("idf32_s", [128, 128], F32)
    i2f = A("i2f_s", [2, 2], F32)
    onesr = A("onesr_s", [1, 128], BF16)
    sel8 = A("sel8_s", [8, 1024], BF16)
    row8 = A("row8", [8, 256], BF16)
    c2sf = A("c2sf", [128, 16], F32)
    c2sb = A("c2sb", [128, 16], BF16)
    wdt_sb = A("wdt_sb", [RNK, 2 * D], BF16)
    wx_sb = A("wx_sb", [128, 1024], BF16)
    wout_sb = A("wout_sb", [128, 8192], BF16)
    win_b = [A(f"win_b{i}", [128, D], BF16) for i in range(8)]
    hm_all = A("hm_all", [128, KB * 515], BF16)
    ffn_b = [A(f"ffn_b{i}", [128, D], BF16) for i in range(8)]
    xt = [A(f"xt{i}", [128, D], F32) for i in range(4)]
    stA = A("stA", [128, 160], F32)
    xhat = [A(f"xhat{i}", [128, D], BF16) for i in range(2)]
    hh2 = xhat
    x_lin = A("x_lin", [128, KB * TC], F32)
    tailb = A("tailb", [128, 32], BF16)
    dt_c = A("dt_c", [128, KB * TC], F32)
    dtx_c = A("dtx_c", [128, KB * TC], BF16)
    y_c = A("y_c", [128, KB * TC], F32)
    sz_c = A("sz_c", [128, KB * TC], BF16)
    bcB = A("bcB", [16, T], BF16)
    bcC = A("bcC", [16, T], BF16)
    selB = A("selB_s", [16, 2048], BF16)
    dA8 = A("dA8", [128, KB * TC], F32)
    dBx_b = A("dBx_b", [128, KB * TC], BF16)
    hc_b = A("hc_b", [128, KB * TC], BF16)
    dtr_b = hc_b[0:RNK, 0:TC]
    h1t = [dt_c.bitcast(BF16)[:, i * 1024:(i + 1) * 1024] for i in range(4)]
    gag_b = [dA8.bitcast(BF16)[:, 0:512], dA8.bitcast(BF16)[:, 512:1024]]
    carr = [A(f"carr{i}", [128, 128], F32) for i in range(2)]
    scr16 = A("scr16", [128, 2048], BF16)
    gin_b = [scr16[:, 0:512], scr16[:, 512:1024]]
    go_b = [A(f"go_b{i}", [128, TC], F32) for i in range(2)]
    cst = A("cst", [128, 4], F32)

    bcbuf = scr16
    re2 = go_b[0]
    im2 = go_b[1]
    tst = dBx_b
    p0st = dt_c[:, 2048:4096]
    stgb = dBx_b[:, 0:2048]
    mrow = y_c[0:2, 0:2048]
    abc = sz_c[:, 0:1024]
    sbc = sz_c[:, 1024:2048]
    h1c = x_lin.bitcast(BF16)[:, 0:4096]
    h2f = dt_c.bitcast(BF16)[:, 0:4096]
    stg32b = y_c[:, 2048:4096]
    wcr = y_c[0:8, 0:1024]
    stgb2 = dBx_b[:, 2048:4096]
    rb8 = dt_c.bitcast(BF16)[:, 4096:8192]

    pA = nc.alloc_psum_tensor("pA", [128, 2048], F32).ap()
    pB = nc.alloc_psum_tensor("pB", [128, 2048], F32).ap()
    pAb = pA.bitcast(BF16)
    pBb = pB.bitcast(BF16)

    p = Prog()

    def mcol(g, dblk, b):
        c0 = (g * 8 + dblk) * 2 + b
        return modv[:, c0:c0 + 1]

    def vcol(base, dblk):
        return vecs[:, base + dblk:base + dblk + 1]

    tv0 = p.step("sync", lambda e: e.dma_start(out=vecs[:], in_=dvecs[:]))
    tcst = p.step("vector", lambda e: (e.memset(cst[:, 0:1], EPS),
                                       e.memset(cst[:, 1:2], 1.0),
                                       e.memset(cst[:, 2:3], PI2),
                                       e.memset(cst[:, 3:4], 1e-30))[-1])
    tid = p.step("sync", lambda e: e.dma_start(out=idents[:], in_=dids[:]))
    tif = p.step("sync", lambda e: e.dma_start(out=idf32[:], in_=didf[:]))
    ti2 = p.step("sync", lambda e: e.dma_start(out=i2f[:], in_=di2f[:]))
    ton = p.step("sync", lambda e: e.dma_start(out=onesr[:], in_=dones[:]))
    ts8 = p.step("sync", lambda e: e.dma_start(out=sel8[:], in_=dsel8[:]))
    tsbc_l = p.step("sync", lambda e: e.dma_start(out=selB[:], in_=dselbc[:]))
    tc2 = p.step("sync", lambda e: e.dma_start(out=c2sf[:], in_=dc2t[:]))

    tk = p.step("scalar", lambda e: e.activation(acoef[:], vecs[:, 128:384], AF.Exp),
                deps=[tv0])
    tok_acoef = p.step("scalar", lambda e: e.activation(
        acoef[:], acoef[:], AF.Identity, scale=-1.0), deps=[tk])
    tok_c2s = p.step("scalar", lambda e: e.activation(c2sf[:], c2sf[:], AF.Silu),
                     deps=[tc2])
    tok_c2s = p.step("vector", lambda e: e.tensor_copy(c2sb[:], c2sf[:]),
                     deps=[tok_c2s])

    tok_tr = ti2
    tok_ev = None
    tok_k = None
    for r in range(3):
        for k in range(KB):
            sl = slice(r * 2048, (r + 1) * 2048)
            tl = p.step("sync", lambda e, k=k, sl=sl: e.dma_start(
                out=p0st[:], in_=dwmod[k * 128:(k + 1) * 128, sl]),
                deps=[tok_c2s, tok_k])
            tcv = p.step("vector", lambda e: e.tensor_copy(stgb[:], p0st[:]),
                         deps=[tl, tok_k])
            tok_k = p.step("tensor", lambda e, k=k: [
                e.matmul(pA[0:2, q * 512:(q + 1) * 512],
                         c2sb[:, k * 2:(k + 1) * 2],
                         stgb[:, q * 512:(q + 1) * 512],
                         start=(k == 0), stop=(k == KB - 1)) for q in range(4)][-1],
                deps=[tcv, tok_c2s, tok_ev if k == 0 else None])
        tok_ev = p.step("vector", lambda e: e.tensor_copy(mrow, pA[0:2, :]),
                        deps=[tok_k])
        tok_tr = p.step("tensor", lambda e, r=r: [
            e.transpose(pB[:, (r * 16 + j) * 2:(r * 16 + j) * 2 + 2],
                        mrow[0:2, j * 128:(j + 1) * 128], i2f[:])
            for j in range(16)][-1], deps=[tok_ev, tok_tr])
    tok_modp = p.step("vector", lambda e: e.tensor_copy(modv[:], pB[:, 0:96]),
                      deps=[tok_tr])

    def modfix(e):
        for g in (1, 4):
            i = e.tensor_scalar(modv[:, g * 16:(g + 1) * 16],
                                modv[:, g * 16:(g + 1) * 16], 1.0, None, OP.add)
        return i
    tok_mod = p.step("vector", modfix, deps=[tok_modp], vsmall=True)
    for g in (2, 5):
        sl = modv[:, g * 16:(g + 1) * 16]
        tok_mod = p.step("vector", lambda e, sl=sl: e.tensor_scalar(
            stA[:, 0:16], sl, -1.0, None, OP.mult), deps=[tok_mod], vsmall=True)
        tok_mod = p.step("vector", lambda e, sl=sl: e.tensor_tensor(
            stA[:, 0:16], sl, stA[:, 0:16], OP.max), deps=[tok_mod], vsmall=True)
        tok_mod = p.step("vector", lambda e: e.tensor_scalar(
            stA[:, 0:16], stA[:, 0:16], 0.25, None, OP.max), deps=[tok_mod],
            vsmall=True)
        tok_sgn = p.step("scalar", lambda e, sl=sl: e.activation(
            stA[:, 16:32], sl, AF.Sign, bias=cst[:, 3:4]), deps=[tok_mod, tcst])
        tok_mod = p.step("vector", lambda e, sl=sl: e.tensor_tensor(
            sl, stA[:, 0:16], stA[:, 16:32], OP.mult), deps=[tok_sgn, tok_mod],
            vsmall=True)

    tw = p.step("gpsimd", lambda e: e.dma_start(out=wx_sb[:], in_=dwx[:]),
                deps=[tok_tr], sem="DG")
    tw = p.step("gpsimd", lambda e: e.dma_start(
        out=wdt_sb[:, 0:1024], in_=dwdt[0]), deps=[], sem="DG")
    tw = p.step("gpsimd", lambda e: e.dma_start(
        out=wdt_sb[:, 1024:2048], in_=dwdt[1]), deps=[], sem="DG")
    for h in range(4):
        m_, kh = divmod(h, 2)
        tw = p.step("gpsimd", lambda e, m_=m_, kh=kh, h=h: e.dma_start(
            out=wout_sb[:, h * 2048:(h + 1) * 2048],
            in_=dwout[m_].rearrange("(k p) j -> p k j", p=128)[:, kh * 4:(kh + 1) * 4, :]),
            deps=[], sem="DG")
    tdg = [None, None]
    for m in range(2):
        for k in range(KB):
            i = (m * KB + k) % 2
            sdg = "DG" if i == 0 else "DG2"
            tdg[i] = p.step("gpsimd", lambda e, m=m, k=k: e.dma_start(
                out=winS[m, 1, k],
                in_=dwin[m, k * 128:(k + 1) * 128, 1024:2048]), deps=[], sem=sdg)
    wld_sl = [x_lin[:, 0:1024], x_lin[:, 1024:2048],
              x_lin[:, 2048:3072], x_lin[:, 3072:4096]]
    wot_sl = [hc_b[:, 0:1024], hc_b[:, 1024:2048],
              hc_b[:, 2048:3072], hc_b[:, 3072:4096]]
    bcs_sl = [y_c[:, 0:1024], y_c[:, 1024:2048],
              y_c[:, 2048:3072], y_c[:, 3072:4096]]
    fin_sl = wld_sl + [dA8[:, 0:1024], dA8[:, 1024:2048],
                       dA8[:, 2048:3072], dA8[:, 3072:4096]]

    p0c_exp = {}

    def p0c_gen():
        tbd = [None] * 8
        tld_w = [None] * 16
        tmm_w = [None] * 64
        tst_w = [None] * 64
        for m_ in range(2):
            for tap in range(4):
                r = m_ * 4 + tap
                tbd[r] = p.step("sync", lambda e, r=r, tap=tap: e.dma_start(
                    out=bcs_sl[tap],
                    in_=bass.AP(dwcr.tensor, r * 1024, [[0, 128], [1, 1024]])),
                    deps=[tmm_w[28 + tap] if m_ == 1 else
                          (tok_tr if tap < 2 else None)],
                    sem=("DL" if r % 2 == 0 else "DL2"))
            for k in range(KB):
                i = m_ * 8 + k
                tld_w[i] = p.step("sync", lambda e, m_=m_, k=k, i=i: e.dma_start(
                    out=wld_sl[i % 4], in_=dwin[m_, k * 128:(k + 1) * 128, 0:1024]),
                    deps=[tmm_w[(i - 4) * 4 + 3] if i >= 4 else None],
                    sem=f"WX{i % 4}")
                for tap in range(4):
                    j = i * 4 + tap
                    tmm_w[j] = p.step("gpsimd", lambda e, i=i, tap=tap, j=j:
                        e.tensor_tensor(
                            wot_sl[j % 4], wld_sl[i % 4], bcs_sl[tap], OP.mult),
                        deps=[tld_w[i], tbd[m_ * 4 + tap],
                              tst_w[j - 4] if j >= 4 else None])
                    tst_w[j] = p.step("gpsimd", lambda e, m_=m_, tap=tap, k=k,
                                      j=j: e.dma_start(
                        out=winX4[m_, tap, k], in_=wot_sl[j % 4]),
                        deps=[tmm_w[j]], sem="DG2")
                if i % 4 == 3:
                    yield
        p0c_exp["winx4"] = tst_w[63]
        tsn_f = [None] * 16
        tst_f = [None] * 16
        for i in range(16):
            mat, k = divmod(i, KB)
            sA_, sB_ = (dth, dthn) if mat == 0 else (dph, dphn)
            bias = PI2 if mat == 0 else 0.0
            a0, a1 = (2 * i) % 8, (2 * i + 1) % 8
            dep0 = ([tmm_w[63], tld_w[15]] if i < 4 else [tsn_f[i - 4]])
            tl1 = p.step("sync", lambda e, s=sA_, k=k, a0=a0: e.dma_start(
                out=fin_sl[a0], in_=s[k * 128:(k + 1) * 128, :]), deps=dep0,
                sem=f"FB{i % 4}")
            tl2 = p.step("sync", lambda e, s=sB_, k=k, a1=a1: e.dma_start(
                out=fin_sl[a1], in_=s[k * 128:(k + 1) * 128, :]), deps=dep0,
                sem=f"FB{i % 4}")
            tv = p.step("gpsimd", lambda e, a0=a0, a1=a1: e.tensor_tensor(
                fin_sl[a0], fin_sl[a0], fin_sl[a1], OP.add),
                deps=[tl1, tl2])
            tv = p.step("gpsimd", lambda e, a0=a0: e.tensor_scalar(
                fin_sl[a0], fin_sl[a0], 1.5, -1.5, OP.min, OP.max),
                deps=[tv])
            tsn_f[i] = p.step("scalar", lambda e, bias=bias, a0=a0, i=i:
                e.activation(
                    wot_sl[i % 4], fin_sl[a0], AF.Sin,
                    bias=(cst[:, 2:3] if bias != 0.0 else 0.0)),
                deps=[tv, tcst,
                      tst_f[i - 4] if i >= 4 else tst_w[60 + i]])
            tst_f[i] = p.step("gpsimd", lambda e, mat=mat, k=k, i=i: e.dma_start(
                out=ffnS[mat, k], in_=wot_sl[i % 4]), deps=[tsn_f[i]],
                sem=("DG" if i % 2 == 0 else "DG2"))
            if i % 4 == 3:
                yield

    _p0c = p0c_gen()

    def p0c_hook(g):
        next(_p0c, None)
        next(_p0c, None)
        if g == NCH - 1:
            next(_p0c, None)

    bar0 = p.bar()

    def phaseA(b, bar_in, hook=None):
        p.phase = "A"
        tds = []
        t1 = p.step("vector", lambda e, b=b: e.tensor_copy(
            stA[:, 0:8], modv[:, 16 + b:32 + b:2]), deps=bar_in, vsmall=True)
        t1 = p.step("vector", lambda e, b=b: e.tensor_copy(
            stA[:, 8:16], modv[:, 0 + b:16 + b:2]), deps=[t1], vsmall=True)
        t2 = p.step("tensor", lambda e: e.transpose(
            pA[0:8, 0:128], stA[:, 0:8], idf32[:]), deps=[t1, tif] + bar_in)
        t2 = p.step("tensor", lambda e: e.transpose(
            pA[0:8, 128:256], stA[:, 8:16], idf32[:]), deps=[t2])
        t3 = p.step("vector", lambda e: e.tensor_copy(row8[0:8, 0:256],
                                                      pA[0:8, 0:256]), deps=[t2],
                    vsmall=True)
        t4 = p.step("tensor", lambda e: [
            e.matmul(pA[:, 1024 + kd * 128:1024 + (kd + 1) * 128],
                     sel8[:, kd * 128:(kd + 1) * 128],
                     row8[0:8, 0:128], start=True, stop=True)
            for kd in range(KB)][-1], deps=[t3, ts8])
        t5 = p.step("tensor", lambda e: [
            e.matmul(pA[:, kd * 128:(kd + 1) * 128],
                     sel8[:, kd * 128:(kd + 1) * 128],
                     row8[0:8, 128:256], start=True, stop=True)
            for kd in range(KB)][-1], deps=[t4])
        tabc = p.step("vector", lambda e: e.tensor_copy(abc[:], pA[:, 1024:2048]),
                      deps=[t4] + bar_in)
        tsbc = p.step("vector", lambda e: e.tensor_copy(sbc[:], pA[:, 0:1024]),
                      deps=[t5, tabc])

        tdma = None
        prev_tr = tsbc
        prev_ttr_h1 = [tsbc] * 4
        for g in range(NCH):
            if hook is not None:
                hook(g)
            txt = []
            tstats = []
            for i in range(4):
                ti = g * 4 + i
                tl = p.step("sync" if hook is not None else "gpsimd",
                    lambda e, b=b, ti=ti, i=i: e.dma_start(
                        out=xt[i][:], in_=dx[b, ti * 128:(ti + 1) * 128, :]),
                    deps=[prev_tr] if g > 0 else bar_in,
                    sem=(f"XT{i}" if hook is not None else f"XG{i}"))
                txt.append(tl)
                tsv = p.step("vector", lambda e, i=i: e.bn_stats(
                    stA[:, 128:134], xt[i][:, 0:512]),
                    deps=[tl, tsbc, tstats[i - 1] if i > 0 else None], vsmall=True)
                tsv = p.step("vector", lambda e, i=i: e.bn_stats(
                    stA[:, 134:140], xt[i][:, 512:1024]), deps=[tsv], vsmall=True)
                tsv = p.step("vector", lambda e, i=i: e.bn_aggr(
                    stA[:, 32 + 2 * i:34 + 2 * i], stA[:, 128:140]), deps=[tsv],
                    vsmall=True)
                tstats.append(tsv)
            tin = p.step("scalar", lambda e: e.activation(
                stA[:, 144:148], stA[:, 33:41:2], AF.Ln, bias=cst[:, 0:1]),
                deps=tstats + [tok_acoef])
            tin = p.step("scalar", lambda e: e.activation(
                stA[:, 48:52], stA[:, 144:148], AF.Exp, scale=-0.5), deps=[tin])
            tnm = p.step("vector", lambda e: e.tensor_tensor(
                stA[:, 52:56], stA[:, 32:40:2], stA[:, 48:52], OP.mult),
                deps=[tin], vsmall=True)
            tnm = p.step("vector", lambda e: e.tensor_scalar(
                stA[:, 52:56], stA[:, 52:56], -1.0, None, OP.mult), deps=[tnm],
                vsmall=True)
            th1 = [None] * 4
            ttr_hm = [None] * 4
            ttr_h1 = [None] * 4
            for i in range(4):
                txh = p.step("scalar", lambda e, i=i: e.activation(
                    xhat[i % 2][:], xt[i][:], AF.Identity,
                    scale=stA[:, 48 + i:49 + i], bias=stA[:, 52 + i:53 + i]),
                    deps=[tnm, txt[i], th1[i - 2] if i >= 2 else None])
                tm1 = p.step("vector", lambda e, i=i: e.tensor_tensor(
                    h1t[i][:], xhat[i % 2][:], abc[:], OP.mult),
                    deps=[txh, tsbc, prev_ttr_h1[i]])
                tm2 = p.step("vector", lambda e, i=i: e.scalar_tensor_tensor(
                    h1t[i][:], h1t[i][:], 0.0, sbc[:], OP.bypass, OP.add,
                    accum_out=stA[:, 64 + i:65 + i]), deps=[tm1])
                tsq = p.step("scalar", lambda e, i=i: e.activation(
                    xhat[i % 2][:], h1t[i][:], AF.Square,
                    accum_out=stA[:, 80 + i:81 + i]), deps=[tm2, txh])
                th1[i] = tsq
            tv2 = p.step("vector", lambda e: e.tensor_scalar(
                stA[:, 96:100], stA[:, 64:68], 1.0 / D, None, OP.mult), deps=th1,
                vsmall=True)
            tv2 = p.step("vector", lambda e: e.tensor_tensor(
                stA[:, 100:104], stA[:, 96:100], stA[:, 96:100], OP.mult),
                deps=[tv2], vsmall=True)
            tv2 = p.step("vector", lambda e: e.tensor_scalar(
                stA[:, 104:108], stA[:, 80:84], 1.0 / D, None, OP.mult), deps=[tv2],
                vsmall=True)
            tv2 = p.step("vector", lambda e: e.tensor_tensor(
                stA[:, 104:108], stA[:, 104:108], stA[:, 100:104], OP.subtract),
                deps=[tv2], vsmall=True)
            ti2_ = p.step("scalar", lambda e: e.activation(
                stA[:, 148:152], stA[:, 104:108], AF.Ln, bias=cst[:, 0:1]),
                deps=[tv2] + th1)
            ti2_ = p.step("scalar", lambda e: e.activation(
                stA[:, 108:112], stA[:, 148:152], AF.Exp, scale=-0.5), deps=[ti2_])
            tnm2 = p.step("vector", lambda e: e.tensor_tensor(
                stA[:, 112:116], stA[:, 96:100], stA[:, 108:112], OP.mult),
                deps=[ti2_], vsmall=True)
            tnm2 = p.step("vector", lambda e: e.tensor_scalar(
                stA[:, 112:116], stA[:, 112:116], -1.0, None, OP.mult), deps=[tnm2],
                vsmall=True)
            for i in range(4):
                thh = p.step("scalar", lambda e, i=i: e.activation(
                    hh2[i % 2][:], h1t[i][:], AF.Identity,
                    scale=stA[:, 108 + i:109 + i], bias=stA[:, 112 + i:113 + i]),
                    deps=[tnm2, th1[i], ttr_hm[i - 2] if i >= 2 else None])
                ttr_hm[i] = p.step("tensor", lambda e, i=i: [
                    e.transpose(pAb[:, kd * 512 + i * 128:kd * 512 + (i + 1) * 128],
                                hh2[i % 2][:, kd * 128:(kd + 1) * 128], idents[:])
                    for kd in range(KB)][-1],
                    deps=[thh, tid, prev_tr if i < 2 else None,
                          tsbc if i < 2 else None])
                ttr_h1[i] = p.step("tensor", lambda e, i=i: [
                    e.transpose(pBb[:, kd * 512 + i * 128:kd * 512 + (i + 1) * 128],
                                h1t[i][:, kd * 128:(kd + 1) * 128], idents[:])
                    for kd in range(KB)][-1], deps=[th1[i], ttr_hm[i]])
            prev_ttr_h1 = ttr_h1
            tev = None
            for kd in range(KB):
                tev = p.step("vector", lambda e, kd=kd: e.tensor_scalar(
                    tst[:, kd * 512:(kd + 1) * 512], pAb[:, kd * 512:(kd + 1) * 512],
                    vcol(0, kd), vcol(8, kd), OP.mult, OP.add),
                    deps=[ttr_hm[3], tv0, tdma])
            td = None
            for kd in range(KB):
                td = p.step("gpsimd", lambda e, b=b, g=g, kd=kd: e.dma_start(
                    out=hmS[b, g, kd], in_=tst[:, kd * 512:(kd + 1) * 512]),
                    deps=[tev], sem="DG")
            tds.append(td)
            tev2 = None
            for kd in range(KB):
                tev2 = p.step("vector", lambda e, kd=kd: e.tensor_copy(
                    tst[:, kd * 512:(kd + 1) * 512], pBb[:, kd * 512:(kd + 1) * 512]),
                    deps=[ttr_h1[3], td])
            for kd in range(KB):
                tdma = p.step("gpsimd", lambda e, b=b, g=g, kd=kd: e.dma_start(
                    out=h1S[b, g, kd], in_=tst[:, kd * 512:(kd + 1) * 512]),
                    deps=[tev2], sem="DG")
            prev_tr = tev2
        return p.bar(), {"hmst": tds}

    state = {"out_dma": None}

    szxb = [xt[0].bitcast(BF16), xt[1].bitcast(BF16)]
    xs_b = x_lin.bitcast(BF16)[:, 4096:8192]

    kt_list4 = [(k, tap) for k in range(KB) for tap in range(4)]
    hmk = hm_all.rearrange("p (k t) -> p k t", k=KB)

    def mk_mctx(b, m):
        OFF = 3 if m == 0 else 0
        HOF = 0 if m == 0 else 512

        def sz_sl(c, j):
            if c % 2 == 0:
                return sz_c[:, j * 512:(j + 1) * 512]
            if m == 0:
                return szxb[j // 4][:, (j % 4) * 512:(j % 4 + 1) * 512]
            return wout_sb[:, j * 512:(j + 1) * 512]

        def rhs_hm0(k):
            sl = hm_all[:, k * 515 + OFF:k * 515 + OFF + 512]
            return sl[:, ::-1] if m == 1 else sl

        def rhs_x0(k, tap):
            if m == 0:
                return hm_all[:, k * 515 + tap:k * 515 + tap + 512]
            return hm_all[:, k * 515 + 3 - tap:k * 515 + 515 - tap][:, ::-1]

        return dict(b=b, m=m, OFF=OFF, HOF=HOF, sz_sl=sz_sl,
                    rhs_hm0=rhs_hm0, rhs_x0=rhs_x0)

    def mamba0(b, m, bar_in, pre0=None, nxt=None, pa_ext=None,
               prev_hand=None):
        p.phase = "pre"
        OFF = 3 if m == 0 else 0
        HOF = 0 if m == 0 else 512
        tok_wrl = None
        if m == 0 and b > 0:
            for h in range(2):
                tok_wrl = p.step("gpsimd", lambda e, h=h: e.dma_start(
                    out=wout_sb[:, h * 2048:(h + 1) * 2048],
                    in_=dwout[0].rearrange("(k p) j -> p k j", p=128)
                    [:, h * 4:(h + 1) * 4, :]),
                    deps=bar_in, sem="DG")
        tok_yf = None
        tok_scan_done = None
        tok_dbx_done = None
        tok_wout_done = None
        tok_traw7 = None
        st_xl = None
        carry_tok = {}
        wo_hist = {}
        yf_hist = {}
        pre_t = {}
        ctx = mk_mctx(b, m)
        sz_sl = ctx["sz_sl"]
        if pre0 is None:
            hz_dep = (([prev_hand["tail"]] if prev_hand is not None else [])
                      if pa_ext is not None else bar_in)
            tok_hz = p.step("vector", lambda e: e.memset(
                hmk[:, :, HOF:HOF + 3], 0.0), deps=hz_dep, vsmall=True)

        def early(ctx_, c, st, out):
            b = ctx_["b"]
            m = ctx_["m"]
            OFF = ctx_["OFF"]
            sz_sl = ctx_["sz_sl"]
            rhs_hm0 = ctx_["rhs_hm0"]
            rhs_x0 = ctx_["rhs_x0"]
            kt_list = kt_list4
            thm = [None] * KB
            for k in range(KB):
                cc = c if m == 0 else NCH - 1 - c
                thm[k] = p.step("sync", lambda e, cc=cc, k=k: e.dma_start(
                    out=hm_all[:, k * 515 + OFF:k * 515 + OFF + 512],
                    in_=hmS[b, cc, k]), deps=st["hm"], sem="HM")
            out["thm"] = thm
            yield

            def jps(j):
                if j < 4:
                    return pB[:, j * 512:(j + 1) * 512]
                return pA[:, (j - 4) * 512:(j - 3) * 512]
            twl = [None] * KB
            twlb = [None] * KB
            zmm = [None] * KB
            zmmb = [None] * KB
            szl = [None] * KB
            out["sz"] = szl
            for k in range(4):
                twl[k] = p.step("sync", lambda e, k=k: e.dma_start(
                    out=win_b[k][:], in_=winS[m, 1, k]),
                    deps=[thm[k], st["zbuf"]], sem=f"WB{k}")
            yield
            for k in range(4):
                zmm[k] = p.step("tensor", lambda e, k=k: [
                    e.matmul(jps(j)[:, 0:512],
                             win_b[k][:, j * 128:(j + 1) * 128],
                             rhs_hm0(k), start=(k == 0), stop=(k == KB - 1))
                    for j in range(6)][-1],
                    deps=[twl[k]] + ([st["pb"]] if k == 0 else []))
                twl[k + 4] = p.step("sync", lambda e, k=k: e.dma_start(
                    out=win_b[k][:], in_=winS[m, 1, k + 4]),
                    deps=[thm[k + 4], zmm[k]], sem=f"WB{k}")
            yield
            for k in range(4, KB):
                zmm[k] = p.step("tensor", lambda e, k=k: [
                    e.matmul(jps(j)[:, 0:512],
                             win_b[k - 4][:, j * 128:(j + 1) * 128],
                             rhs_hm0(k), start=(k == 0), stop=(k == KB - 1))
                    for j in range(6)][-1], deps=[twl[k]])
            yield
            for jj in range(6):
                szl[jj] = p.step("scalar", lambda e, jj=jj, c=c:
                    e.activation(sz_sl(c, jj), jps(jj)[:, 0:512], AF.Silu),
                    deps=[zmm[KB - 1]] + ([st["szfree"]] if jj == 0 else []))
            for k in range(4):
                twlb[k] = p.step("sync", lambda e, k=k: e.dma_start(
                    out=win_b[k][:, 0:256], in_=winS[m, 1, k][:, 768:1024]),
                    deps=[thm[k], zmm[k + 4]], sem=f"WB{k}")
            yield
            for k in range(4):
                zmmb[k] = p.step("tensor", lambda e, k=k: [
                    e.matmul(pB[:, j * 512:(j + 1) * 512],
                             win_b[k][:, j * 128:(j + 1) * 128],
                             rhs_hm0(k), start=(k == 0), stop=(k == KB - 1))
                    for j in range(2)][-1],
                    deps=[twlb[k]] + ([szl[5]] if k == 0 else []))
                twlb[k + 4] = p.step("sync", lambda e, k=k: e.dma_start(
                    out=win_b[k][:, 0:256],
                    in_=winS[m, 1, k + 4][:, 768:1024]),
                    deps=[thm[k + 4], zmmb[k]], sem=f"WB{k}")
            for k in range(4, KB):
                zmmb[k] = p.step("tensor", lambda e, k=k: [
                    e.matmul(pB[:, j * 512:(j + 1) * 512],
                             win_b[k - 4][:, j * 128:(j + 1) * 128],
                             rhs_hm0(k), start=(k == 0), stop=(k == KB - 1))
                    for j in range(2)][-1], deps=[twlb[k]])
            yield
            for jj in range(6, KB):
                szl[jj] = p.step("scalar", lambda e, jj=jj, c=c:
                    e.activation(sz_sl(c, jj),
                                 pB[:, (jj - 6) * 512:(jj - 5) * 512], AF.Silu),
                    deps=[zmmb[KB - 1]])
            out["zlast"] = zmmb[KB - 1]
            tw2 = [None] * 32
            xmm = [None] * 32
            for idx in range(4):
                k, tap = kt_list[idx]
                tw2[idx] = p.step("sync", lambda e, k=k, tap=tap, idx=idx:
                    e.dma_start(out=win_b[4 + idx % 4][:], in_=winX4[m, tap, k]),
                    deps=[thm[k], st["xbuf"]], sem=f"WX{idx % 4}")
            yield
            for g in range(4):
                for idx in range(g * 8, g * 8 + 8):
                    k, tap = kt_list[idx]
                    xmm[idx] = p.step("tensor", lambda e, k=k, tap=tap, idx=idx: [
                        e.matmul(jps(j)[:, 0:512],
                                 win_b[4 + idx % 4][:, j * 128:(j + 1) * 128],
                                 rhs_x0(k, tap), start=(idx == 0),
                                 stop=(idx == 31))
                        for j in range(6)][-1],
                        deps=[tw2[idx]] + ([szl[7]] if idx == 0 else []))
                    if idx + 4 < 32:
                        k2, tap2 = kt_list[idx + 4]
                        tw2[idx + 4] = p.step("sync",
                            lambda e, k2=k2, tap2=tap2, idx=idx: e.dma_start(
                                out=win_b[4 + (idx + 4) % 4][:],
                                in_=winX4[m, tap2, k2]),
                            deps=[thm[k2], xmm[idx]], sem=f"WX{(idx + 4) % 4}")
                yield
            sil = [None] * KB
            out["si"] = sil
            for j in range(6):
                sil[j] = p.step("scalar", lambda e, j=j: e.activation(
                    xs_b[:, j * 512:(j + 1) * 512],
                    jps(j)[:, 0:512], AF.Silu,
                    bias=vcol(32 + m * 8, j)),
                    deps=[xmm[31]] + ([st["xlfree"]] if j == 0 else []))
            tw2b = [None] * 32
            xmmb = [None] * 32
            for idx in range(4):
                k, tap = kt_list[idx]
                tw2b[idx] = p.step("sync", lambda e, k=k, tap=tap, idx=idx:
                    e.dma_start(out=win_b[4 + idx % 4][:, 0:256],
                                in_=winX4[m, tap, k][:, 768:1024]),
                    deps=[thm[k], xmm[28 + idx]], sem=f"WX{idx % 4}")
            yield
            for g in range(2):
                for idx in range(g * 16, g * 16 + 16):
                    k, tap = kt_list[idx]
                    xmmb[idx] = p.step("tensor", lambda e, k=k, tap=tap, idx=idx: [
                        e.matmul(pB[:, j * 512:(j + 1) * 512],
                                 win_b[4 + idx % 4][:, j * 128:(j + 1) * 128],
                                 rhs_x0(k, tap), start=(idx == 0),
                                 stop=(idx == 31))
                        for j in range(2)][-1],
                        deps=[tw2b[idx]] + ([sil[5]] if idx == 0 else []))
                    if idx + 4 < 32:
                        k2, tap2 = kt_list[idx + 4]
                        tw2b[idx + 4] = p.step("sync",
                            lambda e, k2=k2, tap2=tap2, idx=idx: e.dma_start(
                                out=win_b[4 + (idx + 4) % 4][:, 0:256],
                                in_=winX4[m, tap2, k2][:, 768:1024]),
                            deps=[thm[k2], xmmb[idx]], sem=f"WX{(idx + 4) % 4}")
                if g == 0:
                    yield
            for j in range(2):
                sil[6 + j] = p.step("scalar", lambda e, j=j: e.activation(
                    xs_b[:, (6 + j) * 512:(7 + j) * 512],
                    pB[:, j * 512:(j + 1) * 512], AF.Silu,
                    bias=vcol(32 + m * 8, 6 + j)), deps=[xmmb[31]])
            if m == 0:
                out["tail"] = p.step("vector", lambda e: e.tensor_copy(
                    hmk[:, :, 0:3], hmk[:, :, 512:515]),
                    deps=[xmmb[31], zmmb[KB - 1]], vsmall=True)
            else:
                out["tail"] = p.step("vector", lambda e: e.tensor_copy(
                    hmk[:, :, 512:515], hmk[:, :, 0:3]),
                    deps=[xmmb[31], zmmb[KB - 1]], vsmall=True)
            out["xlast"] = xmmb[31]
            out["pb_done"] = sil[7]

        def mk_st(c):
            if c == 0:
                bi = list(bar_in)
                st0 = {"pb": bi, "szfree": bi, "xlfree": bi,
                       "hm": bi + [tok_hz], "zbuf": bi, "xbuf": bi}
                if pa_ext is not None:
                    cc0 = 0 if m == 0 else NCH - 1
                    st0["hm"] = ([pa_ext["hmst"][cc0], tok_hz]
                                 + ([prev_hand["tail"]]
                                    if prev_hand is not None else []))
                    st0["zbuf"] = ([tdg[0], tdg[1]] if prev_hand is None
                                   else [prev_hand["zlast"]])
                    st0["xbuf"] = ([p0c_exp["winx4"]] if prev_hand is None
                                   else [prev_hand["xlast"]])
                return st0
            return {"hm": [pre_t[c - 1]["tail"]],
                    "zbuf": pre_t[c - 1]["zlast"],
                    "pb": tok_traw7,
                    "szfree": ([wo_hist[c - 2], yf_hist[c - 2],
                                state.get(("epi_q1", b, c - 2))]
                               if c >= 2 else list(bar_in)),
                    "xbuf": pre_t[c - 1]["xlast"],
                    "xlfree": st_xl}

        if pre0 is None:
            out0 = {}
            for _ in early(ctx, 0, mk_st(0), out0):
                pass
            pre_t[0] = out0
        else:
            pre_t[0] = pre0

        for c in range(NCH):
            p.phase = "pre"
            pre = pre_t[c]
            sil = pre["si"]
            szl = pre["sz"]
            def mmwx(e):
                for k in range(KB):
                    base = m * 512 + k * 64
                    e.matmul(pB[0:RNK, 0:512], wx_sb[:, base:base + 32],
                             xs_b[:, k * 512:(k + 1) * 512],
                             start=(k == 0), stop=(k == KB - 1))
                    e.matmul(pB[0:16, 512:1024], wx_sb[:, base + 32:base + 48],
                             xs_b[:, k * 512:(k + 1) * 512],
                             start=(k == 0), stop=(k == KB - 1))
                    i = e.matmul(pB[0:16, 1024:1536],
                                 wx_sb[:, base + 48:base + 64],
                                 xs_b[:, k * 512:(k + 1) * 512],
                                 start=(k == 0), stop=(k == KB - 1))
                return i
            tok_wx = p.step("tensor", mmwx,
                            deps=[sil[KB - 1], tok_wout_done]
                            + (bar_in if c == 0 else []))
            tok_dtr = p.step("vector", lambda e: e.tensor_copy(
                dtr_b[:], pB[0:RNK, 0:512]), deps=[tok_wx])
            tok_bcr = p.step("vector", lambda e, c=c: (
                e.tensor_copy(bcB[:, c * 512:(c + 1) * 512], pB[0:16, 512:1024]),
                e.tensor_copy(bcC[:, c * 512:(c + 1) * 512], pB[0:16, 1024:1536])
            )[-1], deps=[tok_dtr])
            tok_dt = [None] * KB
            traw_l = [None] * KB
            for kd in range(KB):
                bank = kd % 4
                tmm = p.step("tensor", lambda e, kd=kd, bank=bank: e.matmul(
                    pB[:, bank * 512:(bank + 1) * 512],
                    wdt_sb[:, m * 1024 + kd * 128:m * 1024 + (kd + 1) * 128],
                    dtr_b[:], start=True, stop=True),
                    deps=[tok_dtr, tok_bcr if kd in (1, 2) else None,
                          tok_dt[kd - 4] if kd >= 4 else None])
                traw_l[kd] = p.step("scalar", lambda e, kd=kd, bank=bank:
                    e.activation(
                        dt_c[:, kd * 512:(kd + 1) * 512],
                        pB[:, bank * 512:(bank + 1) * 512],
                        AF.Identity, bias=vcol(48 + m * 8, kd)),
                    deps=[tmm, tok_scan_done if kd == 0 else None])
                tab = p.step("scalar", lambda e, kd=kd: e.activation(
                    dA8[:, kd * 512:(kd + 1) * 512],
                    dt_c[:, kd * 512:(kd + 1) * 512],
                    AF.Abs), deps=[traw_l[kd],
                                   tok_scan_done if kd < 2 else None])
                tab = p.step("scalar", lambda e, kd=kd: e.activation(
                    dA8[:, kd * 512:(kd + 1) * 512],
                    dA8[:, kd * 512:(kd + 1) * 512],
                    AF.Exp, scale=-1.0), deps=[tab])
                tab = p.step("scalar", lambda e, kd=kd: e.activation(
                    dA8[:, kd * 512:(kd + 1) * 512],
                    dA8[:, kd * 512:(kd + 1) * 512],
                    AF.Ln, bias=cst[:, 1:2]), deps=[tab])
                tok_dt[kd] = p.step("vector", lambda e, kd=kd:
                    e.scalar_tensor_tensor(
                        dt_c[:, kd * 512:(kd + 1) * 512],
                        dt_c[:, kd * 512:(kd + 1) * 512],
                        0.0, dA8[:, kd * 512:(kd + 1) * 512],
                        OP.max, OP.add), deps=[tab])
            tok_traw7 = traw_l[KB - 1]
            tok_dtx_w = p.step("vector", lambda e: e.tensor_tensor(
                dtx_c[:, 0:4096], dt_c[:, 0:4096], xs_b[:, 0:4096], OP.mult),
                deps=[tok_dt[KB - 1], tok_dbx_done if c > 0 else None])
            tok_dtx = [tok_dtx_w] * KB
            tok_yi = [None] * KB
            for kd in range(KB):
                tok_yi[kd] = p.step("scalar", lambda e, kd=kd: e.activation(
                    y_c[:, kd * 512:(kd + 1) * 512],
                    xs_b[:, kd * 512:(kd + 1) * 512],
                    AF.Identity, scale=vcol(16 + m * 8, kd)),
                    deps=[sil[kd], tok_dt[KB - 1], tok_yf])
            st_xl = [tok_dtx[KB - 1], tok_yi[KB - 1]]
            p.phase = "scan"
            out_nxt = None
            if c + 1 < NCH:
                out_n = {}
                gen = early(ctx, c + 1, mk_st(c + 1), out_n)
                pre_t[c + 1] = out_n
            elif nxt is not None:
                nctx = mk_mctx(*nxt)
                tok_hz2 = p.step("vector", lambda e: e.memset(
                    hmk[:, :, nctx["HOF"]:nctx["HOF"] + 3], 0.0),
                    deps=[pre_t[c]["tail"]], vsmall=True)
                st_n = {"hm": [pre_t[c]["tail"], tok_hz2],
                        "zbuf": pre_t[c]["zlast"],
                        "pb": tok_traw7,
                        "szfree": [wo_hist[c - 2], yf_hist[c - 2],
                                   state.get(("epi_q1", b, c - 2))],
                        "xbuf": pre_t[c]["xlast"],
                        "xlfree": st_xl}
                out_nxt = {}
                gen = early(nctx, 0, st_n, out_nxt)
            else:
                gen = None
            pc = c % 2
            tok_exp_p = [None] * KB
            tok_scan_p = [None] * KB
            tok_dbx_g = None
            tok_prod = [None] * KB
            tok_prod_g = None
            tok_acc_g = None
            tok_acc_gG = None
            tok_accV = None
            tok_accG = None
            tevB_p = None
            tevC_p = None
            for n in range(NST):
                tbcB = p.step("tensor", lambda e, n=n, c=c: e.matmul(
                    pA[:, 1024:1536],
                    selB[:, n * 128:(n + 1) * 128],
                    bcB[:, c * 512:(c + 1) * 512], start=True, stop=True),
                    deps=[tok_bcr, tsbc_l, tevB_p,
                          tok_dbx_g if n >= 2 else tok_dt[KB - 1]])
                tbcC = p.step("tensor", lambda e, n=n, c=c: e.matmul(
                    pA[:, 1536:2048],
                    selB[:, n * 128:(n + 1) * 128],
                    bcC[:, c * 512:(c + 1) * 512],
                    start=True, stop=True),
                    deps=[tbcB, tevC_p, tok_prod_g if n >= 2 else None])
                par = n % 2
                tevB = p.step("scalar", lambda e, par=par: e.activation(
                    bcbuf[:, par * 512:(par + 1) * 512],
                    pA[:, 1024:1536], AF.Identity),
                    deps=[tbcB, tok_dbx_g if n >= 2 else None])
                tevC = p.step("scalar", lambda e, par=par: e.activation(
                    bcbuf[:, 1024 + par * 512:1024 + (par + 1) * 512],
                    pA[:, 1536:2048], AF.Identity),
                    deps=[tbcC, tok_prod_g if n >= 2 else None])
                tevB_p = tevB
                tevC_p = tevC
                tok_exp = [None] * KB
                for kd in range(KB):
                    war = tok_scan_p[kd] if n > 0 else tok_dt[KB - 1]
                    tok_exp[kd] = p.step("scalar", lambda e, kd=kd, n=n:
                        e.activation(
                            dA8[:, kd * 512:(kd + 1) * 512],
                            dt_c[:, kd * 512:(kd + 1) * 512], AF.Exp,
                            scale=acoef[:, m * 128 + kd * 16 + n:
                                        m * 128 + kd * 16 + n + 1]),
                        deps=[tok_dt[kd] if n == 0 else tok_exp_p[kd], war,
                              tok_acoef])
                brep = bass.AP(scr16.tensor, par * 512,
                               [[2048, 128], [0, KB], [1, 512]])
                crep = bass.AP(scr16.tensor, 1024 + par * 512,
                               [[2048, 128], [0, KB], [1, 512]])
                tok_dbx_w = p.step("vector", lambda e, brep=brep:
                    e.tensor_tensor(dBx_b[:, 0:4096], dtx_c[:, 0:4096],
                                    brep, OP.mult),
                    deps=[tevB, tok_dtx[KB - 1],
                          tok_accV if n > 0 else None,
                          tok_accG if n > 0 else None])
                tok_dbx = [tok_dbx_w] * KB
                tok_dbx_g = tok_dbx_w
                tok_scan = [None] * KB
                for kd in range(KB):
                    if c == 0:
                        init = 0.0
                        depc = None
                    else:
                        init = carr[1 - pc][:, n * 8 + kd:n * 8 + kd + 1]
                        depc = carry_tok[(c - 1, n)]
                    tok_scan[kd] = p.step("vector",
                        lambda e, kd=kd, init=init: e.tensor_tensor_scan(
                            hc_b[:, kd * 512:(kd + 1) * 512],
                            dA8[:, kd * 512:(kd + 1) * 512],
                            dBx_b[:, kd * 512:(kd + 1) * 512],
                            init, OP.mult, OP.add),
                        deps=[tok_exp[kd], tok_dbx[kd], depc,
                              tok_prod[kd] if n > 0 else None])
                tok_scan_p = tok_scan
                tok_exp_p = tok_exp
                tcar = p.step("scalar", lambda e, n=n, pc=pc: e.activation(
                    carr[pc][:, n * 8:(n + 1) * 8], hc_b[:, 511::512],
                    AF.Identity),
                    deps=[tok_scan[KB - 1]])
                carry_tok[(c, n)] = tcar
                tok_prod_w = p.step("vector", lambda e, crep=crep:
                    e.tensor_tensor(dBx_b[:, 0:4096], hc_b[:, 0:4096],
                                    crep, OP.mult),
                    deps=[tevC, tok_scan[KB - 1],
                          tok_accG if n > 0 else None])
                tok_prod = [tok_prod_w] * KB
                tok_prod_g = tok_prod_w
                tok_accV = p.step("vector", lambda e: e.tensor_tensor(
                    y_c[:, 0:2560], dBx_b[:, 0:2560], y_c[:, 0:2560], OP.add),
                    deps=[tok_prod_w,
                          tok_yi[4] if n == 0 else None])
                tok_accG = p.step("gpsimd", lambda e: e.tensor_tensor(
                    y_c[:, 2560:4096], dBx_b[:, 2560:4096],
                    y_c[:, 2560:4096], OP.add),
                    deps=[tok_prod_w, tok_accG if n > 0 else tok_yi[KB - 1]])
                tok_acc_g = tok_accV
                tok_acc_gG = tok_accG
                if gen is not None:
                    next(gen, None)
            if gen is not None:
                for _ in gen:
                    pass
            tok_scan_done = [tok_acc_g, tok_acc_gG]
            tok_dbx_done = [tok_acc_g, tok_acc_gG]
            def sz_h(cc_, h):
                if cc_ % 2 == 0:
                    return sz_c[:, h * 2048:(h + 1) * 2048]
                if m == 0:
                    return szxb[h][:, 0:2048]
                return wout_sb[:, h * 2048:(h + 1) * 2048]
            tok_yf1 = p.step("vector", lambda e, c=c: e.tensor_tensor(
                sz_h(c, 0), y_c[:, 0:2048], sz_h(c, 0), OP.mult),
                deps=[tok_accV, tok_accG, szl[KB - 1]])
            tok_yf2 = p.step("vector", lambda e, c=c: e.tensor_tensor(
                sz_h(c, 1), y_c[:, 2048:4096], sz_h(c, 1), OP.mult),
                deps=[tok_accV, tok_accG])
            tok_yf_l = [tok_yf1] * 4 + [tok_yf2] * 4
            tok_yf = tok_yf2
            yf_hist[c] = tok_yf
            p.phase = "pre"
            tok_wo = None
            for k in range(KB):
                tok_wo = p.step("tensor", lambda e, c=c, k=k: [
                    e.matmul(pB[:, j * 512:(j + 1) * 512],
                             wout_sb[:, m * 4096 + k * 512 + j * 128:
                                     m * 4096 + k * 512 + (j + 1) * 128],
                             sz_sl(c, k), start=(k == 0), stop=(k == KB - 1))
                    for j in range(4)][-1],
                    deps=[tok_yf_l[k], tok_dt[KB - 1] if k == 0 else None,
                          tok_wrl if k == 0 else None]
                    + (([pre_t[c + 1]["pb_done"]] if c + 1 < NCH else
                        ([out_nxt["pb_done"]] if out_nxt is not None else []))
                       if k == 0 else []))
            wo_hist[c] = tok_wo
            if m == 0:
                tev = None
                tdma = None
                prev_f = state.get(("fwd_dma", c - 1))
                for j in range(4):
                    tev = p.step("scalar", lambda e, j=j: e.activation(
                        scr16[:, j * 512:(j + 1) * 512],
                        pB[:, j * 512:(j + 1) * 512], AF.Identity),
                        deps=[tok_wo, prev_f])
                    tdma = p.step("gpsimd", lambda e, j=j, c=c: e.dma_start(
                        out=fwdS[j, :, c * 512:(c + 1) * 512],
                        in_=scr16[:, j * 512:(j + 1) * 512]),
                        deps=[tev], sem="DG")
                state[("fwd_dma", c)] = tdma
                tok_wout_done = [tev, tdma]
            else:
                tok_wout_done = epilogue(b, c, tok_wo, st_xl,
                                         lambda j, c=c: sz_sl(c, j),
                                         tok_sd=tok_scan_done)
        hand = {"tail": pre_t[NCH - 1]["tail"],
                "zlast": pre_t[NCH - 1]["zlast"],
                "xlast": pre_t[NCH - 1]["xlast"]}
        return p.bar(), out_nxt, hand


    def epilogue(b, c, tok_wo, tok_xfree, stash_sl, tok_sd=None):
        p.phase = "epi"
        th1l = [None] * KB
        for k in range(KB):
            th1l[k] = p.step("sync", lambda e, b=b, c=c, k=k: e.dma_start(
                out=h1c[:, k * 512:(k + 1) * 512], in_=h1S[b, c, k]),
                deps=[tok_xfree, state.get("epi_gin")], sem="HC")
        tfa = [None] * KB
        for k in range(KB):
            tfa[k] = p.step("sync", lambda e, k=k: e.dma_start(
                out=ffn_b[k][:], in_=ffnS[0, k]),
                deps=[state.get(("ffn_rb", k))], sem=f"FB{k}")
        tgt = [None] * KB
        tgag_p = [None] * KB
        tgin_p = [None] * KB
        for dblk in range(KB):
            if dblk < 4:
                tfl = p.step("gpsimd", lambda e, dblk=dblk, c=c: e.dma_start(
                    out=gag_b[dblk % 2][:], in_=fwdS[dblk, :, c * 512:(c + 1) * 512]),
                    deps=[state[("fwd_dma", c)],
                          tgt[dblk - 2] if dblk >= 2 else
                          (tok_sd if tok_sd is not None else tok_wo)],
                    sem=f"GG{dblk % 2}")
                tgin = p.step("vector", lambda e, dblk=dblk: e.tensor_tensor(
                    gin_b[dblk % 2][:], gag_b[dblk % 2][:],
                    h1c[:, dblk * 512:(dblk + 1) * 512], OP.add),
                    deps=[tfl, th1l[dblk], tgag_p[dblk - 2] if dblk >= 2 else None])
            else:
                tgin = p.step("vector", lambda e, dblk=dblk: e.tensor_tensor(
                    gin_b[dblk % 2][:], pB[:, (dblk - 4) * 512:(dblk - 3) * 512],
                    h1c[:, dblk * 512:(dblk + 1) * 512], OP.add),
                    deps=[tok_wo, th1l[dblk], tgag_p[dblk - 2]])
            tgin_p[dblk] = tgin
            tgel = p.step("scalar", lambda e, dblk=dblk: e.activation(
                go_b[dblk % 2][:], gin_b[dblk % 2][:], AF.Gelu),
                deps=[tgin, tgag_p[dblk - 2] if dblk >= 2 else None])
            tgag = p.step("scalar", lambda e, dblk=dblk, b=b: e.activation(
                gag_b[dblk % 2][:], go_b[dblk % 2][:], AF.Identity,
                scale=mcol(2, dblk, b)),
                deps=[tgel, tgt[dblk - 2] if dblk >= 2 else None])
            tgag_p[dblk] = tgag
            tgt[dblk] = p.step("tensor", lambda e, dblk=dblk: [
                e.transpose(pAb[:, i * 1024 + dblk * 128:i * 1024 + (dblk + 1) * 128],
                            gag_b[dblk % 2][:, i * 128:(i + 1) * 128], idents[:])
                for i in range(4)][-1], deps=[tgag])
        state["epi_gin"] = tgin_p[KB - 1]
        txt = [None] * 4
        tstats = [None] * 4
        for i in range(4):
            tl = p.step("gpsimd", lambda e, b=b, c=c, i=i: e.dma_start(
                out=xt[i][:], in_=dx[b, (c * 4 + i) * 128:(c * 4 + i + 1) * 128, :]),
                deps=[tok_xfree, state["out_dma"]], sem=f"XG{i}")
            txt[i] = p.step("vector", lambda e, i=i: e.tensor_tensor(
                xt[i][:], xt[i][:], pAb[:, i * 1024:(i + 1) * 1024], OP.add),
                deps=[tl, tgt[KB - 1]])
            tsv = p.step("vector", lambda e, i=i: e.bn_stats(
                stA[:, 128:134], xt[i][:, 0:512]),
                deps=[txt[i], tstats[i - 1] if i > 0 else None], vsmall=True)
            tsv = p.step("vector", lambda e, i=i: e.bn_stats(
                stA[:, 134:140], xt[i][:, 512:1024]), deps=[tsv], vsmall=True)
            tstats[i] = p.step("vector", lambda e, i=i: e.bn_aggr(
                stA[:, 32 + 2 * i:34 + 2 * i], stA[:, 128:140]), deps=[tsv],
                vsmall=True)
        tin = p.step("scalar", lambda e: e.activation(
            stA[:, 144:148], stA[:, 33:41:2], AF.Ln, bias=cst[:, 0:1]), deps=tstats)
        tin = p.step("scalar", lambda e: e.activation(
            stA[:, 48:52], stA[:, 144:148], AF.Exp, scale=-0.5), deps=[tin])
        tnm = p.step("vector", lambda e: e.tensor_tensor(
            stA[:, 52:56], stA[:, 32:40:2], stA[:, 48:52], OP.mult), deps=[tin],
            vsmall=True)
        tnm = p.step("vector", lambda e: e.tensor_scalar(
            stA[:, 52:56], stA[:, 52:56], -1.0, None, OP.mult), deps=[tnm],
            vsmall=True)
        ttr = [None] * 4
        for i in range(4):
            txh = p.step("scalar", lambda e, i=i: e.activation(
                xhat[i % 2][:], xt[i][:], AF.Identity,
                scale=stA[:, 48 + i:49 + i], bias=stA[:, 52 + i:53 + i]),
                deps=[tnm, txt[i], ttr[i - 2] if i >= 2 else None])
            ttr[i] = p.step("tensor", lambda e, i=i: [
                e.transpose(pAb[:, kd * 512 + i * 128:kd * 512 + (i + 1) * 128],
                            xhat[i % 2][:, kd * 128:(kd + 1) * 128], idents[:])
                for kd in range(KB)][-1],
                deps=[txh, txt[3]])
        tevh = [None] * KB
        for kd in range(KB):
            tevh[kd] = p.step("scalar", lambda e, kd=kd, b=b: e.activation(
                h2f[:, kd * 512:(kd + 1) * 512], pAb[:, kd * 512:(kd + 1) * 512],
                AF.Identity, scale=mcol(4, kd, b), bias=mcol(3, kd, b)),
                deps=[ttr[3]])
        tok_h2 = tevh[KB - 1]
        mmA = [[None] * KB, [None] * KB]
        for h in range(2):
            ps_h = pA if h == 0 else pB
            for k in range(KB):
                mmA[h][k] = p.step("tensor", lambda e, k=k, h=h, ps_h=ps_h: [
                    e.matmul(ps_h[:, j * 512:(j + 1) * 512],
                             ffn_b[k][:, (h * 4 + j) * 128:(h * 4 + j + 1) * 128],
                             h2f[:, k * 512:(k + 1) * 512],
                             start=(k == 0), stop=(k == KB - 1))
                    for j in range(4)][-1],
                    deps=[tfa[k], tok_h2 if k == 0 else None,
                          txt[3] if k == 0 and h == 0 else None])
        tfb = [None] * KB
        for k in range(KB):
            tfb[k] = p.step("gpsimd", lambda e, k=k: e.dma_start(
                out=ffn_b[k][:], in_=ffnS[1, k]),
                deps=[mmA[1][k]], sem=f"FG{k % 4}")
        tevA = [None] * KB
        for j in range(KB):
            tevA[j] = p.step("scalar", lambda e, j=j: e.activation(
                stash_sl(j),
                (pA if j < 4 else pB)[:, (j % 4) * 512:(j % 4 + 1) * 512],
                AF.Identity),
                deps=[mmA[j // 4][KB - 1], tok_wo if j == 0 else None])
        mmB = [[None] * KB, [None] * KB]
        for h in range(2):
            ps_h = pA if h == 0 else pB
            for k in range(KB):
                mmB[h][k] = p.step("tensor", lambda e, k=k, h=h, ps_h=ps_h: [
                    e.matmul(ps_h[:, j * 512:(j + 1) * 512],
                             ffn_b[k][:, (h * 4 + j) * 128:(h * 4 + j + 1) * 128],
                             h2f[:, k * 512:(k + 1) * 512],
                             start=(k == 0), stop=(k == KB - 1))
                    for j in range(4)][-1],
                    deps=[tfb[k] if h == 0 else None,
                          tevA[3 + h * 4] if k == 0 else None])
                if h == 1:
                    state[("ffn_rb", k)] = mmB[1][k]
        tok_tr_l = None
        for j in range(KB):
            tq1 = p.step("vector", lambda e, j=j: e.tensor_tensor(
                re2[:], stash_sl(j), stash_sl(j), OP.mult),
                deps=[mmB[j // 4][KB - 1], tok_tr_l])
            if j == KB - 1:
                state[("epi_q1", b, c)] = tq1
            tq2 = p.step("scalar", lambda e, j=j: e.activation(
                im2[:], (pA if j < 4 else pB)[:, (j % 4) * 512:(j % 4 + 1) * 512],
                AF.Square), deps=[tq1])
            tad = p.step("vector", lambda e: e.tensor_tensor(
                re2[:], re2[:], im2[:], OP.add), deps=[tq2])
            tsr = p.step("scalar", lambda e: e.activation(
                re2[:], re2[:], AF.Sqrt), deps=[tad])
            tok_tr_l = p.step("scalar", lambda e, j=j, b=b: e.activation(
                rb8[:, j * 512:(j + 1) * 512], re2[:], AF.Identity,
                scale=mcol(5, j, b)), deps=[tsr])
        tok_t2 = p.step("tensor", lambda e: [
            e.transpose(pAb[:, i * 1024 + kd * 128:i * 1024 + (kd + 1) * 128],
                        rb8[:, kd * 512 + i * 128:kd * 512 + (i + 1) * 128],
                        idents[:])
            for kd in range(KB) for i in range(4)][-1],
            deps=[tok_tr_l, txt[3], ttr[3]])
        tdo = None
        for i in range(4):
            tx2 = p.step("vector", lambda e, i=i: e.tensor_tensor(
                xt[i][:], xt[i][:], pAb[:, i * 1024:(i + 1) * 1024], OP.add),
                deps=[tok_t2, txt[i], tok_tr_l])
            tdo = p.step("gpsimd", lambda e, b=b, c=c, i=i: e.dma_start(
                out=dout[b, (c * 4 + i) * 128:(c * 4 + i + 1) * 128, :],
                in_=xt[i][:]), deps=[tx2], sem="DG")
        state["out_dma"] = tdo
        return tdo

    barrier = [tok_mod, tok_tr, ts8, tid, tif, tcst, tv0, tok_acoef]
    hand = None
    for b in range(BL):
        barrier, pa_ext = phaseA(b, barrier,
                                 hook=(p0c_hook if b == 0 else None))
        barrier, pre_m1, hand = mamba0(b, 0, barrier, nxt=(b, 1),
                                       pa_ext=pa_ext, prev_hand=hand)
        barrier, _, hand = mamba0(b, 1, barrier, pre0=pre_m1)

    p.step("gpsimd", lambda e: e.nop(), deps=p.bar())

    import contextlib
    stack = contextlib.ExitStack()
    sems = {}
    for s in COMPUTE_SEMS + DMA_SEMS:
        sems[s] = stack.enter_context(nc.semaphore(s))
    with nc.Block() as block:
        @block.sync
        def _(e):
            p.emit("sync", e, sems)

        @block.vector
        def _(e):
            p.emit("vector", e, sems)

        @block.scalar
        def _(e):
            p.emit("scalar", e, sems)

        @block.tensor
        def _(e):
            p.emit("tensor", e, sems)

        @block.gpsimd
        def _(e):
            p.emit("gpsimd", e, sems)
    stack.close()
    return nc


_CACHE = {}


def _host_layout(inputs):
    bf = ml_dtypes.bfloat16
    ident = np.eye(128, dtype=np.float32)
    i2f = np.eye(2, dtype=np.float32)
    ones = np.ones((1, 128), np.float32).astype(bf)
    sel8 = np.zeros((8, 1024), np.float32)
    for kd in range(8):
        sel8[kd, kd * 128:(kd + 1) * 128] = 1.0
    selbc = np.zeros((16, 2048), np.float32)
    for n in range(16):
        selbc[n, n * 128:(n + 1) * 128] = 1.0

    def packvec(v):
        return np.ascontiguousarray(v.reshape(KB, 128).T)

    vecs = np.zeros((128, 384), np.float32)
    vecs[:, 0:8] = packvec(inputs["L0_attn_ln_g"])
    vecs[:, 8:16] = packvec(inputs["L0_attn_ln_b"])
    for m, tag in enumerate(("m1", "m2")):
        vecs[:, 16 + m * 8:24 + m * 8] = packvec(inputs[f"L0_attn_{tag}_Dp"])
        vecs[:, 32 + m * 8:40 + m * 8] = packvec(inputs[f"L0_attn_{tag}_bconv"])
        vecs[:, 48 + m * 8:56 + m * 8] = packvec(inputs[f"L0_attn_{tag}_bdt"])
        wc = inputs[f"L0_attn_{tag}_Wconv"].reshape(KB, 128, 4)
        vecs[:, 64 + m * 32:96 + m * 32] = wc.transpose(1, 0, 2).reshape(128, 32)
        al = inputs[f"L0_attn_{tag}_Alog"].reshape(KB, 128, NST)
        vecs[:, 128 + m * 128:256 + m * 128] = al.transpose(1, 0, 2).reshape(128, 128)

    win = np.stack([inputs["L0_attn_m1_Win"], inputs["L0_attn_m2_Win"]])
    wout = np.stack([inputs["L0_attn_m1_Wout"], inputs["L0_attn_m2_Wout"]])
    wx = np.concatenate(
        [inputs[f"L0_attn_{tag}_Wx"].reshape(KB, 128, 64).transpose(1, 0, 2)
         .reshape(128, 512) for tag in ("m1", "m2")], axis=1)
    wdt = np.stack([inputs["L0_attn_m1_Wdt"], inputs["L0_attn_m2_Wdt"]])

    wconvr = np.zeros((8, 1024), np.float32)
    for m, tag in enumerate(("m1", "m2")):
        wc = inputs[f"L0_attn_{tag}_Wconv"]
        for tap in range(4):
            wconvr[m * 4 + tap] = wc[:, tap]
    common = {
        "win": np.ascontiguousarray(win),
        "wout": np.ascontiguousarray(wout),
        "wx": np.ascontiguousarray(wx),
        "wdt": np.ascontiguousarray(wdt),
        "wmod": np.ascontiguousarray(inputs["L0_wmod"]),
        "vecs": vecs,
        "theta": np.ascontiguousarray(inputs["L0_ffn_theta"]),
        "theta_n": np.ascontiguousarray(inputs["L0_ffn_theta_noise"]),
        "phi": np.ascontiguousarray(inputs["L0_ffn_phi"]),
        "phi_n": np.ascontiguousarray(inputs["L0_ffn_phi_noise"]),
        "idents": ident.astype(bf),
        "idf32": ident,
        "i2f": i2f,
        "onesr": ones,
        "sel8": sel8.astype(bf),
        "selbc": selbc.astype(bf),
        "wconvr": wconvr,
    }
    in_maps = []
    x = inputs["L0_attn_x"]
    c = inputs["L0_c"]
    for core in range(NCORES):
        mm = dict(common)
        mm["x"] = np.ascontiguousarray(x[2 * core:2 * core + 2])
        cc = c[2 * core:2 * core + 2]
        c2t = cc.T.reshape(KB, 128, 2).transpose(1, 0, 2).reshape(128, 16)
        mm["c2t"] = np.ascontiguousarray(c2t)
        in_maps.append(mm)
    return in_maps


def kernel(**inputs):
    if "nc" not in _CACHE:
        _CACHE["nc"] = build_program()
    nc = _CACHE["nc"]
    in_maps = _host_layout(inputs)
    res = run_bass_kernel_spmd(nc, in_maps, core_ids=list(range(NCORES)))
    out = np.empty((16, T, D), np.float32)
    for core in range(NCORES):
        out[2 * core:2 * core + 2] = res.results[core]["out"]
    return out
